# version 03
# baseline (speedup 1.0000x reference)
import contextlib
import numpy as np
import ml_dtypes
import concourse.bass as bass
import concourse.mybir as mybir
from concourse.bass_utils import run_bass_kernel_spmd

F32 = mybir.dt.float32
BF16 = mybir.dt.bfloat16
U8 = mybir.dt.uint8
AF = mybir.ActivationFunctionType
ALU = mybir.AluOpType
AX = mybir.AxisListType
BF = ml_dtypes.bfloat16

T = 1024
CS = 128
NCH = 8
DM = 4096
NDT = 32
H, P, N, G = 128, 64, 128, 8
INTER = 8192
NIT = 64
CONV = 10240
PROJ = 18560
NCT = 145
EPS = 1e-5
CT_DT = 144
CT_XBC = list(range(64, 144))
CT_GATE = list(range(0, 64))
CT_ORDER = [CT_DT] + CT_XBC + CT_GATE
LA = 2
NSLOT = 4

STAGE = "full"
DEBUG_TAPS = ()


class Rec:
    QD = 8

    def __init__(self):
        self.tbl = {}
        self.dry = True
        self.sems = {}
        self.qcnt = {}
        self.log = None
        self.stream = None

    def sem(self, name):
        if name in ("sqs", "sqg"):
            self.qcnt[name] = 0
            for i in range(self.QD):
                self.sems.setdefault(f"{name}{i}", [None, 0])
        else:
            self.sems.setdefault(name, [None, 0])

    def reset_pass(self, dry):
        self.dry = dry
        for v in self.sems.values():
            v[1] = 0
        for k in self.qcnt:
            self.qcnt[k] = 0

    def bind(self, name, handle):
        self.sems[name][0] = handle

    def inc(self, instr, name, n=1, mark=None):
        if name in ("sqs", "sqg"):
            assert n in (0, 16)
            if n == 0:
                return self.fence(name, mark)
            actual = f"{name}{self.qcnt[name] % self.QD}"
            self.qcnt[name] += 1
        else:
            actual = name
        s = self.sems[actual]
        s[1] += n
        if mark is not None:
            key = mark
            val = (actual, s[1])
            if self.dry:
                assert key not in self.tbl, f"dup mark {key}"
                self.tbl[key] = val
            else:
                assert self.tbl.get(key) == val, \
                    f"mark drift {key}: {self.tbl.get(key)} vs {val}"
        if instr is not None:
            instr.then_inc(s[0], n)
        if self.log is not None and not self.dry and n > 0:
            self.log.append((self.stream, "i", actual, n, mark))
        return s[1]

    def fence(self, qname, mark):
        snap = tuple((f"{qname}{i}", self.sems[f"{qname}{i}"][1]) for i in range(self.QD))
        if self.dry:
            assert mark not in self.tbl, f"dup mark {mark}"
            self.tbl[mark] = ("F", snap)
        else:
            assert self.tbl.get(mark) == ("F", snap), f"fence drift {mark}"
        return None

    def wait(self, eng, name, mark, extra=0):
        if eng is None:
            return
        if mark not in self.tbl:
            if self.dry:
                return
            raise KeyError(f"wait on unknown mark {mark}")
        ent = self.tbl[mark]
        if ent[0] == "F":
            for actual, v in ent[1]:
                if v > 0:
                    eng.wait_ge(self.sems[actual][0], v)
                    if self.log is not None and not self.dry:
                        self.log.append((self.stream, "w", actual, v, mark))
            return
        actual, v = ent
        v += extra
        if v > 0:
            eng.wait_ge(self.sems[actual][0], v)
            if self.log is not None and not self.dry:
                self.log.append((self.stream, "w", actual, v, mark))

    def wait_all(self, eng, qname):
        if eng is None:
            return
        for i in range(self.QD):
            actual = f"{qname}{i}"
            v = self.sems[actual][1]
            if v > 0:
                eng.wait_ge(self.sems[actual][0], v)
                if self.log is not None and not self.dry:
                    self.log.append((self.stream, "w", actual, v, None))


class EngW:
    def __init__(self, e, R=None, q=None):
        self._e = e
        self._R = R
        self._q = q
        self.last = None

    def __getattr__(self, k):
        f = getattr(self._e, k)
        if not callable(f):
            return f
        if k == "dma_start" and self._q is not None:
            def g(*a, **kw):
                R, q = self._R, self._q
                actual = f"{q}{R.qcnt[q] % R.QD}"
                prev = R.sems[actual][1]
                if prev > 0:
                    self._e.wait_ge(R.sems[actual][0], prev)
                    if R.log is not None and not R.dry:
                        R.log.append((R.stream, "w", actual, prev, "thr"))
                r = f(*a, **kw)
                self.last = r
                return r
            return g
        def g(*a, **kw):
            r = f(*a, **kw)
            self.last = r
            return r
        return g


def _host_inputs(core, x_seq_half, W1, conv_w, conv_b, dt_bias, A_log, D, norm_w,
                 W_o, A_w, B_w, Wq, keys):
    ins = {}
    ins["x_fm"] = np.ascontiguousarray(
        x_seq_half.T.reshape(NDT, 128, T))
    ins["w1_l"] = np.ascontiguousarray(
        W1.reshape(NCT, 128, NDT, 128).transpose(0, 3, 2, 1).reshape(NCT, 128, NDT * 128))
    cw = conv_w[:, 0, :]
    cwd = np.zeros((80, 128, 4, 128), np.float32)
    idx = np.arange(128)
    for k in range(4):
        cwd[:, idx, k, idx] = cw.reshape(80, 128, 4)[:, :, k]
    ins["convw_l"] = np.ascontiguousarray(cwd.reshape(80, 128, 4 * 128))
    ins["convw_cols"] = np.ascontiguousarray(
        cw.reshape(80, 128, 4).transpose(1, 0, 2).reshape(128, 320))
    ins["convb_l"] = np.ascontiguousarray(conv_b.reshape(80, 128).T)
    ins["dtbias_c"] = np.ascontiguousarray(dt_bias.reshape(128, 1))
    ins["alog_c"] = np.ascontiguousarray(A_log.reshape(128, 1))
    ins["d_c"] = np.ascontiguousarray(D.reshape(128, 1))
    ins["nw_l"] = np.ascontiguousarray(norm_w.reshape(NIT, 128).T)
    ins["cmask_c"] = np.full((128, 1), float(core & 1), np.float32)
    ins["wo_l"] = np.ascontiguousarray(
        W_o.reshape(32, 128, 64, 128).transpose(0, 3, 2, 1).reshape(32, 128, 64 * 128))
    ins["awt_l"] = np.ascontiguousarray(A_w.T.reshape(32, 128, 512))
    ins["bwt_l"] = np.ascontiguousarray(
        B_w.reshape(32, 128, 4, 128).transpose(0, 3, 2, 1).reshape(32, 128, 512))
    ins["wqt_l"] = np.ascontiguousarray(
        Wq.T.reshape(32, 128, 16).transpose(1, 0, 2).reshape(128, 512))
    ins["keyst_l"] = np.ascontiguousarray(keys.T)
    ins["idn_f_l"] = np.eye(128, dtype=np.float32)
    ins["tribig_l"] = np.where(np.triu(np.ones((128, 128), bool)), 1e30, 0.0).astype(np.float32)
    ins["onesrow_l"] = np.ones((1, 128), np.float32)
    ins["onescol_l"] = np.ones((128, 1), np.float32)
    return ins


def kernel(L0_attn_input_states, L0_attn_in_proj_w, L0_attn_conv_w, L0_attn_conv_b,
           L0_attn_dt_bias, L0_attn_A_log, L0_attn_D, L0_attn_norm_w, L0_attn_out_w,
           L0_ffn_A_w, L0_ffn_B_w, L0_ffn_Wq_w, L0_ffn_keys):
    x = np.ascontiguousarray(np.asarray(L0_attn_input_states, np.float32))
    args = (np.asarray(L0_attn_in_proj_w, np.float32),
            np.asarray(L0_attn_conv_w, np.float32),
            np.asarray(L0_attn_conv_b, np.float32),
            np.asarray(L0_attn_dt_bias, np.float32),
            np.asarray(L0_attn_A_log, np.float32),
            np.asarray(L0_attn_D, np.float32),
            np.asarray(L0_attn_norm_w, np.float32),
            np.asarray(L0_attn_out_w, np.float32),
            np.asarray(L0_ffn_A_w, np.float32),
            np.asarray(L0_ffn_B_w, np.float32),
            np.asarray(L0_ffn_Wq_w, np.float32),
            np.asarray(L0_ffn_keys, np.float32))
    nc = build_program()
    in_maps = []
    for core in range(8):
        s, hf = core >> 1, core & 1
        in_maps.append(_host_inputs(core, x[s, hf * T:(hf + 1) * T, :], *args))
    res = run_bass_kernel_spmd(nc, in_maps, core_ids=list(range(8)))
    out = np.empty((4, 2048, 4096), np.float32)
    for core in range(8):
        s, hf = core >> 1, core & 1
        o = np.asarray(res.results[core]["out_fm"])
        out[s, hf * T:(hf + 1) * T, :] = o.transpose(2, 0, 1).reshape(T, DM)
    return out


KB = 1024


def build_program():
    nc = bass.Bass("TRN2", target_bir_lowering=False)

    x_fm = nc.dram_tensor("x_fm", [NDT, 128, T], F32, kind="ExternalInput").ap()
    w1_l = nc.dram_tensor("w1_l", [NCT, 128, NDT * 128], F32, kind="ExternalInput").ap()
    convw_l = nc.dram_tensor("convw_l", [80, 128, 512], F32, kind="ExternalInput").ap()
    convw_cols = nc.dram_tensor("convw_cols", [128, 320], F32, kind="ExternalInput").ap()
    convb_l = nc.dram_tensor("convb_l", [128, 80], F32, kind="ExternalInput").ap()
    dtbias_c = nc.dram_tensor("dtbias_c", [128, 1], F32, kind="ExternalInput").ap()
    alog_c = nc.dram_tensor("alog_c", [128, 1], F32, kind="ExternalInput").ap()
    d_c = nc.dram_tensor("d_c", [128, 1], F32, kind="ExternalInput").ap()
    nw_l = nc.dram_tensor("nw_l", [128, NIT], F32, kind="ExternalInput").ap()
    cmask_c = nc.dram_tensor("cmask_c", [128, 1], F32, kind="ExternalInput").ap()
    wo_l = nc.dram_tensor("wo_l", [32, 128, 8192], F32, kind="ExternalInput").ap()
    awt_l = nc.dram_tensor("awt_l", [32, 128, 512], F32, kind="ExternalInput").ap()
    bwt_l = nc.dram_tensor("bwt_l", [32, 128, 512], F32, kind="ExternalInput").ap()
    wqt_l = nc.dram_tensor("wqt_l", [128, 512], F32, kind="ExternalInput").ap()
    keyst_l = nc.dram_tensor("keyst_l", [16, 64], F32, kind="ExternalInput").ap()
    idn_f_l = nc.dram_tensor("idn_f_l", [128, 128], F32, kind="ExternalInput").ap()
    tribig_l = nc.dram_tensor("tribig_l", [128, 128], F32, kind="ExternalInput").ap()
    onesrow_l = nc.dram_tensor("onesrow_l", [1, 128], F32, kind="ExternalInput").ap()
    onescol_l = nc.dram_tensor("onescol_l", [128, 1], F32, kind="ExternalInput").ap()

    out_fm = nc.dram_tensor("out_fm", [NDT, 128, T], F32, kind="ExternalOutput").ap()

    xtok_hbm = nc.dram_tensor("xtok_hbm", [NCH, 128, INTER], BF16).ap()
    sg_hbm = nc.dram_tensor("sg_hbm", [NCH, 128, INTER], BF16).ap()
    yp_hbm = nc.dram_tensor("yp_hbm", [NCH, 128, INTER], BF16).ap()
    x2_hbm = nc.dram_tensor("x2_hbm", [NDT, 128, T], F32).ap()
    wo_bf = nc.dram_tensor("wo_bf", [32, 128, 8192], BF16).ap()
    tails_bi = nc.dram_tensor("tails_bi", [128, 240], BF16).ap()
    tails_bo = nc.dram_tensor("tails_bo", [256, 240], BF16).ap()
    carry_bi = nc.dram_tensor("carry_bi", [128, INTER], BF16).ap()
    carry_bo = nc.dram_tensor("carry_bo", [256, INTER], BF16).ap()

    taps = {}
    def mktap(name, shape, dtype=F32):
        if name in DEBUG_TAPS:
            taps[name] = nc.dram_tensor("tap_" + name, shape, dtype,
                                        kind="ExternalOutput").ap()
        return taps.get(name)

    R = Rec()
    for s_ in ("sqg", "sqs", "st", "sv", "ss", "sc"):
        R.sem(s_)

    with contextlib.ExitStack() as ST0:
        def sbt(name, shape, dtype):
            return ST0.enter_context(nc.sbuf_tensor(name, shape, dtype))
        def psb(name, shape, dtype=F32):
            return ST0.enter_context(nc.psum_tensor(name, shape, dtype))

        C_sb = sbt("C_sb", [128, 8 * 1024], BF16)
        expcum_b = sbt("expcum_b", [128, T], BF16)
        dplb = sbt("dplb", [128, T], F32)
        dtb = sbt("dtb", [128, 128], F32)
        idn_f = sbt("idn_f", [128, 128], F32)
        idn_b = sbt("idn_b", [128, 128], BF16)
        tribig = sbt("tribig", [128, 128], BF16)
        onesrow_f = sbt("onesrow_f", [1, 128], F32)
        onescol_b = sbt("onescol_b", [128, 1], BF16)
        dtbias = sbt("dtbias", [128, 1], F32)
        negA = sbt("negA", [128, 1], F32)
        dcol = sbt("dcol", [128, 1], F32)
        cmask = sbt("cmask", [128, 1], F32)
        nw = sbt("nw", [128, NIT], F32)
        smallsc = sbt("smallsc", [128, 8], F32)
        rowbuf = sbt("rowbuf", [1, 1024], F32)
        rowbuf2 = sbt("rowbuf2", [1, 1024], F32)
        mrow = sbt("mrow", [1, T], F32)
        rrow = sbt("rrow", [1, T], F32)
        keyst = sbt("keyst", [16, 64], BF16)
        wqt_all = sbt("wqt_all", [128, 512], BF16)
        convb = sbt("convb", [128, 80], F32)
        constc = sbt("constc", [128, 2], F32)
        convwc = sbt("convwc", [128, 320], F32)

        arena = sbt("arena", [128, 160 * KB], U8)
        def vw(off_kb, parts, nel, dtype):
            esz = 2 if dtype == BF16 else 4
            off = int(off_kb * KB)
            assert off % 64 == 0 and off + nel * esz <= 160 * KB, (off_kb, nel)
            return arena[0:parts, off: off + nel * esz].bitcast(dtype)

        xn = vw(0, 128, NDT * 1024, BF16)
        xtok_sb = [vw(0, 128, INTER, BF16), vw(16, 128, INTER, BF16)]
        prev = vw(32, 128, INTER, BF16)
        etile = [vw(48 + 0.25 * i, 128, 128, BF16) for i in range(4)]
        stile = [vw(49 + 0.25 * i, 128, 128, BF16) for i in range(4)]
        ctile = [vw(50 + 0.25 * i, 128, 128, BF16) for i in range(4)]
        acb = [vw(51 + 0.25 * i, 128, 128, BF16) for i in range(4)]
        bwt_s = [vw(52 + 0.25 * i, 128, 128, BF16) for i in range(4)]
        sgr = vw(0, 128, INTER, BF16)
        wob = [vw(0, 128, 8192, BF16), vw(16, 128, 8192, BF16)]
        xre = [vw(32 + 2 * i, 128, 512, F32) for i in range(2)]
        x2t = [vw(36 + 2 * i, 128, 512, F32) for i in range(2)]
        rstdT = vw(40, 128, 8, F32)
        g2t = [vw(40.5 + 0.25 * i, 128, 128, BF16) for i in range(2)]
        ttile = vw(41, 128, 128, F32)
        acb3 = [vw(42 + 0.25 * i, 128, 128, BF16) for i in range(4)]
        ctile3 = [vw(43 + 0.25 * i, 128, 128, BF16) for i in range(4)]
        bwts3 = [vw(44 + 0.25 * i, 128, 128, BF16) for i in range(4)]
        carry = vw(48, 128, INTER, BF16)
        xn2 = vw(0, 128, 32 * 512, BF16)
        bwtd = [vw(32 + i, 128, 512, BF16) for i in range(2)]
        x2l = [vw(85 + 2 * i, 128, 512, F32) for i in range(2)]

        xld = [vw(64 + 4 * i, 128, T, F32) for i in range(2)]
        xsq = [vw(72 + 2 * i, 128, T, BF16) for i in range(2)]
        mb = vw(76, 128, T, F32)
        rb = vw(80, 128, T, F32)
        wbuf = [vw(64 + 4 * i, 128, 2048, BF16) for i in range(4)]
        cwb = [vw(80 + i, 128, 512, BF16) for i in range(2)]
        xbcr = [vw(82, 128, 3 + T, BF16), vw(84.0625, 128, 3 + T, BF16)]
        hsfm = [vw(86.25, 128, T, BF16), vw(88.25, 128, T, BF16)]
        xstrip = [vw(90.25, 128, T, BF16), vw(92.25, 128, T, BF16)]
        first3 = vw(94.25, 128, 240, BF16)
        tails = vw(94.75, 128, 240, BF16)
        pad3 = vw(95.25, 128, 240, BF16)
        ypr = vw(80, 128, INTER, BF16)
        zgsb = vw(64, 128, 4 * 512, BF16)
        zgt_half = vw(68, 128, 4 * 512, BF16)
        awtb = [vw(72 + i, 128, 512, BF16) for i in range(2)]
        qsb = vw(74, 128, 16, BF16)
        qT = vw(74.25, 16, 128, BF16)
        ssb = vw(74.5, 128, 64, F32)
        esb = vw(74.75, 128, 64, F32)
        m8a = vw(75, 128, 8, F32)
        m8b = vw(75.25, 128, 8, F32)
        msk = vw(75.5, 128, 64, F32)
        gsb = vw(76, 128, 256, BF16)
        negt1 = vw(76.5, 128, 1, F32)
        rs1 = vw(76.5625, 128, 1, F32)
        outbuf = [vw(77 + 2 * i, 128, 512, F32) for i in range(2)]
        mb2 = vw(81, 128, 512, F32)
        rb2 = vw(83, 128, 512, F32)

        B_sb = vw(96, 128, INTER, BF16)
        sraw = vw(112, 128, 1024, BF16)
        btok = vw(114, 128, 1024, BF16)
        x6 = vw(116, 128, 6, BF16)
        patch3 = vw(116.0625, 128, 240, F32)
        patchh = vw(117.0625, 128, 192, BF16)
        ghalf = vw(96, 128, 64 * 512, BF16)

        dtr = vw(128, 128, T, F32)
        dt_t = vw(132, 128, T, F32)
        tmpA = vw(136, 128, T, F32)
        ncmb = vw(140, 128, T, F32)
        patcht = vw(128, 3, INTER, BF16)
        ypc = vw(128, 128, INTER, BF16)
        cum = vw(144, 128, T, F32)
        ncmbT = vw(148, 128, T, F32)
        wTT = vw(152, 128, T, F32)
        cdecb = vw(156, 128, T, F32)

        ps_mm = [psb(f"ps_mm{i}", [128, 512], F32) for i in range(4)]
        ps_cv = [psb(f"ps_cv{i}", [128, 512], F32) for i in range(2)]
        ps_tr = psb("ps_tr", [128, 1024], BF16)
        ps_x = psb("ps_x", [128, 512], F32)

        for s_ in list(R.sems):
            R.bind(s_, ST0.enter_context(nc.semaphore(s_)))

        def tile_ct(i):
            return CT_ORDER[i]
        def is_xbc(i):
            return 1 <= i <= 80
        def is_gate(i):
            return i >= 81
        def evac_sem(i):
            return "ss" if is_gate(i) else "sv"
        def xn_sl(kt, th):
            return xn[:, kt * 1024 + th * 512: kt * 1024 + (th + 1) * 512]

        def p0_sync(sy):
            for dm in range(NDT):
                if dm >= 2:
                    R.wait(sy, "sv", f"p0.xc.{dm-2}")
                    R.wait(sy, "ss", f"p0.xq.{dm-2}")
                R.inc(sy and sy.dma_start(out=xld[dm % 2], in_=x_fm[dm]),
                      "sqs", 16, mark=f"p0.x.{dm}")
            for apd, aps_ in [(idn_f[:, :], idn_f_l), (onesrow_f[0:1, :], onesrow_l),
                              (dtbias[:, :], dtbias_c), (negA[:, :], alog_c),
                              (dcol[:, :], d_c), (cmask[:, :], cmask_c),
                              (convb[:, :], convb_l), (convwc[:, :], convw_cols),
                              (nw[:, :], nw_l)]:
                R.inc(sy and sy.dma_start(out=apd, in_=aps_), "sqs", 16)
            R.inc(None, "sqs", 0, mark="consts_f")

        def p0_vector(v):
            if v:
                v.memset(constc[:, 0:1], EPS)
                v.memset(constc[:, 1:2], 1.0)
            R.inc(v and v.last, "sv", 1, mark="p0.constc")
            for dm in range(NDT):
                R.wait(v, "sqs", f"p0.x.{dm}")
                R.inc(v and v.tensor_copy(xn[:, dm * 1024:(dm + 1) * 1024], xld[dm % 2]),
                      "sv", 1, mark=f"p0.xc.{dm}")
            R.wait(v, "st", f"p0.st.{NDT-1}")
            if v:
                v.tensor_scalar(mrow[0:1, 0:512], ps_mm[0][0:1, :], 1.0 / DM, None, ALU.mult)
                v.tensor_scalar(mrow[0:1, 512:1024], ps_mm[1][0:1, :], 1.0 / DM, None, ALU.mult)
            R.inc(v and v.last, "sv", 1, mark="p0.mrow")
            if v:
                v.tensor_scalar(rowbuf[0:1, 0:512], ps_mm[2][0:1, :], 1.0 / DM, None, ALU.mult)
                v.tensor_scalar(rowbuf[0:1, 512:1024], ps_mm[3][0:1, :], 1.0 / DM, None, ALU.mult)
                v.tensor_tensor(rowbuf2[0:1, :], mrow[0:1, :], mrow[0:1, :], ALU.mult)
                v.tensor_tensor(rowbuf[0:1, :], rowbuf[0:1, :], rowbuf2[0:1, :], ALU.subtract)
            R.inc(v and v.last, "sv", 1, mark="p0.var")
            for r, dst in enumerate([mb[:, 0:512], mb[:, 512:1024],
                                     rb[:, 0:512], rb[:, 512:1024]]):
                R.wait(v, "st", f"p0.bc.{r}")
                R.inc(v and v.tensor_copy(dst, ps_x[:, :]), "sv", 1, mark=f"p0.bce.{r}")
            for dm in range(NDT):
                if v:
                    sl = xn[:, dm * 1024:(dm + 1) * 1024]
                    v.tensor_tensor(sl, sl, mb[:, :], ALU.subtract)
                    v.tensor_tensor(sl, sl, rb[:, :], ALU.mult)
                R.inc(v and v.last, "sv", 1, mark=f"p0.app.{dm}")

        def p0_scalar(s):
            for dm in range(NDT):
                R.wait(s, "sv", f"p0.xc.{dm}")
                if dm >= 2:
                    R.wait(s, "st", f"p0.st.{dm-2}")
                R.inc(s and s.activation(xsq[dm % 2], xn[:, dm * 1024:(dm + 1) * 1024],
                                         AF.Square), "ss", 1, mark=f"p0.xq.{dm}")
            R.wait(s, "sv", "p0.var")
            R.wait(s, "sv", "p0.constc")
            if s:
                s.activation(rowbuf2[0:1, :], rowbuf[0:1, :], AF.Ln, bias=constc[0:1, 0:1])
                s.activation(rrow[0:1, :], rowbuf2[0:1, :], AF.Exp, scale=-0.5)
            R.inc(s and s.last, "ss", 1, mark="p0.rstd")

        def p0_tensor(t):
            R.wait(t, "sqg", "consts_b")
            for dm in range(NDT):
                R.wait(t, "sv", f"p0.xc.{dm}")
                R.wait(t, "ss", f"p0.xq.{dm}")
                if t:
                    for half in range(2):
                        t.matmul(ps_mm[half][0:1, :], onescol_b[:, :],
                                 xn_sl(dm, half), start=(dm == 0), stop=(dm == NDT - 1))
                        t.matmul(ps_mm[2 + half][0:1, :], onescol_b[:, :],
                                 xsq[dm % 2][:, half * 512:(half + 1) * 512],
                                 start=(dm == 0), stop=(dm == NDT - 1))
                R.inc(t and t.last, "st", 1, mark=f"p0.st.{dm}")
            R.wait(t, "sqs", "consts_f")
            srcs = [mrow[0:1, 0:512], mrow[0:1, 512:1024],
                    rrow[0:1, 0:512], rrow[0:1, 512:1024]]
            for r, src in enumerate(srcs):
                R.wait(t, "sv", "p0.mrow" if r < 2 else "p0.var")
                if r >= 2:
                    R.wait(t, "ss", "p0.rstd")
                if r >= 1:
                    R.wait(t, "sv", f"p0.bce.{r-1}")
                R.inc(t and t.matmul(ps_x[:, :], onesrow_f[0:1, :], src,
                                     start=True, stop=True), "st", 1, mark=f"p0.bc.{r}")

        def p1_gpsimd(g):
            R.inc(g and g.dma_start(out=wo_bf, in_=wo_l), "sqg", 16, mark="wo_precast")
            for apd, aps_ in [(idn_b[:, :], idn_f_l), (tribig[:, :], tribig_l),
                              (onescol_b[:, :], onescol_l), (keyst[:, :], keyst_l),
                              (wqt_all[:, :], wqt_l)]:
                R.inc(g and g.dma_start(out=apd, in_=aps_), "sqg", 16)
            R.inc(None, "sqg", 0, mark="consts_b")
            R.wait(g, "sv", f"p0.app.{NDT-1}")
            for i in range(NCT):
                ct = tile_ct(i)
                if i >= 2:
                    R.wait(g, "st", f"p1.mm.{i-2}")
                if is_xbc(i) and i >= 3 and is_xbc(i - 2):
                    R.wait(g, "st", f"p1.conv.{i-2}")
                for half in range(2):
                    R.inc(g and g.dma_start(out=wbuf[2 * (i % 2) + half],
                                            in_=w1_l[ct][:, half * 2048:(half + 1) * 2048]),
                          "sqg", 16, mark=f"p1.wl.{i}.{half}")
                if is_xbc(i):
                    R.inc(g and g.dma_start(out=cwb[i % 2], in_=convw_l[ct - 64]),
                          "sqg", 16, mark=f"p1.cw.{i}")
            R.wait(g, "sv", "p1.tails")
            R.inc(g and g.dma_start(out=tails_bi, in_=tails), "sqg", 16, mark="p1.tout")
            R.wait(g, "sqg", "p1.tout")
            R.inc(g and g.collective_compute(
                "AllGather", ALU.bypass, replica_groups=[[0, 1], [2, 3], [4, 5], [6, 7]],
                ins=[tails_bi.opt()], outs=[tails_bo.opt()]), "sc", 1, mark="p1.cc")
            R.wait(g, "sc", "p1.cc")
            R.inc(g and g.dma_start(out=pad3, in_=tails_bo[0:128, :]), "sqg", 16,
                  mark="p1.pad3")

        def p1_sync(sy):
            for i in range(1, NCT):
                ct = tile_ct(i)
                if is_xbc(i) and ct - 64 < 64:
                    j = ct - 64
                    R.wait(sy, "ss", f"p1.str.{i}")
                    R.inc(sy and sy.dma_start(
                        out=xtok_hbm[0:8, :, j * 128:(j + 1) * 128]
                        .rearrange("c p t -> p c t"),
                        in_=xstrip[i % 2].rearrange("p (c t) -> p c t", t=128)),
                        "sqs", 16, mark=f"p1.xsp.{i}")
                elif is_gate(i):
                    R.wait(sy, "ss", f"p1.ev.{i}")
                    R.inc(sy and sy.dma_start(
                        out=sg_hbm[:, :, ct * 128:(ct + 1) * 128]
                        .rearrange("c p t -> p c t"),
                        in_=hsfm[i % 2].rearrange("p (c t) -> p c t", t=128)),
                        "sqs", 16, mark=f"p1.sgsp.{i}")
            R.wait(sy, "ss", "p1.str0.63")
            R.inc(sy and sy.dma_start(out=xtok_hbm[0][0:3, :], in_=patcht[0:3, :]),
                  "sqs", 16, mark="p1.xsp0")
            R.fence("sqs", "p1.sync_done")

        def p1_tensor(t):
            R.wait(t, "sqg", "consts_b")
            for i in range(81):
                p1_tensor_tile(t, i)
            p1_tensor_prep(t)
            for i in range(81, NCT):
                p1_tensor_tile(t, i)
            for k in range(64):
                R.wait(t, "ss", "p1.patchs")
                if k >= 1:
                    R.wait(t, "ss", f"p1.str0.{k-1}")
                R.inc(t and t.transpose(ps_tr[0:3, (k % 8) * 128:(k % 8 + 1) * 128],
                                        patchh[:, k * 3:(k + 1) * 3], idn_b[:, :]),
                      "st", 1, mark=f"p1.tr0.{k}")

        def p1_tensor_tile(t, i):
            ct = tile_ct(i)
            if i >= 2:
                R.wait(t, evac_sem(i - 2), f"p1.ev.{i-2}")
            if i == 0:
                R.wait(t, "sv", "p0.var")
                R.wait(t, "sv", f"p0.app.{NDT-1}")
            if i == 1:
                R.wait(t, "sv", "p0.var")
            for half in range(2):
                R.wait(t, "sqg", f"p1.wl.{i}.{half}")
                if t:
                    for kt16 in range(16):
                        kt = half * 16 + kt16
                        for th in range(2):
                            t.matmul(ps_mm[2 * (i % 2) + th][:, :],
                                     wbuf[2 * (i % 2) + half][:, kt16 * 128:(kt16 + 1) * 128],
                                     xn_sl(kt, th), start=(kt == 0), stop=(kt == 31))
            R.inc(t and t.last, "st", 1, mark=f"p1.mm.{i}")
            if is_xbc(i):
                j = ct - 64
                R.wait(t, "sv", f"p1.ev.{i}")
                R.wait(t, "sqg", f"p1.cw.{i}")
                if i >= 2:
                    R.wait(t, "ss", f"p1.cve.{i-1}")
                if t:
                    for th in range(2):
                        for k in range(4):
                            t.matmul(ps_cv[th][:, :], cwb[i % 2][:, k * 128:(k + 1) * 128],
                                     xbcr[i % 2][:, k + th * 512: k + th * 512 + 512],
                                     start=(k == 0), stop=(k == 3))
                R.inc(t and t.last, "st", 1, mark=f"p1.conv.{i}")
                if j < 64:
                    R.wait(t, "ss", f"p1.cve.{i}")
                    if i >= 2 and 1 <= i - 1 <= 80 and tile_ct(i - 1) - 64 < 64:
                        R.wait(t, "ss", f"p1.str.{i-1}")
                    if t:
                        for c in range(8):
                            t.transpose(ps_tr[:, c * 128:(c + 1) * 128],
                                        hsfm[i % 2][:, c * 128:(c + 1) * 128], idn_b[:, :])
                    R.inc(t and t.last, "st", 1, mark=f"p1.tr.{i}")

        def p1_vector(v):
            R.wait(v, "sqs", "consts_f")
            if v:
                v.memset(xbcr[0][:, 0:3], 0.0)
                v.memset(xbcr[1][:, 0:3], 0.0)
            R.inc(v and v.last, "sv", 1, mark="p1.padz")
            for i in range(81):
                p1_vector_tile(v, i)
            p1_vector_prep(v)
            R.wait(v, "sqg", "p1.pad3")
            R.inc(v and v.tensor_scalar(pad3, pad3, cmask[:, 0:1], None, ALU.mult), "sv", 1)
            for j in range(80):
                if v:
                    v.tensor_copy(x6[:, 0:3], pad3[:, j * 3:(j + 1) * 3])
                    v.tensor_copy(x6[:, 3:6], first3[:, j * 3:(j + 1) * 3])
                    v.tensor_scalar(patch3[:, j * 3:(j + 1) * 3], x6[:, 0:3],
                                    convwc[:, 4 * j:4 * j + 1], None, ALU.mult)
                    for k in range(1, 4):
                        v.scalar_tensor_tensor(patch3[:, j * 3:(j + 1) * 3], x6[:, k:k + 3],
                                               convwc[:, 4 * j + k:4 * j + k + 1],
                                               patch3[:, j * 3:(j + 1) * 3], ALU.mult, ALU.add)
            R.inc(v and v.last, "sv", 1, mark="p1.patchv")

        def p1_vector_tile(v, i):
            ct = tile_ct(i)
            R.wait(v, "st", f"p1.mm.{i}")
            if i == 0:
                if v:
                    v.tensor_copy(dtr[:, 0:512], ps_mm[0][:, :])
                    v.tensor_copy(dtr[:, 512:1024], ps_mm[1][:, :])
                R.inc(v and v.last, "sv", 1, mark="p1.ev.0")
            elif is_xbc(i):
                j = ct - 64
                R.wait(v, "sv", "p1.padz")
                if i >= 3 and is_xbc(i - 2):
                    R.wait(v, "st", f"p1.conv.{i-2}")
                if v:
                    v.tensor_copy(xbcr[i % 2][:, 3:515], ps_mm[2 * (i % 2)][:, :])
                    v.tensor_copy(xbcr[i % 2][:, 515:1027], ps_mm[2 * (i % 2) + 1][:, :])
                    v.tensor_copy(first3[:, j * 3:(j + 1) * 3], xbcr[i % 2][:, 3:6])
                    v.tensor_copy(tails[:, j * 3:(j + 1) * 3], xbcr[i % 2][:, 1024:1027])
                R.inc(v and v.last, "sv", 1, mark=f"p1.ev.{i}")
                if i == 80:
                    R.inc(None, "sv", 0, mark="p1.tails")

        def p1_scalar(s):
            R.wait(s, "sqs", "consts_f")
            R.inc(s and s.activation(smallsc[:, 0:1], negA[:, 0:1], AF.Exp), "ss", 1,
                  mark="p1.alog")
            for i in range(1, 81):
                p1_scalar_tile(s, i)
            p1_scalar_prep(s)
            for i in range(81, NCT):
                p1_scalar_tile(s, i)
            R.wait(s, "sv", "p1.patchv")
            for j in range(80):
                dst = (patchh[:, j * 3:(j + 1) * 3] if j < 64 else
                       (B_sb[:, (j - 64) * 1024:(j - 64) * 1024 + 3] if j < 72 else
                        C_sb[:, (j - 72) * 1024:(j - 72) * 1024 + 3]))
                if s:
                    s.activation(dst, patch3[:, j * 3:(j + 1) * 3], AF.Silu,
                                 bias=convb[:, j:j + 1])
            R.inc(s and s.last, "ss", 1, mark="p1.patchs")
            for k in range(64):
                R.wait(s, "st", f"p1.tr0.{k}")
                R.inc(s and s.copy(patcht[0:3, k * 128:(k + 1) * 128],
                                   ps_tr[0:3, (k % 8) * 128:(k % 8 + 1) * 128]),
                      "ss", 1, mark=f"p1.str0.{k}")

        def p1_scalar_tile(s, i):
            ct = tile_ct(i)
            if is_gate(i):
                R.wait(s, "st", f"p1.mm.{i}")
                if i in (81, 82):
                    R.wait(s, "st", f"p1.tr.{i-18}")
                if i >= 83:
                    R.wait(s, "sqs", f"p1.sgsp.{i-2}")
                if s:
                    s.activation(hsfm[i % 2][:, 0:512], ps_mm[2 * (i % 2)][:, :], AF.Silu)
                    s.activation(hsfm[i % 2][:, 512:1024], ps_mm[2 * (i % 2) + 1][:, :],
                                 AF.Silu)
                R.inc(s and s.last, "ss", 1, mark=f"p1.ev.{i}")
                return
            j = ct - 64
            R.wait(s, "st", f"p1.conv.{i}")
            if j < 64:
                if i >= 3 and tile_ct(i - 2) - 64 < 64:
                    R.wait(s, "st", f"p1.tr.{i-2}")
                dst0, dst1 = hsfm[i % 2][:, 0:512], hsfm[i % 2][:, 512:1024]
            elif j < 72:
                dst0 = B_sb[:, (j - 64) * 1024:(j - 64) * 1024 + 512]
                dst1 = B_sb[:, (j - 64) * 1024 + 512:(j - 63) * 1024]
            else:
                dst0 = C_sb[:, (j - 72) * 1024:(j - 72) * 1024 + 512]
                dst1 = C_sb[:, (j - 72) * 1024 + 512:(j - 71) * 1024]
            if s:
                s.activation(dst0, ps_cv[0][:, :], AF.Silu, bias=convb[:, j:j + 1])
                s.activation(dst1, ps_cv[1][:, :], AF.Silu, bias=convb[:, j:j + 1])
            R.inc(s and s.last, "ss", 1, mark=f"p1.cve.{i}")
            if j < 64:
                R.wait(s, "st", f"p1.tr.{i}")
                if i >= 3 and tile_ct(i - 2) - 64 < 64:
                    R.wait(s, "sqs", f"p1.xsp.{i-2}")
                if s:
                    for c in range(8):
                        s.copy(xstrip[i % 2][:, c * 128:(c + 1) * 128],
                               ps_tr[:, c * 128:(c + 1) * 128])
                R.inc(s and s.last, "ss", 1, mark=f"p1.str.{i}")

        def p1_scalar_prep(s):
            R.wait(s, "sv", "p1.ev.0")
            if s:
                s.activation(dt_t, dtr, AF.Abs, bias=dtbias[:, 0:1])
                s.activation(tmpA, dt_t, AF.Exp, scale=-1.0)
                s.activation(dt_t, tmpA, AF.Ln, bias=constc[:, 1:2])
                s.activation(tmpA, dtr, AF.Relu, bias=dtbias[:, 0:1])
            R.inc(s and s.last, "ss", 1, mark="p1.sp1")
            R.wait(s, "sv", "p1.dt")
            R.inc(s and s.activation(tmpA, dt_t, AF.Ln), "ss", 1, mark="p1.lndt")
            R.wait(s, "sv", "p1.ncmb")
            R.inc(s and s.activation(dt_t, cum, AF.Exp), "ss", 1, mark="p1.expc")
            for c in range(NCH):
                R.inc(s and s.activation(tmpA[:, c * 128:(c + 1) * 128],
                                         ncmb[:, c * 128:(c + 1) * 128], AF.Exp,
                                         bias=cum[:, c * 128 + 127:(c + 1) * 128]),
                      "ss", 1, mark=f"p1.w.{c}")
            for c in range(NCH):
                R.wait(s, "st", f"p1.trn.{c}")
                R.inc(s and s.copy(ncmbT[:, c * 128:(c + 1) * 128],
                                   ps_x[:, (c % 2) * 128:(c % 2 + 1) * 128]),
                      "ss", 1, mark=f"p1.ncmbTe.{c}")
            for c in range(NCH):
                R.wait(s, "st", f"p1.trw.{c}")
                R.inc(s and s.copy(wTT[:, c * 128:(c + 1) * 128],
                                   ps_x[:, (c % 2) * 128:(c % 2 + 1) * 128]),
                      "ss", 1, mark=f"p1.wTe.{c}")
            for c in range(NCH):
                R.wait(s, "st", f"p1.trc.{c}")
                R.inc(s and s.copy(rowbuf[0:1, c * 128:(c + 1) * 128], ps_x[0:1, 256:384]),
                      "ss", 1, mark=f"p1.cdr.{c}")
            R.wait(s, "st", "p1.trd")
            R.inc(s and s.copy(mrow[0:1, 0:128], ps_x[0:1, 256:384]), "ss", 1, mark="p1.drow")

        def p1_vector_prep(v):
            R.wait(v, "ss", "p1.sp1")
            R.inc(v and v.tensor_tensor(dt_t, dt_t, tmpA, ALU.add), "sv", 1, mark="p1.dt")
            R.wait(v, "ss", "p1.alog")
            R.inc(v and v.tensor_scalar(negA[:, 0:1], smallsc[:, 0:1], -1.0, None, ALU.mult),
                  "sv", 1, mark="p1.negA")
            R.wait(v, "ss", "p1.lndt")
            if v:
                v.tensor_scalar(dtr, dt_t, negA[:, 0:1], None, ALU.mult)
                for c in range(NCH):
                    v.tensor_tensor_scan(cum[:, c * 128:(c + 1) * 128],
                                         dtr[:, c * 128:(c + 1) * 128],
                                         dtr[:, c * 128:(c + 1) * 128],
                                         0.0, ALU.add, ALU.bypass)
            R.inc(v and v.last, "sv", 1, mark="p1.cum")
            R.inc(v and v.tensor_tensor(ncmb, tmpA, cum, ALU.subtract), "sv", 1,
                  mark="p1.ncmb")
            R.wait(v, "ss", "p1.expc")
            R.inc(v and v.tensor_copy(expcum_b[:, :], dt_t), "sv", 1, mark="p1.expcb")
            if v:
                v.memset(rowbuf2[0:1, 0:128], 1.0)
            for c in range(1, NCH):
                R.wait(v, "ss", f"p1.cdr.{c-1}")
                if v:
                    v.tensor_tensor(rowbuf2[0:1, c * 128:(c + 1) * 128],
                                    rowbuf2[0:1, (c - 1) * 128: c * 128],
                                    rowbuf[0:1, (c - 1) * 128: c * 128], ALU.mult)
            R.inc(v and v.last, "sv", 1, mark="p1.dplrows")
            for c in range(NCH):
                R.wait(v, "st", f"p1.bcc.{c}")
                R.inc(v and v.tensor_copy(cdecb[:, c * 128:(c + 1) * 128],
                                          ps_x[:, 128 * (2 + c % 2): 128 * (3 + c % 2)]),
                      "sv", 1, mark=f"p1.cdecb.{c}")
            for c in range(NCH):
                R.wait(v, "st", f"p1.bcd.{c}")
                R.inc(v and v.tensor_scalar(dplb[:, c * 128:(c + 1) * 128],
                                            ps_x[:, 128 * (2 + c % 2): 128 * (3 + c % 2)],
                                            cmask[:, 0:1], None, ALU.mult),
                      "sv", 1, mark=f"p1.dplb.{c}")
            R.wait(v, "st", "p1.bcD")
            R.inc(v and v.tensor_copy(dtb[:, :], ps_x[:, 128:256]), "sv", 1, mark="p1.dtb")

        def p1_tensor_prep(t):
            for c in range(NCH):
                R.wait(t, "sv", "p1.ncmb")
                if c >= 1:
                    R.wait(t, "ss", f"p1.ncmbTe.{c-1}")
                R.inc(t and t.transpose(ps_x[:, (c % 2) * 128:(c % 2 + 1) * 128],
                                        ncmb[:, c * 128:(c + 1) * 128], idn_f[:, :]),
                      "st", 1, mark=f"p1.trn.{c}")
            for c in range(NCH):
                R.wait(t, "ss", f"p1.w.{c}")
                if c >= 1:
                    R.wait(t, "ss", f"p1.wTe.{c-1}")
                else:
                    R.wait(t, "ss", f"p1.ncmbTe.{NCH-1}")
                R.inc(t and t.transpose(ps_x[:, (c % 2) * 128:(c % 2 + 1) * 128],
                                        tmpA[:, c * 128:(c + 1) * 128], idn_f[:, :]),
                      "st", 1, mark=f"p1.trw.{c}")
            for c in range(NCH):
                R.wait(t, "ss", "p1.expc")
                if c >= 1:
                    R.wait(t, "ss", f"p1.cdr.{c-1}")
                R.inc(t and t.transpose(ps_x[0:1, 256:384],
                                        dt_t[:, c * 128 + 127:(c + 1) * 128], idn_f[:, :]),
                      "st", 1, mark=f"p1.trc.{c}")
            R.wait(t, "ss", f"p1.cdr.{NCH-1}")
            R.inc(t and t.transpose(ps_x[0:1, 256:384], dcol[:, 0:1], idn_f[:, :]),
                  "st", 1, mark="p1.trd")
            for c in range(NCH):
                R.wait(t, "ss", f"p1.cdr.{c}")
                if c == 0:
                    R.wait(t, "ss", "p1.drow")
                if c >= 1:
                    R.wait(t, "sv", f"p1.cdecb.{c-1}")
                R.inc(t and t.matmul(ps_x[:, 128 * (2 + c % 2): 128 * (3 + c % 2)],
                                     onesrow_f[0:1, :], rowbuf[0:1, c * 128:(c + 1) * 128],
                                     start=True, stop=True), "st", 1, mark=f"p1.bcc.{c}")
            for c in range(NCH):
                R.wait(t, "sv", "p1.dplrows")
                if c >= 1:
                    R.wait(t, "sv", f"p1.dplb.{c-1}")
                else:
                    R.wait(t, "sv", f"p1.cdecb.{NCH-1}")
                R.inc(t and t.matmul(ps_x[:, 128 * (2 + c % 2): 128 * (3 + c % 2)],
                                     onesrow_f[0:1, :], rowbuf2[0:1, c * 128:(c + 1) * 128],
                                     start=True, stop=True), "st", 1, mark=f"p1.bcd.{c}")
            R.wait(t, "ss", "p1.drow")
            R.wait(t, "sv", f"p1.dplb.{NCH-1}")
            R.inc(t and t.matmul(ps_x[:, 128:256], onesrow_f[0:1, :], mrow[0:1, 0:128],
                                 start=True, stop=True), "st", 1, mark="p1.bcD")

        def p2_tensor(t):
            R.wait(t, "sv", "p1.cum")
            R.wait(t, "sv", "p1.expcb")
            R.wait(t, "ss", f"p1.ev.{NCT-1}")
            R.wait(t, "ss", f"p1.ev.{NCT-2}")
            R.wait(t, "ss", "p1.patchs")
            R.wait(t, "ss", "p1.str0.63")
            for c in range(NCH):
                for w in range(2):
                    if c > 0 or w > 0:
                        pc, pw = (c, w - 1) if w > 0 else (c - 1, 1)
                        R.wait(t, "ss", f"p2.sre.{pc}.{pw}")
                    if c > 0 and w == 0:
                        R.wait(t, "sv", f"p2.scan.{c-1}.127")
                    if t:
                        for g_ in range(4 * w, 4 * w + 4):
                            t.matmul(
                                ps_cv[1][:, (g_ % 4) * 128:(g_ % 4 + 1) * 128],
                                B_sb[:, g_ * 1024 + c * 128: g_ * 1024 + (c + 1) * 128],
                                C_sb[:, g_ * 1024 + c * 128: g_ * 1024 + (c + 1) * 128],
                                start=True, stop=True)
                    R.inc(t and t.last, "st", 1, mark=f"p2.srw.{c}.{w}")
                    if c > 0 or w > 0:
                        pc, pw = (c, w - 1) if w > 0 else (c - 1, 1)
                        R.wait(t, "ss", f"p2.bte.{pc}.{pw}")
                    if t:
                        for g_ in range(4 * w, 4 * w + 4):
                            t.transpose(
                                ps_tr[:, g_ * 128:(g_ + 1) * 128],
                                B_sb[:, g_ * 1024 + c * 128: g_ * 1024 + (c + 1) * 128],
                                idn_b[:, :])
                    R.inc(t and t.last, "st", 1, mark=f"p2.btw.{c}.{w}")
                for j2 in range(128 + LA):
                    if j2 < 128:
                        h = j2
                        if h >= 2:
                            R.wait(t, "ss", f"p2.se.{c}.{h-2}")
                        elif c > 0:
                            R.wait(t, "ss", f"p2.se.{c-1}.127")
                        if t:
                            t.matmul(ps_mm[h % 2][:, 0:128],
                                     idn_f[:, h:h + 1].broadcast_to((128, 128)),
                                     cum[:, c * 128:(c + 1) * 128], start=True, stop=True)
                            t.matmul(ps_mm[h % 2][:, 128:256],
                                     idn_b[:, h:h + 1].broadcast_to((128, 128)),
                                     expcum_b[:, c * 128:(c + 1) * 128],
                                     start=True, stop=True)
                        R.inc(t and t.last, "st", 1, mark=f"p2.bc.{c}.{h}")
                    if j2 >= LA:
                        h = j2 - LA
                        if h == 0:
                            R.wait(t, "sqs", f"p2.xtok.{c}")
                            if c == 0:
                                R.wait(t, "sv", "p2.prev0")
                        R.wait(t, "sv", f"p2.dv.{c}.{h}")
                        p_ = h // 2
                        bank = ps_mm[2 + p_ % 2]
                        reg = bank[(h % 2) * 64:(h % 2) * 64 + 64, 0:128]
                        if h % 2 == 0:
                            if p_ >= 2:
                                R.wait(t, "ss", f"p2.ev.{c}.{p_-2}")
                            elif c > 0:
                                R.wait(t, "ss", f"p2.ev.{c-1}.{62 + p_}")
                        if c > 0:
                            R.wait(t, "sv", f"p2.scan.{c-1}.{h}")
                        if h >= 2:
                            R.wait(t, "sv", f"p2.scan.{c}.{h-2}")
                        elif c > 0:
                            R.wait(t, "sv", f"p2.scan.{c-1}.{126 + h}")
                        if h == 1:
                            R.wait(t, "ss", f"p2.sre.{c}.1")
                        if t:
                            t.matmul(reg, xtok_sb[c % 2][:, h * 64:(h + 1) * 64],
                                     stile[h % 4][:, :], start=True, stop=False)
                            t.matmul(reg, prev[:, h * 64:(h + 1) * 64],
                                     ctile[h % 4][:, :], start=False, stop=True)
                            t.matmul(ps_cv[h % 2][:, 256:320],
                                     bwt_s[h % 4][:, :],
                                     xtok_sb[c % 2][:, h * 64:(h + 1) * 64],
                                     start=True, stop=True)
                        R.inc(t and t.last, "st", 1, mark=f"p2.yg.{c}.{h}")

        def p2_scalar(s):
            for c in range(NCH):
                for w in range(2):
                    R.wait(s, "st", f"p2.srw.{c}.{w}")
                    if s:
                        for g_ in range(4 * w, 4 * w + 4):
                            s.copy(sraw[:, g_ * 128:(g_ + 1) * 128],
                                   ps_cv[1][:, (g_ % 4) * 128:(g_ % 4 + 1) * 128])
                    R.inc(s and s.last, "ss", 1, mark=f"p2.sre.{c}.{w}")
                    R.wait(s, "st", f"p2.btw.{c}.{w}")
                    if s:
                        for g_ in range(4 * w, 4 * w + 4):
                            s.copy(btok[:, g_ * 128:(g_ + 1) * 128],
                                   ps_tr[:, g_ * 128:(g_ + 1) * 128])
                    R.inc(s and s.last, "ss", 1, mark=f"p2.bte.{c}.{w}")
                for j2 in range(128 + LA):
                    if j2 < 128:
                        h = j2
                        R.wait(s, "st", f"p2.bc.{c}.{h}")
                        if h >= 4:
                            R.wait(s, "sv", f"p2.dv.{c}.{h-4}")
                        elif c > 0:
                            R.wait(s, "sv", f"p2.dv.{c-1}.127")
                        if s:
                            s.activation(etile[h % 4][:, :],
                                         ps_mm[h % 2][:, 0:128],
                                         AF.Exp,
                                         bias=ncmbT[:, c * 128 + h: c * 128 + h + 1])
                            s.copy(acb[h % 4][:, :], ps_mm[h % 2][:, 128:256])
                        R.inc(s and s.last, "ss", 1, mark=f"p2.se.{c}.{h}")
                    if j2 >= LA and (j2 - LA) % 2 == 1:
                        p_ = (j2 - LA) // 2
                        R.wait(s, "st", f"p2.yg.{c}.{2 * p_ + 1}")
                        if p_ == 0:
                            R.wait(s, "sqs", "p1.xsp0" if c == 0 else f"p2.ypsp.{c-1}")
                        bank = ps_mm[2 + p_ % 2]
                        R.inc(s and s.copy(ypc[:, p_ * 128:(p_ + 1) * 128],
                                           bank[:, 0:128]),
                              "ss", 1, mark=f"p2.ev.{c}.{p_}")

        def p2_vector(v):
            R.wait(v, "sqg", "consts_b")
            R.wait(v, "ss", f"p1.wTe.{NCH-1}")
            R.wait(v, "st", f"p1.mm.{NCT-1}")
            R.inc(v and v.memset(prev[:, :], 0.0), "sv", 1, mark="p2.prev0")
            for c in range(NCH):
                for j2 in range(128 + LA):
                    if j2 < 128:
                        h = j2
                        gr = h // 16
                        R.wait(v, "ss", f"p2.se.{c}.{h}")
                        R.wait(v, "ss", f"p2.sre.{c}.{h // 64}")
                        R.wait(v, "ss", f"p2.bte.{c}.{h // 64}")
                        if v:
                            v.tensor_tensor(stile[h % 4][:, :], etile[h % 4][:, :],
                                            tribig[:, :], ALU.min)
                            v.tensor_tensor(stile[h % 4][:, :], stile[h % 4][:, :],
                                            sraw[:, gr * 128:(gr + 1) * 128], ALU.mult)
                            v.scalar_tensor_tensor(stile[h % 4][:, :], idn_b[:, :],
                                                   dtb[:, h:h + 1], stile[h % 4][:, :],
                                                   ALU.mult, ALU.add)
                            v.tensor_tensor(ctile[h % 4][:, :],
                                            C_sb[:, gr * 1024 + c * 128:
                                                 gr * 1024 + (c + 1) * 128],
                                            acb[h % 4][:, :], ALU.mult)
                            v.tensor_scalar(bwt_s[h % 4][:, :],
                                            btok[:, gr * 128:(gr + 1) * 128],
                                            wTT[:, c * 128 + h: c * 128 + h + 1],
                                            None, ALU.mult)
                        R.inc(v and v.last, "sv", 1, mark=f"p2.dv.{c}.{h}")
                    if j2 >= LA:
                        h = j2 - LA
                        R.wait(v, "st", f"p2.yg.{c}.{h}")
                        R.inc(v and v.scalar_tensor_tensor(
                            prev[:, h * 64:(h + 1) * 64], prev[:, h * 64:(h + 1) * 64],
                            cdecb[:, c * 128 + h: c * 128 + h + 1],
                            ps_cv[h % 2][:, 256:320],
                            ALU.mult, ALU.add), "sv", 1, mark=f"p2.scan.{c}.{h}")
            R.inc(None, "sv", 0, mark="p2.scan_done")

        def p2_sync(sy):
            for c in range(NCH):
                R.wait(sy, "sqs", "p1.sync_done")
                R.wait(sy, "st", f"p1.mm.{NCT-1}")
                if c >= 2:
                    R.wait(sy, "st", f"p2.yg.{c-2}.127")
                R.inc(sy and sy.dma_start(out=xtok_sb[c % 2][:, :], in_=xtok_hbm[c]),
                      "sqs", 16, mark=f"p2.xtok.{c}")
                R.wait(sy, "ss", f"p2.ev.{c}.63")
                R.inc(sy and sy.dma_start(out=yp_hbm[c], in_=ypc[:, :]), "sqs", 16,
                      mark=f"p2.ypsp.{c}")
            R.fence("sqs", "p2.sync_done")

        def p2_gpsimd(g):
            R.wait(g, "sv", "p2.scan_done")
            R.inc(g and g.dma_start(out=carry_bi, in_=prev[:, :]), "sqg", 16, mark="p2.cout")
            R.wait(g, "sqg", "p2.cout")
            R.inc(g and g.collective_compute(
                "AllGather", ALU.bypass, replica_groups=[[0, 1], [2, 3], [4, 5], [6, 7]],
                ins=[carry_bi.opt()], outs=[carry_bo.opt()]), "sc", 1, mark="p2.cc")
            R.wait(g, "sc", "p2.cc")
            R.inc(g and g.dma_start(out=carry[:, :], in_=carry_bo[0:128, :]), "sqg", 16,
                  mark="p2.cin")

        def p3_tensor(t):
            for hf in range(2):
                for c in range(4 * hf, 4 * hf + 4):
                    for j2 in range(128 + LA):
                        if j2 < 128:
                            h = j2
                            if h >= 2:
                                R.wait(t, "ss", f"p3.se.{c}.{h-2}")
                            elif c > 0:
                                R.wait(t, "ss", f"p3.se.{c-1}.127")
                            else:
                                R.wait(t, "ss", "p2.se.7.127")
                            R.inc(t and t.matmul(
                                ps_mm[h % 2][:, 0:128],
                                idn_b[:, h:h + 1].broadcast_to((128, 128)),
                                expcum_b[:, c * 128:(c + 1) * 128],
                                start=True, stop=True), "st", 1, mark=f"p3.bc.{c}.{h}")
                        if j2 >= LA:
                            h = j2 - LA
                            R.wait(t, "sv", f"p3.dv.{c}.{h}")
                            p_ = h // 2
                            bank = ps_mm[2 + p_ % 2]
                            reg = bank[(h % 2) * 64:(h % 2) * 64 + 64, 0:128]
                            if h % 2 == 0:
                                if p_ >= 2:
                                    R.wait(t, "sv", f"p3.pr.{c}.{p_-2}")
                                elif c > 0:
                                    R.wait(t, "sv", f"p3.pr.{c-1}.{62 + p_}")
                                else:
                                    R.wait(t, "ss", "p2.ev.7.63")
                            R.inc(t and t.matmul(reg, bwts3[h % 4][:, 0:64],
                                                 ctile3[h % 4][:, :],
                                                 start=True, stop=True),
                                  "st", 1, mark=f"p3.y2.{c}.{h}")
                    for p_ in range(64):
                        R.wait(t, "ss", f"p3.sq.{c}.{p_}")
                        if p_ == 0:
                            if c == 0:
                                R.wait(t, "ss", "p2.sre.7.1")
                                R.wait(t, "sv", "p2.scan.7.127")
                            else:
                                R.wait(t, "ss", f"p3.rstd.{c-1}")
                        R.inc(t and t.matmul(
                            ps_cv[1][:, (c % 2) * 8 + p_ // 8:(c % 2) * 8 + p_ // 8 + 1],
                            g2t[p_ % 2][:, :], onescol_b[:, :],
                            start=(p_ % 8 == 0), stop=(p_ % 8 == 7)),
                            "st", 1, mark=f"p3.vm.{c}.{p_}")
                    R.inc(None, "st", 0, mark=f"p3.var.{c}")
                    for g_ in range(8):
                        R.wait(t, "ss", f"p3.rstd.{c}")
                        if g_ >= 1:
                            R.wait(t, "ss", f"p3.re.{c}.{g_-1}")
                        R.inc(t and t.transpose(ps_x[0:1, 384:512],
                                                rstdT[:, g_:g_ + 1], idn_f[:, :]),
                              "st", 1, mark=f"p3.rt.{c}.{g_}")
                    for g_ in range(8):
                        R.wait(t, "ss", f"p3.re.{c}.7")
                        if g_ >= 1:
                            R.wait(t, "sv", f"p3.gn.{c}.{8 * (g_ - 1) + 7}")
                        R.inc(t and t.matmul(ps_x[:, (g_ % 3) * 128:(g_ % 3 + 1) * 128],
                                             onesrow_f[0:1, :],
                                             rowbuf[0:1, g_ * 128:(g_ + 1) * 128],
                                             start=True, stop=True),
                              "st", 1, mark=f"p3.rb.{c}.{g_}")
                for dm in range(NDT):
                    R.wait(t, "sqs", f"p3.wo.{hf}.{dm}")
                    R.wait(t, "sv", f"p3.gn.{4*hf+3}.63")
                    if dm >= 2:
                        R.wait(t, "sv", f"p3.x2.{hf}.{dm-2}")
                    if t:
                        for kt in range(64):
                            t.matmul(ps_mm[2 + dm % 2][:, :],
                                     wob[dm % 2][:, kt * 128:(kt + 1) * 128],
                                     ghalf[:, kt * 512:(kt + 1) * 512],
                                     start=(kt == 0), stop=(kt == 63))
                    R.inc(t and t.last, "st", 1, mark=f"p3.omm.{hf}.{dm}")
                p4_tensor_half(t, hf)

        def p3_scalar(s):
            for hf in range(2):
                for c in range(4 * hf, 4 * hf + 4):
                    for j2 in range(128 + LA):
                        if j2 < 128:
                            h = j2
                            R.wait(s, "st", f"p3.bc.{c}.{h}")
                            if h >= 4:
                                R.wait(s, "sv", f"p3.dv.{c}.{h-4}")
                            elif c > 0:
                                R.wait(s, "sv", f"p3.dv.{c-1}.127")
                            R.inc(s and s.copy(acb3[h % 4][:, :],
                                               ps_mm[h % 2][:, 0:128]),
                                  "ss", 1, mark=f"p3.se.{c}.{h}")
                    for p_ in range(64):
                        R.wait(s, "sv", f"p3.pr.{c}.{p_}")
                        if p_ >= 2:
                            R.wait(s, "st", f"p3.vm.{c}.{p_-2}")
                        R.inc(s and s.activation(
                            g2t[p_ % 2][:, :],
                            ghalf[:, p_ * 512 + (c % 4) * 128:
                                  p_ * 512 + (c % 4 + 1) * 128],
                            AF.Square), "ss", 1, mark=f"p3.sq.{c}.{p_}")
                    R.wait(s, "st", f"p3.var.{c}")
                    if s:
                        s.activation(smallsc[:, 0:8],
                                     ps_cv[1][:, (c % 2) * 8:(c % 2) * 8 + 8],
                                     AF.Ln, scale=1.0 / 1024.0, bias=constc[:, 0:1])
                        s.activation(rstdT[:, :], smallsc[:, 0:8], AF.Exp, scale=-0.5)
                    R.inc(s and s.last, "ss", 1, mark=f"p3.rstd.{c}")
                    for g_ in range(8):
                        R.wait(s, "st", f"p3.rt.{c}.{g_}")
                        R.inc(s and s.copy(rowbuf[0:1, g_ * 128:(g_ + 1) * 128],
                                           ps_x[0:1, 384:512]),
                              "ss", 1, mark=f"p3.re.{c}.{g_}")
                p4_scalar_half(s, hf)

        def p3_vector(v):
            for hf in range(2):
                for c in range(4 * hf, 4 * hf + 4):
                    for j2 in range(128 + LA):
                        if j2 < 128:
                            h = j2
                            gr = h // 16
                            R.wait(v, "ss", f"p3.se.{c}.{h}")
                            if h == 0 and c == 0:
                                R.wait(v, "sqg", "p2.cin")
                            if h >= 4:
                                R.wait(v, "st", f"p3.y2.{c}.{h-4}")
                            elif c > 0:
                                R.wait(v, "st", f"p3.y2.{c-1}.127")
                            if v:
                                v.tensor_tensor(ctile3[h % 4][:, :],
                                                C_sb[:, gr * 1024 + c * 128:
                                                     gr * 1024 + (c + 1) * 128],
                                                acb3[h % 4][:, :], ALU.mult)
                                v.tensor_scalar(bwts3[h % 4][:, 0:64],
                                                carry[:, h * 64:(h + 1) * 64],
                                                dplb[:, c * 128 + h: c * 128 + h + 1],
                                                None, ALU.mult)
                            R.inc(v and v.last, "sv", 1, mark=f"p3.dv.{c}.{h}")
                        if j2 >= LA and (j2 - LA) % 2 == 1:
                            p_ = (j2 - LA) // 2
                            if c == 0 and p_ == 0:
                                R.wait(v, "sqs", "p2.sync_done")
                            R.wait(v, "st", f"p3.y2.{c}.{2 * p_ + 1}")
                            R.wait(v, "sqs", f"p3.yp.{c}")
                            R.wait(v, "sqs", f"p3.sg.{c}")
                            bank = ps_mm[2 + p_ % 2]
                            if v:
                                v.tensor_tensor(ttile[:, :], ypr[:, p_ * 128:(p_ + 1) * 128],
                                                bank[:, 0:128], ALU.add)
                                v.tensor_tensor(
                                    ghalf[:, p_ * 512 + (c % 4) * 128:
                                          p_ * 512 + (c % 4 + 1) * 128],
                                    ttile[:, :], sgr[:, p_ * 128:(p_ + 1) * 128], ALU.mult)
                            R.inc(v and v.last, "sv", 1, mark=f"p3.pr.{c}.{p_}")
                    for p_ in range(64):
                        g_ = p_ // 8
                        R.wait(v, "st", f"p3.rb.{c}.{g_}")
                        sl = ghalf[:, p_ * 512 + (c % 4) * 128: p_ * 512 + (c % 4 + 1) * 128]
                        R.inc(v and v.scalar_tensor_tensor(
                            sl, sl, nw[:, p_:p_ + 1],
                            ps_x[:, (g_ % 3) * 128:(g_ % 3 + 1) * 128],
                            ALU.mult, ALU.mult), "sv", 1, mark=f"p3.gn.{c}.{p_}")
                for dm in range(NDT):
                    R.wait(v, "st", f"p3.omm.{hf}.{dm}")
                    R.wait(v, "sqs", f"p3.xr.{hf}.{dm}")
                    if dm >= 2:
                        R.wait(v, "sqs", f"p3.x2s.{hf}.{dm-2}")
                    R.inc(v and v.tensor_tensor(x2t[dm % 2][:, :], xre[dm % 2][:, :],
                                                ps_mm[2 + dm % 2][:, :], ALU.add),
                          "sv", 1, mark=f"p3.x2.{hf}.{dm}")
                p4_vector_half(v, hf)

        def p3_sync(sy):
            for hf in range(2):
                for c in range(4 * hf, 4 * hf + 4):
                    if c == 0:
                        R.wait(sy, "st", "p2.yg.7.127")
                        R.wait(sy, "sv", "p2.scan.7.127")
                        R.wait(sy, "ss", "p2.ev.7.63")
                    if c == 4:
                        R.wait(sy, "st", f"p3.omm.0.{NDT-1}")
                    if c % 4 != 0:
                        R.wait(sy, "sv", f"p3.pr.{c-1}.63")
                    R.wait(sy, "sqs", "p1.sync_done")
                    R.wait(sy, "sqs", f"p2.ypsp.{c}")
                    R.inc(sy and sy.dma_start(out=ypr[:, :], in_=yp_hbm[c]),
                          "sqs", 16, mark=f"p3.yp.{c}")
                    R.inc(sy and sy.dma_start(out=sgr[:, :], in_=sg_hbm[c]),
                          "sqs", 16, mark=f"p3.sg.{c}")
                for dm in range(NDT):
                    R.wait(sy, "sqg", "wo_precast")
                    R.wait(sy, "sv", f"p3.pr.{4*hf+3}.63")
                    if dm >= 2:
                        R.wait(sy, "st", f"p3.omm.{hf}.{dm-2}")
                    R.inc(sy and sy.dma_start(out=wob[dm % 2][:, :], in_=wo_bf[dm]),
                          "sqs", 16, mark=f"p3.wo.{hf}.{dm}")
                    if dm == 0:
                        R.wait(sy, "sqg", "p2.cout")
                    if dm >= 2:
                        R.wait(sy, "sv", f"p3.x2.{hf}.{dm-2}")
                    R.inc(sy and sy.dma_start(out=xre[dm % 2][:, :],
                                              in_=x_fm[dm][:, hf * 512:(hf + 1) * 512]),
                          "sqs", 16, mark=f"p3.xr.{hf}.{dm}")
                    R.wait(sy, "sv", f"p3.x2.{hf}.{dm}")
                    R.inc(sy and sy.dma_start(out=x2_hbm[dm][:, hf * 512:(hf + 1) * 512],
                                              in_=x2t[dm % 2][:, :]),
                          "sqs", 16, mark=f"p3.x2s.{hf}.{dm}")
                p4_sync_half(sy, hf)

        def p4_tensor_half(t, hf):
            for dm in range(NDT):
                R.wait(t, "sv", f"p4.xc.{hf}.{dm}")
                R.wait(t, "ss", f"p4.sq.{hf}.{dm}")
                if dm == 0:
                    R.wait(t, "ss", f"p3.se.{4*hf+3}.127")
                if t:
                    t.matmul(ps_mm[0][0:1, :], onescol_b[:, :],
                             xn2[:, dm * 512:(dm + 1) * 512],
                             start=(dm == 0), stop=(dm == NDT - 1))
                    t.matmul(ps_mm[1][0:1, :], onescol_b[:, :],
                             zgsb[:, (dm % 2) * 512:(dm % 2 + 1) * 512],
                             start=(dm == 0), stop=(dm == NDT - 1))
                R.inc(t and t.last, "st", 1, mark=f"p4.st.{hf}.{dm}")
            for r in range(2):
                R.wait(t, "sv", f"p4.var.{hf}")
                R.wait(t, "sv", f"p3.gn.{4*hf+3}.63")
                if r == 1:
                    R.wait(t, "ss", f"p4.rstd.{hf}")
                    R.wait(t, "sv", f"p4.bce.{hf}.0")
                src = mrow[0:1, 0:512] if r == 0 else rrow[0:1, 0:512]
                R.inc(t and t.matmul(ps_x[:, 0:512], onesrow_f[0:1, :], src,
                                     start=True, stop=True), "st", 1,
                      mark=f"p4.bc.{hf}.{r}")
            for tt in range(4):
                for kt in range(32):
                    R.wait(t, "sv", f"p4.xn2.{hf}.{kt}")
                    if kt == 0 and tt >= 1:
                        R.wait(t, "sv", f"p4.qe.{hf}.{tt-1}")
                    if tt == 0 and kt == 0 and hf == 0:
                        R.wait(t, "sv", "p2.scan.7.127")
                    if t:
                        t.matmul(ps_cv[0][:, tt * 16:(tt + 1) * 16],
                                 xn2[:, kt * 512 + tt * 128: kt * 512 + (tt + 1) * 128],
                                 wqt_all[:, kt * 16:(kt + 1) * 16],
                                 start=(kt == 0), stop=(kt == 31))
                R.inc(t and t.last, "st", 1, mark=f"p4.q.{hf}.{tt}")
                R.wait(t, "sv", f"p4.qe.{hf}.{tt}")
                if tt == 0 and hf == 1:
                    R.wait(t, "ss", "p4.zte.0.3.3")
                if tt == 0 and hf == 0:
                    R.wait(t, "ss", "p2.bte.7.1")
                if tt >= 1:
                    R.wait(t, "ss", f"p4.qte.{hf}.{tt-1}")
                R.inc(t and t.transpose(ps_tr[0:16, 0:128], qsb[:, :], idn_b[:, :]),
                      "st", 1, mark=f"p4.qt.{hf}.{tt}")
                R.wait(t, "ss", f"p4.qte.{hf}.{tt}")
                if tt >= 1:
                    R.wait(t, "sv", f"p4.m8.{hf}.{tt-1}")
                else:
                    R.wait(t, "ss", f"p3.rstd.{4*hf+3}")
                    if hf == 1:
                        R.wait(t, "sv", "p4.m8.0.3")
                R.inc(t and t.matmul(ps_cv[1][:, 448:512], qT[0:16, :], keyst[:, :],
                                     start=True, stop=True), "st", 1,
                      mark=f"p4.sc.{hf}.{tt}")
            for kt in range(32):
                R.wait(t, "sqg", f"p4.aw.{hf}.{kt}")
                if kt == 0:
                    R.wait(t, "sv", f"p4.var.{hf}")
                    R.wait(t, "sv", f"p3.x2.{hf}.{NDT-1}")
                if t:
                    for tt in range(4):
                        t.matmul(ps_mm[tt][:, :],
                                 xn2[:, kt * 512 + tt * 128: kt * 512 + (tt + 1) * 128],
                                 awtb[kt % 2][:, :], start=(kt == 0), stop=(kt == 31))
                R.inc(t and t.last, "st", 1, mark=f"p4.zk.{hf}.{kt}")
            for tt in range(4):
                for kb in range(4):
                    R.wait(t, "sv", f"p4.zg.{hf}.{tt}")
                    sl = 4 * tt + kb
                    if sl >= 1:
                        R.wait(t, "ss", f"p4.zte.{hf}.{(sl - 1) // 4}.{(sl - 1) % 4}")
                    R.inc(t and t.transpose(ps_tr[:, (sl % 8) * 128:(sl % 8 + 1) * 128],
                                            zgsb[:, tt * 512 + kb * 128:
                                                 tt * 512 + (kb + 1) * 128],
                                            idn_b[:, :]), "st", 1,
                          mark=f"p4.zt.{hf}.{tt}.{kb}")
            for dm in range(NDT):
                R.wait(t, "sqg", f"p4.bw.{hf}.{dm}")
                R.wait(t, "ss", f"p4.zte.{hf}.3.3")
                if dm >= 2:
                    R.wait(t, "sv", f"p4.fin.{hf}.{dm-2}")
                if t:
                    for kb in range(4):
                        t.matmul(ps_mm[2 + dm % 2][:, :],
                                 bwtd[dm % 2][:, kb * 128:(kb + 1) * 128],
                                 zgt_half[:, kb * 512:(kb + 1) * 512],
                                 start=(kb == 0), stop=(kb == 3))
                R.inc(t and t.last, "st", 1, mark=f"p4.o.{hf}.{dm}")

        def p4_scalar_half(s, hf):
            for dm in range(NDT):
                R.wait(s, "sv", f"p4.xc.{hf}.{dm}")
                if dm >= 2:
                    R.wait(s, "st", f"p4.st.{hf}.{dm-2}")
                R.inc(s and s.activation(zgsb[:, (dm % 2) * 512:(dm % 2 + 1) * 512],
                                         xn2[:, dm * 512:(dm + 1) * 512], AF.Square),
                      "ss", 1, mark=f"p4.sq.{hf}.{dm}")
            R.wait(s, "sv", f"p4.var.{hf}")
            if s:
                s.activation(rowbuf2[0:1, 0:512], rowbuf[0:1, 0:512], AF.Ln,
                             bias=constc[0:1, 0:1])
                s.activation(rrow[0:1, 0:512], rowbuf2[0:1, 0:512], AF.Exp, scale=-0.5)
            R.inc(s and s.last, "ss", 1, mark=f"p4.rstd.{hf}")
            for tt in range(4):
                R.wait(s, "st", f"p4.qt.{hf}.{tt}")
                R.inc(s and s.copy(qT[0:16, :], ps_tr[0:16, 0:128]), "ss", 1,
                      mark=f"p4.qte.{hf}.{tt}")
                R.wait(s, "sv", f"p4.m8.{hf}.{tt}")
                R.inc(s and s.mul(negt1[:, 0:1], m8a[:, 0:1], -1.0), "ss", 1,
                      mark=f"p4.nt1.{hf}.{tt}")
                R.wait(s, "sv", f"p4.m8b.{hf}.{tt}")
                R.inc(s and s.activation(esb[:, :], ssb[:, :], AF.Exp,
                                         bias=negt1[:, 0:1]), "ss", 1,
                      mark=f"p4.e.{hf}.{tt}")
                R.wait(s, "sv", f"p4.rs.{hf}.{tt}")
                if s:
                    s.activation(smallsc[:, 1:2], rs1[:, 0:1], AF.Ln)
                    s.activation(rs1[:, 0:1], smallsc[:, 1:2], AF.Exp, scale=-1.0)
                R.inc(s and s.last, "ss", 1, mark=f"p4.rr.{hf}.{tt}")
            for tt in range(4):
                for kb in range(4):
                    R.wait(s, "st", f"p4.zt.{hf}.{tt}.{kb}")
                    sl = 4 * tt + kb
                    R.inc(s and s.copy(zgt_half[:, kb * 512 + tt * 128:
                                                kb * 512 + (tt + 1) * 128],
                                       ps_tr[:, (sl % 8) * 128:(sl % 8 + 1) * 128]),
                          "ss", 1, mark=f"p4.zte.{hf}.{tt}.{kb}")

        def p4_vector_half(v, hf):
            for dm in range(NDT):
                R.wait(v, "sqs", f"p4.x.{hf}.{dm}")
                if dm == 0:
                    R.wait(v, "st", f"p3.omm.{hf}.{NDT-1}")
                R.inc(v and v.tensor_copy(xn2[:, dm * 512:(dm + 1) * 512],
                                          x2l[dm % 2][:, :]),
                      "sv", 1, mark=f"p4.xc.{hf}.{dm}")
            R.wait(v, "st", f"p4.st.{hf}.{NDT-1}")
            if v:
                v.tensor_scalar(mrow[0:1, 0:512], ps_mm[0][0:1, :], 1.0 / DM, None, ALU.mult)
                v.tensor_scalar(rowbuf[0:1, 0:512], ps_mm[1][0:1, :], 1.0 / DM, None,
                                ALU.mult)
                v.tensor_tensor(rowbuf2[0:1, 0:512], mrow[0:1, 0:512], mrow[0:1, 0:512],
                                ALU.mult)
                v.tensor_tensor(rowbuf[0:1, 0:512], rowbuf[0:1, 0:512],
                                rowbuf2[0:1, 0:512], ALU.subtract)
            R.inc(v and v.last, "sv", 1, mark=f"p4.var.{hf}")
            for r, dst in enumerate([mb2[:, :], rb2[:, :]]):
                R.wait(v, "st", f"p4.bc.{hf}.{r}")
                R.inc(v and v.tensor_copy(dst, ps_x[:, 0:512]), "sv", 1,
                      mark=f"p4.bce.{hf}.{r}")
            for dm in range(NDT):
                if v:
                    sl = xn2[:, dm * 512:(dm + 1) * 512]
                    v.tensor_tensor(sl, sl, mb2[:, :], ALU.subtract)
                    v.tensor_tensor(sl, sl, rb2[:, :], ALU.mult)
                R.inc(v and v.last, "sv", 1, mark=f"p4.xn2.{hf}.{dm}")
            for tt in range(4):
                R.wait(v, "st", f"p4.q.{hf}.{tt}")
                R.inc(v and v.tensor_copy(qsb[:, :], ps_cv[0][:, tt * 16:(tt + 1) * 16]),
                      "sv", 1, mark=f"p4.qe.{hf}.{tt}")
                R.wait(v, "st", f"p4.sc.{hf}.{tt}")
                if v:
                    v.tensor_copy(ssb[:, :], ps_cv[1][:, 448:512])
                    v.max(m8a[:, :], ssb[:, :])
                R.inc(v and v.last, "sv", 1, mark=f"p4.m8.{hf}.{tt}")
                R.wait(v, "ss", f"p4.nt1.{hf}.{tt}")
                if v:
                    v.tensor_scalar(msk[:, :], ssb[:, :], m8a[:, 7:8], None, ALU.is_ge)
                    v.scalar_tensor_tensor(esb[:, :], msk[:, :], -1e38, ssb[:, :],
                                           ALU.mult, ALU.add)
                    v.max(m8b[:, :], esb[:, :])
                R.inc(v and v.last, "sv", 1, mark=f"p4.m8b.{hf}.{tt}")
                R.wait(v, "ss", f"p4.e.{hf}.{tt}")
                if v:
                    v.tensor_scalar(msk[:, :], ssb[:, :], m8b[:, 7:8], None, ALU.is_ge)
                    v.tensor_tensor(esb[:, :], esb[:, :], msk[:, :], ALU.mult)
                    v.tensor_reduce(rs1[:, 0:1], esb[:, :], AX.X, ALU.add)
                R.inc(v and v.last, "sv", 1, mark=f"p4.rs.{hf}.{tt}")
                R.wait(v, "ss", f"p4.rr.{hf}.{tt}")
                R.inc(v and v.tensor_scalar(gsb[:, tt * 64:(tt + 1) * 64], esb[:, :],
                                            rs1[:, 0:1], None,
                                            ALU.mult), "sv", 1, mark=f"p4.g.{hf}.{tt}")
            for tt in range(4):
                R.wait(v, "st", f"p4.zk.{hf}.{31}")
                R.wait(v, "sv", f"p4.g.{hf}.{tt}")
                zv = ps_mm[tt][:, :].rearrange("p (e r) -> p e r", r=8)
                gv = gsb[:, tt * 64:(tt + 1) * 64].unsqueeze(2).broadcast_to((128, 64, 8))
                R.inc(v and v.tensor_tensor(
                    zgsb[:, tt * 512:(tt + 1) * 512].rearrange("p (e r) -> p e r", r=8),
                    zv, gv, ALU.mult), "sv", 1, mark=f"p4.zg.{hf}.{tt}")
            for dm in range(NDT):
                R.wait(v, "st", f"p4.o.{hf}.{dm}")
                R.wait(v, "sqs", f"p4.x2b.{hf}.{dm}")
                if dm >= 2:
                    R.wait(v, "sqs", f"p4.out.{hf}.{dm-2}")
                R.inc(v and v.scalar_tensor_tensor(
                    outbuf[dm % 2][:, :], ps_mm[2 + dm % 2][:, :], 2.0,
                    x2l[dm % 2][:, :], ALU.mult, ALU.add), "sv", 1,
                    mark=f"p4.fin.{hf}.{dm}")

        def p4_sync_half(sy, hf):
            for dm in range(NDT):
                R.wait(sy, "sqs", f"p3.x2s.{hf}.{dm}")
                if dm >= 2:
                    R.wait(sy, "sv", f"p4.xc.{hf}.{dm-2}")
                R.inc(sy and sy.dma_start(out=x2l[dm % 2][:, :],
                                          in_=x2_hbm[dm][:, hf * 512:(hf + 1) * 512]),
                      "sqs", 16, mark=f"p4.x.{hf}.{dm}")
            for dm in range(NDT):
                if dm < 2:
                    R.wait(sy, "sv", f"p4.xn2.{hf}.{NDT-1}")
                else:
                    R.wait(sy, "sv", f"p4.fin.{hf}.{dm-2}")
                R.inc(sy and sy.dma_start(out=x2l[dm % 2][:, :],
                                          in_=x2_hbm[dm][:, hf * 512:(hf + 1) * 512]),
                      "sqs", 16, mark=f"p4.x2b.{hf}.{dm}")
                R.wait(sy, "sv", f"p4.fin.{hf}.{dm}")
                R.inc(sy and sy.dma_start(out=out_fm[dm][:, hf * 512:(hf + 1) * 512],
                                          in_=outbuf[dm % 2][:, :]),
                      "sqs", 16, mark=f"p4.out.{hf}.{dm}")

        def p4_gpsimd(g):
            for hf in range(2):
                for kt in range(32):
                    if kt >= 2:
                        R.wait(g, "st", f"p4.zk.{hf}.{kt-2}")
                    else:
                        R.wait(g, "sv", f"p3.x2.{hf}.{NDT-1}")
                    R.inc(g and g.dma_start(out=awtb[kt % 2][:, :], in_=awt_l[kt]),
                          "sqg", 16, mark=f"p4.aw.{hf}.{kt}")
                for dm in range(NDT):
                    if dm >= 2:
                        R.wait(g, "st", f"p4.o.{hf}.{dm-2}")
                    R.inc(g and g.dma_start(out=bwtd[dm % 2][:, :], in_=bwt_l[dm]),
                          "sqg", 16, mark=f"p4.bw.{hf}.{dm}")

        def g_body(g):
            p1_gpsimd(g)
            if STAGE != "p1":
                p2_gpsimd(g)
                p4_gpsimd(g)

        def sy_body(sy):
            p0_sync(sy)
            p1_sync(sy)
            if STAGE != "p1":
                p2_sync(sy)
                p3_sync(sy)
            R.wait_all(sy, "sqs")
            R.wait_all(sy, "sqg")

        def t_body(t):
            p0_tensor(t)
            p1_tensor(t)
            if STAGE != "p1":
                p2_tensor(t)
                p3_tensor(t)

        def v_body(v):
            p0_vector(v)
            p1_vector(v)
            if STAGE != "p1":
                p2_vector(v)
                p3_vector(v)

        def s_body(s):
            p0_scalar(s)
            p1_scalar(s)
            if STAGE != "p1":
                p2_scalar(s)
                p3_scalar(s)

        R.reset_pass(True)
        for f in (g_body, sy_body, t_body, v_body, s_body):
            f(None)
        R.reset_pass(False)
        R.log = []
        global LAST_R
        LAST_R = R
        with nc.Block() as block:
            for nm, fb in (("g", g_body), ("sy", sy_body), ("t", t_body),
                           ("v", v_body), ("s", s_body)):
                R.stream = nm
                qn = {"g": "sqg", "sy": "sqs"}.get(nm)
                getattr(block, {"g": "gpsimd", "sy": "sync", "t": "tensor",
                                "v": "vector", "s": "scalar"}[nm])(
                    lambda e, fb=fb, qn=qn: fb(EngW(e, R, qn)))
    return nc


LAST_R = None


# version 04
# speedup vs baseline: 1.1379x; 1.1379x over previous
import contextlib
import numpy as np
import ml_dtypes
import concourse.bass as bass
import concourse.mybir as mybir
from concourse.bass_utils import run_bass_kernel_spmd

F32 = mybir.dt.float32
BF16 = mybir.dt.bfloat16
U8 = mybir.dt.uint8
AF = mybir.ActivationFunctionType
ALU = mybir.AluOpType
AX = mybir.AxisListType
BF = ml_dtypes.bfloat16

T = 1024
CS = 128
NCH = 8
DM = 4096
NDT = 32
H, P, N, G = 128, 64, 128, 8
INTER = 8192
NIT = 64
CONV = 10240
PROJ = 18560
NCT = 145
EPS = 1e-5
CT_DT = 144
CT_XBC = list(range(64, 144))
CT_GATE = list(range(0, 64))
CT_ORDER = [CT_DT] + CT_XBC + CT_GATE
LA = 2
NSLOT = 4

STAGE = "full"
DEBUG_TAPS = ()


class Rec:
    QDS = {"sqs": 16, "sqg": 8}

    def __init__(self):
        self.tbl = {}
        self.dry = True
        self.sems = {}
        self.qcnt = {}
        self.log = None
        self.stream = None

    def sem(self, name):
        if name in ("sqs", "sqg"):
            self.qcnt[name] = 0
            for i in range(self.QDS[name]):
                self.sems.setdefault(f"{name}{i}", [None, 0])
        else:
            self.sems.setdefault(name, [None, 0])

    def reset_pass(self, dry):
        self.dry = dry
        for v in self.sems.values():
            v[1] = 0
        for k in self.qcnt:
            self.qcnt[k] = 0

    def bind(self, name, handle):
        self.sems[name][0] = handle

    def inc(self, instr, name, n=1, mark=None):
        if name in ("sqs", "sqg"):
            assert n in (0, 16)
            if n == 0:
                return self.fence(name, mark)
            actual = f"{name}{self.qcnt[name] % self.QDS[name]}"
            self.qcnt[name] += 1
        else:
            actual = name
        s = self.sems[actual]
        s[1] += n
        if mark is not None:
            key = mark
            val = (actual, s[1])
            if self.dry:
                assert key not in self.tbl, f"dup mark {key}"
                self.tbl[key] = val
            else:
                assert self.tbl.get(key) == val, \
                    f"mark drift {key}: {self.tbl.get(key)} vs {val}"
        if instr is not None:
            instr.then_inc(s[0], n)
        if self.log is not None and not self.dry and n > 0:
            self.log.append((self.stream, "i", actual, n, mark))
        return s[1]

    def fence(self, qname, mark):
        snap = tuple((f"{qname}{i}", self.sems[f"{qname}{i}"][1])
                     for i in range(self.QDS[qname]))
        if self.dry:
            assert mark not in self.tbl, f"dup mark {mark}"
            self.tbl[mark] = ("F", snap)
        else:
            assert self.tbl.get(mark) == ("F", snap), f"fence drift {mark}"
        return None

    def wait(self, eng, name, mark, extra=0):
        if eng is None:
            return
        if mark not in self.tbl:
            if self.dry:
                return
            raise KeyError(f"wait on unknown mark {mark}")
        ent = self.tbl[mark]
        if ent[0] == "F":
            for actual, v in ent[1]:
                if v > 0:
                    eng.wait_ge(self.sems[actual][0], v)
                    if self.log is not None and not self.dry:
                        self.log.append((self.stream, "w", actual, v, mark))
            return
        actual, v = ent
        v += extra
        if v > 0:
            eng.wait_ge(self.sems[actual][0], v)
            if self.log is not None and not self.dry:
                self.log.append((self.stream, "w", actual, v, mark))

    def wait_all(self, eng, qname):
        if eng is None:
            return
        for i in range(self.QDS[qname]):
            actual = f"{qname}{i}"
            v = self.sems[actual][1]
            if v > 0:
                eng.wait_ge(self.sems[actual][0], v)
                if self.log is not None and not self.dry:
                    self.log.append((self.stream, "w", actual, v, None))


class EngW:
    def __init__(self, e, R=None, q=None):
        self._e = e
        self._R = R
        self._q = q
        self.last = None

    def __getattr__(self, k):
        f = getattr(self._e, k)
        if not callable(f):
            return f
        if k == "dma_start" and self._q is not None:
            def g(*a, **kw):
                R, q = self._R, self._q
                actual = f"{q}{R.qcnt[q] % R.QDS[q]}"
                prev = R.sems[actual][1]
                if prev > 0:
                    self._e.wait_ge(R.sems[actual][0], prev)
                    if R.log is not None and not R.dry:
                        R.log.append((R.stream, "w", actual, prev, "thr"))
                r = f(*a, **kw)
                self.last = r
                return r
            return g
        def g(*a, **kw):
            r = f(*a, **kw)
            self.last = r
            return r
        return g


def _host_inputs(core, x_seq_half, W1, conv_w, conv_b, dt_bias, A_log, D, norm_w,
                 W_o, A_w, B_w, Wq, keys):
    ins = {}
    ins["x_fm"] = np.ascontiguousarray(
        x_seq_half.T.reshape(NDT, 128, T))
    ins["w1_l"] = np.ascontiguousarray(
        W1.reshape(NCT, 128, NDT, 128).transpose(0, 3, 2, 1).reshape(NCT, 128, NDT * 128))
    cw = conv_w[:, 0, :]
    cwd = np.zeros((80, 128, 4, 128), np.float32)
    idx = np.arange(128)
    for k in range(4):
        cwd[:, idx, k, idx] = cw.reshape(80, 128, 4)[:, :, k]
    ins["convw_l"] = np.ascontiguousarray(cwd.reshape(80, 128, 4 * 128))
    ins["convw_cols"] = np.ascontiguousarray(
        cw.reshape(80, 128, 4).transpose(1, 0, 2).reshape(128, 320))
    ins["convb_l"] = np.ascontiguousarray(conv_b.reshape(80, 128).T)
    ins["dtbias_c"] = np.ascontiguousarray(dt_bias.reshape(128, 1))
    ins["alog_c"] = np.ascontiguousarray(A_log.reshape(128, 1))
    ins["d_c"] = np.ascontiguousarray(D.reshape(128, 1))
    ins["nw_l"] = np.ascontiguousarray(norm_w.reshape(NIT, 128).T)
    ins["cmask_c"] = np.full((128, 1), float(core & 1), np.float32)
    ins["wo_l"] = np.ascontiguousarray(
        W_o.reshape(32, 128, 64, 128).transpose(0, 3, 2, 1).reshape(32, 128, 64 * 128))
    ins["awt_l"] = np.ascontiguousarray(A_w.T.reshape(32, 128, 512))
    ins["bwt_l"] = np.ascontiguousarray(
        B_w.reshape(32, 128, 4, 128).transpose(0, 3, 2, 1).reshape(32, 128, 512))
    ins["wqt_l"] = np.ascontiguousarray(
        Wq.T.reshape(32, 128, 16).transpose(1, 0, 2).reshape(128, 512))
    ins["keyst_l"] = np.ascontiguousarray(keys.T)
    ins["idn_f_l"] = np.eye(128, dtype=np.float32)
    ins["tribig_l"] = np.where(np.triu(np.ones((128, 128), bool)), 1e30, 0.0).astype(np.float32)
    ins["onesrow_l"] = np.ones((1, 128), np.float32)
    ins["onescol_l"] = np.ones((128, 1), np.float32)
    return ins


def kernel(L0_attn_input_states, L0_attn_in_proj_w, L0_attn_conv_w, L0_attn_conv_b,
           L0_attn_dt_bias, L0_attn_A_log, L0_attn_D, L0_attn_norm_w, L0_attn_out_w,
           L0_ffn_A_w, L0_ffn_B_w, L0_ffn_Wq_w, L0_ffn_keys):
    x = np.ascontiguousarray(np.asarray(L0_attn_input_states, np.float32))
    args = (np.asarray(L0_attn_in_proj_w, np.float32),
            np.asarray(L0_attn_conv_w, np.float32),
            np.asarray(L0_attn_conv_b, np.float32),
            np.asarray(L0_attn_dt_bias, np.float32),
            np.asarray(L0_attn_A_log, np.float32),
            np.asarray(L0_attn_D, np.float32),
            np.asarray(L0_attn_norm_w, np.float32),
            np.asarray(L0_attn_out_w, np.float32),
            np.asarray(L0_ffn_A_w, np.float32),
            np.asarray(L0_ffn_B_w, np.float32),
            np.asarray(L0_ffn_Wq_w, np.float32),
            np.asarray(L0_ffn_keys, np.float32))
    nc = build_program()
    in_maps = []
    for core in range(8):
        s, hf = core >> 1, core & 1
        in_maps.append(_host_inputs(core, x[s, hf * T:(hf + 1) * T, :], *args))
    res = run_bass_kernel_spmd(nc, in_maps, core_ids=list(range(8)))
    out = np.empty((4, 2048, 4096), np.float32)
    for core in range(8):
        s, hf = core >> 1, core & 1
        o = np.asarray(res.results[core]["out_fm"])
        out[s, hf * T:(hf + 1) * T, :] = o.transpose(2, 0, 1).reshape(T, DM)
    return out


KB = 1024


def build_program():
    nc = bass.Bass("TRN2", target_bir_lowering=False)

    x_fm = nc.dram_tensor("x_fm", [NDT, 128, T], F32, kind="ExternalInput").ap()
    w1_l = nc.dram_tensor("w1_l", [NCT, 128, NDT * 128], F32, kind="ExternalInput").ap()
    convw_l = nc.dram_tensor("convw_l", [80, 128, 512], F32, kind="ExternalInput").ap()
    convw_cols = nc.dram_tensor("convw_cols", [128, 320], F32, kind="ExternalInput").ap()
    convb_l = nc.dram_tensor("convb_l", [128, 80], F32, kind="ExternalInput").ap()
    dtbias_c = nc.dram_tensor("dtbias_c", [128, 1], F32, kind="ExternalInput").ap()
    alog_c = nc.dram_tensor("alog_c", [128, 1], F32, kind="ExternalInput").ap()
    d_c = nc.dram_tensor("d_c", [128, 1], F32, kind="ExternalInput").ap()
    nw_l = nc.dram_tensor("nw_l", [128, NIT], F32, kind="ExternalInput").ap()
    cmask_c = nc.dram_tensor("cmask_c", [128, 1], F32, kind="ExternalInput").ap()
    wo_l = nc.dram_tensor("wo_l", [32, 128, 8192], F32, kind="ExternalInput").ap()
    awt_l = nc.dram_tensor("awt_l", [32, 128, 512], F32, kind="ExternalInput").ap()
    bwt_l = nc.dram_tensor("bwt_l", [32, 128, 512], F32, kind="ExternalInput").ap()
    wqt_l = nc.dram_tensor("wqt_l", [128, 512], F32, kind="ExternalInput").ap()
    keyst_l = nc.dram_tensor("keyst_l", [16, 64], F32, kind="ExternalInput").ap()
    idn_f_l = nc.dram_tensor("idn_f_l", [128, 128], F32, kind="ExternalInput").ap()
    tribig_l = nc.dram_tensor("tribig_l", [128, 128], F32, kind="ExternalInput").ap()
    onesrow_l = nc.dram_tensor("onesrow_l", [1, 128], F32, kind="ExternalInput").ap()
    onescol_l = nc.dram_tensor("onescol_l", [128, 1], F32, kind="ExternalInput").ap()

    out_fm = nc.dram_tensor("out_fm", [NDT, 128, T], F32, kind="ExternalOutput").ap()

    xtok_hbm = nc.dram_tensor("xtok_hbm", [NCH, 128, INTER], BF16).ap()
    sg_hbm = nc.dram_tensor("sg_hbm", [NCH, 128, INTER], BF16).ap()
    yp_hbm = nc.dram_tensor("yp_hbm", [NCH, 128, INTER], BF16).ap()
    x2_hbm = nc.dram_tensor("x2_hbm", [NDT, 128, T], F32).ap()
    wo_bf = nc.dram_tensor("wo_bf", [32, 128, 8192], BF16).ap()
    tails_bi = nc.dram_tensor("tails_bi", [128, 240], BF16).ap()
    tails_bo = nc.dram_tensor("tails_bo", [256, 240], BF16).ap()
    carry_bi = nc.dram_tensor("carry_bi", [128, INTER], BF16).ap()
    carry_bo = nc.dram_tensor("carry_bo", [256, INTER], BF16).ap()

    taps = {}
    def mktap(name, shape, dtype=F32):
        if name in DEBUG_TAPS:
            taps[name] = nc.dram_tensor("tap_" + name, shape, dtype,
                                        kind="ExternalOutput").ap()
        return taps.get(name)

    R = Rec()
    for s_ in ("sqg", "sqs", "st", "sv", "ss", "sc"):
        R.sem(s_)

    with contextlib.ExitStack() as ST0:
        def sbt(name, shape, dtype):
            return ST0.enter_context(nc.sbuf_tensor(name, shape, dtype))
        def psb(name, shape, dtype=F32):
            return ST0.enter_context(nc.psum_tensor(name, shape, dtype))

        C_sb = sbt("C_sb", [128, 8 * 1024], BF16)
        expcum_b = sbt("expcum_b", [128, T], BF16)
        dplb = sbt("dplb", [128, T], F32)
        dtb = sbt("dtb", [128, 128], F32)
        idn_f = sbt("idn_f", [128, 128], F32)
        idn_b = sbt("idn_b", [128, 128], BF16)
        tribig = sbt("tribig", [128, 128], BF16)
        onesrow_f = sbt("onesrow_f", [1, 128], F32)
        onescol_b = sbt("onescol_b", [128, 1], BF16)
        dtbias = sbt("dtbias", [128, 1], F32)
        negA = sbt("negA", [128, 1], F32)
        dcol = sbt("dcol", [128, 1], F32)
        cmask = sbt("cmask", [128, 1], F32)
        nw = sbt("nw", [128, NIT], F32)
        smallsc = sbt("smallsc", [128, 8], F32)
        rowbuf = sbt("rowbuf", [1, 1024], F32)
        rowbuf2 = sbt("rowbuf2", [1, 1024], F32)
        mrow = sbt("mrow", [1, T], F32)
        rrow = sbt("rrow", [1, T], F32)
        keyst = sbt("keyst", [16, 64], BF16)
        wqt_all = sbt("wqt_all", [128, 512], BF16)
        convb = sbt("convb", [128, 80], F32)
        constc = sbt("constc", [128, 2], F32)
        convwc = sbt("convwc", [128, 320], F32)

        arena = sbt("arena", [128, 160 * KB], U8)
        def vw(off_kb, parts, nel, dtype):
            esz = 2 if dtype == BF16 else 4
            off = int(off_kb * KB)
            assert off % 64 == 0 and off + nel * esz <= 160 * KB, (off_kb, nel)
            return arena[0:parts, off: off + nel * esz].bitcast(dtype)

        xn = vw(0, 128, NDT * 1024, BF16)
        xtok_sb = [vw(0, 128, INTER, BF16), vw(16, 128, INTER, BF16)]
        prev = vw(32, 128, INTER, BF16)
        etile = [vw(48 + 0.25 * i, 128, 128, BF16) for i in range(4)]
        stile = [vw(49 + 0.25 * i, 128, 128, BF16) for i in range(4)]
        ctile = [vw(50 + 0.25 * i, 128, 128, BF16) for i in range(4)]
        acb = [vw(51 + 0.25 * i, 128, 128, BF16) for i in range(4)]
        bwt_s = [vw(52 + 0.25 * i, 128, 128, BF16) for i in range(4)]
        sgr = [vw(0, 128, INTER, BF16), vw(16, 128, INTER, BF16)]
        wob = [vw(0, 128, 8192, BF16), vw(16, 128, 8192, BF16)]
        xre = [vw(32 + 2 * i, 128, 512, F32) for i in range(2)]
        x2t = [vw(36 + 2 * i, 128, 512, F32) for i in range(2)]
        rstdT = vw(40, 128, 8, F32)
        g2t = [vw(40.5 + 0.25 * i, 128, 128, BF16) for i in range(2)]
        ttile = vw(41, 128, 128, F32)
        acb3 = [vw(42 + 0.25 * i, 128, 128, BF16) for i in range(4)]
        ctile3 = [vw(43 + 0.25 * i, 128, 128, BF16) for i in range(4)]
        bwts3 = [vw(44 + 0.25 * i, 128, 128, BF16) for i in range(4)]
        carry = vw(48, 128, INTER, BF16)
        xn2 = vw(0, 128, 32 * 512, BF16)
        bwtd = [vw(32 + i, 128, 512, BF16) for i in range(4)]
        x2l = [vw(85 + 2 * i, 128, 512, F32) for i in range(2)]

        xld = [vw(64 + 4 * i, 128, T, F32) for i in range(2)]
        xsq = [vw(72 + 2 * i, 128, T, BF16) for i in range(2)]
        mb = vw(76, 128, T, F32)
        rb = vw(80, 128, T, F32)
        wbuf = [vw(64 + 4 * i, 128, 2048, BF16) for i in range(4)]
        cwb = [vw(80 + i, 128, 512, BF16) for i in range(2)]
        xbcr = [vw(82, 128, 3 + T, BF16), vw(84.0625, 128, 3 + T, BF16)]
        hsfm = [vw(86.25, 128, T, BF16), vw(88.25, 128, T, BF16)]
        xstrip = [vw(90.25, 128, T, BF16), vw(92.25, 128, T, BF16)]
        first3 = vw(94.25, 128, 240, BF16)
        tails = vw(94.75, 128, 240, BF16)
        pad3 = vw(95.25, 128, 240, BF16)
        ypr = [vw(80, 128, INTER, BF16), vw(64, 128, INTER, BF16)]
        zgsb = vw(64, 128, 4 * 512, BF16)
        zgt_half = vw(68, 128, 4 * 512, BF16)
        awtb = [vw(72 + i, 128, 512, BF16) for i in range(4)]
        qsb = vw(76, 128, 16, BF16)
        qT = vw(76.0625, 16, 128, BF16)
        ssb = vw(76.375, 128, 64, F32)
        esb = vw(76.625, 128, 64, F32)
        m8a = vw(76.875, 128, 8, F32)
        m8b = vw(76.9375, 128, 8, F32)
        msk = vw(77.25, 128, 64, F32)
        gsb = vw(77.5, 128, 256, BF16)
        negt1 = vw(78, 128, 1, F32)
        rs1 = vw(78.0625, 128, 1, F32)
        outbuf = [vw(100 + 2 * i, 128, 512, F32) for i in range(2)]
        mb2 = vw(96, 128, 512, F32)
        rb2 = vw(98, 128, 512, F32)

        B_sb = vw(96, 128, INTER, BF16)
        sraw = vw(112, 128, 1024, BF16)
        btok = vw(114, 128, 1024, BF16)
        x6 = vw(116, 128, 6, BF16)
        patch3 = vw(116.0625, 128, 240, F32)
        patchh = vw(117.0625, 128, 192, BF16)
        ghalf = vw(96, 128, 64 * 512, BF16)

        dtr = vw(128, 128, T, F32)
        dt_t = vw(132, 128, T, F32)
        tmpA = vw(136, 128, T, F32)
        ncmb = vw(140, 128, T, F32)
        patcht = vw(128, 3, INTER, BF16)
        ypc = vw(128, 128, INTER, BF16)
        cum = vw(144, 128, T, F32)
        ncmbT = vw(148, 128, T, F32)
        wTT = vw(152, 128, T, F32)
        cdecb = vw(156, 128, T, F32)

        ps_mm = [psb(f"ps_mm{i}", [128, 512], F32) for i in range(4)]
        ps_cv = [psb(f"ps_cv{i}", [128, 512], F32) for i in range(2)]
        ps_tr = psb("ps_tr", [128, 1024], BF16)
        ps_x = psb("ps_x", [128, 512], F32)

        for s_ in list(R.sems):
            R.bind(s_, ST0.enter_context(nc.semaphore(s_)))

        def tile_ct(i):
            return CT_ORDER[i]
        def is_xbc(i):
            return 1 <= i <= 80
        def is_gate(i):
            return i >= 81
        def evac_sem(i):
            return "ss" if is_gate(i) else "sv"
        def xn_sl(kt, th):
            return xn[:, kt * 1024 + th * 512: kt * 1024 + (th + 1) * 512]

        def p0_sync(sy):
            for dm in range(NDT):
                if dm >= 2:
                    R.wait(sy, "sv", f"p0.xc.{dm-2}")
                    R.wait(sy, "ss", f"p0.xq.{dm-2}")
                R.inc(sy and sy.dma_start(out=xld[dm % 2], in_=x_fm[dm]),
                      "sqs", 16, mark=f"p0.x.{dm}")
            for apd, aps_ in [(idn_f[:, :], idn_f_l), (onesrow_f[0:1, :], onesrow_l),
                              (dtbias[:, :], dtbias_c), (negA[:, :], alog_c),
                              (dcol[:, :], d_c), (cmask[:, :], cmask_c),
                              (convb[:, :], convb_l), (convwc[:, :], convw_cols),
                              (nw[:, :], nw_l)]:
                R.inc(sy and sy.dma_start(out=apd, in_=aps_), "sqs", 16)
            R.inc(None, "sqs", 0, mark="consts_f")

        def p0_vector(v):
            if v:
                v.memset(constc[:, 0:1], EPS)
                v.memset(constc[:, 1:2], 1.0)
            R.inc(v and v.last, "sv", 1, mark="p0.constc")
            for dm in range(NDT):
                R.wait(v, "sqs", f"p0.x.{dm}")
                R.inc(v and v.tensor_copy(xn[:, dm * 1024:(dm + 1) * 1024], xld[dm % 2]),
                      "sv", 1, mark=f"p0.xc.{dm}")
            R.wait(v, "st", f"p0.st.{NDT-1}")
            if v:
                v.tensor_scalar(mrow[0:1, 0:512], ps_mm[0][0:1, :], 1.0 / DM, None, ALU.mult)
                v.tensor_scalar(mrow[0:1, 512:1024], ps_mm[1][0:1, :], 1.0 / DM, None, ALU.mult)
            R.inc(v and v.last, "sv", 1, mark="p0.mrow")
            if v:
                v.tensor_scalar(rowbuf[0:1, 0:512], ps_mm[2][0:1, :], 1.0 / DM, None, ALU.mult)
                v.tensor_scalar(rowbuf[0:1, 512:1024], ps_mm[3][0:1, :], 1.0 / DM, None, ALU.mult)
                v.tensor_tensor(rowbuf2[0:1, :], mrow[0:1, :], mrow[0:1, :], ALU.mult)
                v.tensor_tensor(rowbuf[0:1, :], rowbuf[0:1, :], rowbuf2[0:1, :], ALU.subtract)
            R.inc(v and v.last, "sv", 1, mark="p0.var")
            for r, dst in enumerate([mb[:, 0:512], mb[:, 512:1024],
                                     rb[:, 0:512], rb[:, 512:1024]]):
                R.wait(v, "st", f"p0.bc.{r}")
                R.inc(v and v.tensor_copy(dst, ps_x[:, :]), "sv", 1, mark=f"p0.bce.{r}")
            for dm in range(NDT):
                if v:
                    sl = xn[:, dm * 1024:(dm + 1) * 1024]
                    v.tensor_tensor(sl, sl, mb[:, :], ALU.subtract)
                    v.tensor_tensor(sl, sl, rb[:, :], ALU.mult)
                R.inc(v and v.last, "sv", 1, mark=f"p0.app.{dm}")

        def p0_scalar(s):
            for dm in range(NDT):
                R.wait(s, "sv", f"p0.xc.{dm}")
                if dm >= 2:
                    R.wait(s, "st", f"p0.st.{dm-2}")
                R.inc(s and s.activation(xsq[dm % 2], xn[:, dm * 1024:(dm + 1) * 1024],
                                         AF.Square), "ss", 1, mark=f"p0.xq.{dm}")
            R.wait(s, "sv", "p0.var")
            R.wait(s, "sv", "p0.constc")
            if s:
                s.activation(rowbuf2[0:1, :], rowbuf[0:1, :], AF.Ln, bias=constc[0:1, 0:1])
                s.activation(rrow[0:1, :], rowbuf2[0:1, :], AF.Exp, scale=-0.5)
            R.inc(s and s.last, "ss", 1, mark="p0.rstd")

        def p0_tensor(t):
            R.wait(t, "sqg", "consts_b")
            for dm in range(NDT):
                R.wait(t, "sv", f"p0.xc.{dm}")
                R.wait(t, "ss", f"p0.xq.{dm}")
                if t:
                    for half in range(2):
                        t.matmul(ps_mm[half][0:1, :], onescol_b[:, :],
                                 xn_sl(dm, half), start=(dm == 0), stop=(dm == NDT - 1))
                        t.matmul(ps_mm[2 + half][0:1, :], onescol_b[:, :],
                                 xsq[dm % 2][:, half * 512:(half + 1) * 512],
                                 start=(dm == 0), stop=(dm == NDT - 1))
                R.inc(t and t.last, "st", 1, mark=f"p0.st.{dm}")
            R.wait(t, "sqs", "consts_f")
            srcs = [mrow[0:1, 0:512], mrow[0:1, 512:1024],
                    rrow[0:1, 0:512], rrow[0:1, 512:1024]]
            for r, src in enumerate(srcs):
                R.wait(t, "sv", "p0.mrow" if r < 2 else "p0.var")
                if r >= 2:
                    R.wait(t, "ss", "p0.rstd")
                if r >= 1:
                    R.wait(t, "sv", f"p0.bce.{r-1}")
                R.inc(t and t.matmul(ps_x[:, :], onesrow_f[0:1, :], src,
                                     start=True, stop=True), "st", 1, mark=f"p0.bc.{r}")

        def p1_gpsimd(g):
            R.inc(g and g.dma_start(out=wo_bf, in_=wo_l), "sqg", 16, mark="wo_precast")
            for apd, aps_ in [(idn_b[:, :], idn_f_l), (tribig[:, :], tribig_l),
                              (onescol_b[:, :], onescol_l), (keyst[:, :], keyst_l),
                              (wqt_all[:, :], wqt_l)]:
                R.inc(g and g.dma_start(out=apd, in_=aps_), "sqg", 16)
            R.inc(None, "sqg", 0, mark="consts_b")
            R.wait(g, "sv", f"p0.app.{NDT-1}")
            for i in range(NCT):
                ct = tile_ct(i)
                if i >= 2:
                    R.wait(g, "st", f"p1.mm.{i-2}")
                if is_xbc(i) and i >= 3 and is_xbc(i - 2):
                    R.wait(g, "st", f"p1.conv.{i-2}")
                for half in range(2):
                    R.inc(g and g.dma_start(out=wbuf[2 * (i % 2) + half],
                                            in_=w1_l[ct][:, half * 2048:(half + 1) * 2048]),
                          "sqg", 16, mark=f"p1.wl.{i}.{half}")
                if is_xbc(i):
                    R.inc(g and g.dma_start(out=cwb[i % 2], in_=convw_l[ct - 64]),
                          "sqg", 16, mark=f"p1.cw.{i}")
            R.wait(g, "sv", "p1.tails")
            R.inc(g and g.dma_start(out=tails_bi, in_=tails), "sqg", 16, mark="p1.tout")
            R.wait(g, "sqg", "p1.tout")
            R.inc(g and g.collective_compute(
                "AllGather", ALU.bypass, replica_groups=[[0, 1], [2, 3], [4, 5], [6, 7]],
                ins=[tails_bi.opt()], outs=[tails_bo.opt()]), "sc", 1, mark="p1.cc")
            R.wait(g, "sc", "p1.cc")
            R.inc(g and g.dma_start(out=pad3, in_=tails_bo[0:128, :]), "sqg", 16,
                  mark="p1.pad3")

        def p1_sync(sy):
            for i in range(1, NCT):
                ct = tile_ct(i)
                if is_xbc(i) and ct - 64 < 64:
                    j = ct - 64
                    R.wait(sy, "ss", f"p1.str.{i}")
                    R.inc(sy and sy.dma_start(
                        out=xtok_hbm[0:8, :, j * 128:(j + 1) * 128]
                        .rearrange("c p t -> p c t"),
                        in_=xstrip[i % 2].rearrange("p (c t) -> p c t", t=128)),
                        "sqs", 16, mark=f"p1.xsp.{i}")
                elif is_gate(i):
                    R.wait(sy, "ss", f"p1.ev.{i}")
                    R.inc(sy and sy.dma_start(
                        out=sg_hbm[:, :, ct * 128:(ct + 1) * 128]
                        .rearrange("c p t -> p c t"),
                        in_=hsfm[i % 2].rearrange("p (c t) -> p c t", t=128)),
                        "sqs", 16, mark=f"p1.sgsp.{i}")
            R.wait(sy, "ss", "p1.str0.63")
            R.inc(sy and sy.dma_start(out=xtok_hbm[0][0:3, :], in_=patcht[0:3, :]),
                  "sqs", 16, mark="p1.xsp0")
            R.fence("sqs", "p1.sync_done")

        def p1_tensor(t):
            R.wait(t, "sqg", "consts_b")
            for i in range(81):
                p1_tensor_tile(t, i)
            p1_tensor_prep(t)
            for i in range(81, NCT):
                p1_tensor_tile(t, i)
            for k in range(64):
                R.wait(t, "ss", "p1.patchs")
                if k >= 1:
                    R.wait(t, "ss", f"p1.str0.{k-1}")
                R.inc(t and t.transpose(ps_tr[0:3, (k % 8) * 128:(k % 8 + 1) * 128],
                                        patchh[:, k * 3:(k + 1) * 3], idn_b[:, :]),
                      "st", 1, mark=f"p1.tr0.{k}")

        def p1_tensor_tile(t, i):
            ct = tile_ct(i)
            if i >= 2:
                R.wait(t, evac_sem(i - 2), f"p1.ev.{i-2}")
            if i == 0:
                R.wait(t, "sv", "p0.var")
                R.wait(t, "sv", f"p0.app.{NDT-1}")
            if i == 1:
                R.wait(t, "sv", "p0.var")
            for half in range(2):
                R.wait(t, "sqg", f"p1.wl.{i}.{half}")
                if t:
                    for kt16 in range(16):
                        kt = half * 16 + kt16
                        for th in range(2):
                            t.matmul(ps_mm[2 * (i % 2) + th][:, :],
                                     wbuf[2 * (i % 2) + half][:, kt16 * 128:(kt16 + 1) * 128],
                                     xn_sl(kt, th), start=(kt == 0), stop=(kt == 31))
            R.inc(t and t.last, "st", 1, mark=f"p1.mm.{i}")
            if is_xbc(i):
                j = ct - 64
                R.wait(t, "sv", f"p1.ev.{i}")
                R.wait(t, "sqg", f"p1.cw.{i}")
                if i >= 2:
                    R.wait(t, "ss", f"p1.cve.{i-1}")
                if t:
                    for th in range(2):
                        for k in range(4):
                            t.matmul(ps_cv[th][:, :], cwb[i % 2][:, k * 128:(k + 1) * 128],
                                     xbcr[i % 2][:, k + th * 512: k + th * 512 + 512],
                                     start=(k == 0), stop=(k == 3))
                R.inc(t and t.last, "st", 1, mark=f"p1.conv.{i}")
                if j < 64:
                    R.wait(t, "ss", f"p1.cve.{i}")
                    if i >= 2 and 1 <= i - 1 <= 80 and tile_ct(i - 1) - 64 < 64:
                        R.wait(t, "ss", f"p1.str.{i-1}")
                    if t:
                        for c in range(8):
                            t.transpose(ps_tr[:, c * 128:(c + 1) * 128],
                                        hsfm[i % 2][:, c * 128:(c + 1) * 128], idn_b[:, :])
                    R.inc(t and t.last, "st", 1, mark=f"p1.tr.{i}")

        def p1_vector(v):
            R.wait(v, "sqs", "consts_f")
            if v:
                v.memset(xbcr[0][:, 0:3], 0.0)
                v.memset(xbcr[1][:, 0:3], 0.0)
            R.inc(v and v.last, "sv", 1, mark="p1.padz")
            for i in range(81):
                p1_vector_tile(v, i)
            p1_vector_prep(v)
            R.wait(v, "sqg", "p1.pad3")
            R.inc(v and v.tensor_scalar(pad3, pad3, cmask[:, 0:1], None, ALU.mult), "sv", 1)
            for j in range(80):
                if v:
                    v.tensor_copy(x6[:, 0:3], pad3[:, j * 3:(j + 1) * 3])
                    v.tensor_copy(x6[:, 3:6], first3[:, j * 3:(j + 1) * 3])
                    v.tensor_scalar(patch3[:, j * 3:(j + 1) * 3], x6[:, 0:3],
                                    convwc[:, 4 * j:4 * j + 1], None, ALU.mult)
                    for k in range(1, 4):
                        v.scalar_tensor_tensor(patch3[:, j * 3:(j + 1) * 3], x6[:, k:k + 3],
                                               convwc[:, 4 * j + k:4 * j + k + 1],
                                               patch3[:, j * 3:(j + 1) * 3], ALU.mult, ALU.add)
            R.inc(v and v.last, "sv", 1, mark="p1.patchv")

        def p1_vector_tile(v, i):
            ct = tile_ct(i)
            R.wait(v, "st", f"p1.mm.{i}")
            if i == 0:
                if v:
                    v.tensor_copy(dtr[:, 0:512], ps_mm[0][:, :])
                    v.tensor_copy(dtr[:, 512:1024], ps_mm[1][:, :])
                R.inc(v and v.last, "sv", 1, mark="p1.ev.0")
            elif is_xbc(i):
                j = ct - 64
                R.wait(v, "sv", "p1.padz")
                if i >= 3 and is_xbc(i - 2):
                    R.wait(v, "st", f"p1.conv.{i-2}")
                if v:
                    v.tensor_copy(xbcr[i % 2][:, 3:515], ps_mm[2 * (i % 2)][:, :])
                    v.tensor_copy(xbcr[i % 2][:, 515:1027], ps_mm[2 * (i % 2) + 1][:, :])
                    v.tensor_copy(first3[:, j * 3:(j + 1) * 3], xbcr[i % 2][:, 3:6])
                    v.tensor_copy(tails[:, j * 3:(j + 1) * 3], xbcr[i % 2][:, 1024:1027])
                R.inc(v and v.last, "sv", 1, mark=f"p1.ev.{i}")
                if i == 80:
                    R.inc(None, "sv", 0, mark="p1.tails")

        def p1_scalar(s):
            R.wait(s, "sqs", "consts_f")
            R.inc(s and s.activation(smallsc[:, 0:1], negA[:, 0:1], AF.Exp), "ss", 1,
                  mark="p1.alog")
            for i in range(1, 81):
                p1_scalar_tile(s, i)
            p1_scalar_prep(s)
            for i in range(81, NCT):
                p1_scalar_tile(s, i)
            R.wait(s, "sv", "p1.patchv")
            for j in range(80):
                dst = (patchh[:, j * 3:(j + 1) * 3] if j < 64 else
                       (B_sb[:, (j - 64) * 1024:(j - 64) * 1024 + 3] if j < 72 else
                        C_sb[:, (j - 72) * 1024:(j - 72) * 1024 + 3]))
                if s:
                    s.activation(dst, patch3[:, j * 3:(j + 1) * 3], AF.Silu,
                                 bias=convb[:, j:j + 1])
            R.inc(s and s.last, "ss", 1, mark="p1.patchs")
            for k in range(64):
                R.wait(s, "st", f"p1.tr0.{k}")
                R.inc(s and s.copy(patcht[0:3, k * 128:(k + 1) * 128],
                                   ps_tr[0:3, (k % 8) * 128:(k % 8 + 1) * 128]),
                      "ss", 1, mark=f"p1.str0.{k}")

        def p1_scalar_tile(s, i):
            ct = tile_ct(i)
            if is_gate(i):
                R.wait(s, "st", f"p1.mm.{i}")
                if i in (81, 82):
                    R.wait(s, "st", f"p1.tr.{i-18}")
                if i >= 83:
                    R.wait(s, "sqs", f"p1.sgsp.{i-2}")
                if s:
                    s.activation(hsfm[i % 2][:, 0:512], ps_mm[2 * (i % 2)][:, :], AF.Silu)
                    s.activation(hsfm[i % 2][:, 512:1024], ps_mm[2 * (i % 2) + 1][:, :],
                                 AF.Silu)
                R.inc(s and s.last, "ss", 1, mark=f"p1.ev.{i}")
                return
            j = ct - 64
            R.wait(s, "st", f"p1.conv.{i}")
            if j < 64:
                if i >= 3 and tile_ct(i - 2) - 64 < 64:
                    R.wait(s, "st", f"p1.tr.{i-2}")
                dst0, dst1 = hsfm[i % 2][:, 0:512], hsfm[i % 2][:, 512:1024]
            elif j < 72:
                dst0 = B_sb[:, (j - 64) * 1024:(j - 64) * 1024 + 512]
                dst1 = B_sb[:, (j - 64) * 1024 + 512:(j - 63) * 1024]
            else:
                dst0 = C_sb[:, (j - 72) * 1024:(j - 72) * 1024 + 512]
                dst1 = C_sb[:, (j - 72) * 1024 + 512:(j - 71) * 1024]
            if s:
                s.activation(dst0, ps_cv[0][:, :], AF.Silu, bias=convb[:, j:j + 1])
                s.activation(dst1, ps_cv[1][:, :], AF.Silu, bias=convb[:, j:j + 1])
            R.inc(s and s.last, "ss", 1, mark=f"p1.cve.{i}")
            if j < 64:
                R.wait(s, "st", f"p1.tr.{i}")
                if i >= 3 and tile_ct(i - 2) - 64 < 64:
                    R.wait(s, "sqs", f"p1.xsp.{i-2}")
                if s:
                    for c in range(8):
                        s.copy(xstrip[i % 2][:, c * 128:(c + 1) * 128],
                               ps_tr[:, c * 128:(c + 1) * 128])
                R.inc(s and s.last, "ss", 1, mark=f"p1.str.{i}")

        def p1_scalar_prep(s):
            R.wait(s, "sv", "p1.ev.0")
            if s:
                s.activation(dt_t, dtr, AF.Abs, bias=dtbias[:, 0:1])
                s.activation(tmpA, dt_t, AF.Exp, scale=-1.0)
                s.activation(dt_t, tmpA, AF.Ln, bias=constc[:, 1:2])
                s.activation(tmpA, dtr, AF.Relu, bias=dtbias[:, 0:1])
            R.inc(s and s.last, "ss", 1, mark="p1.sp1")
            R.wait(s, "sv", "p1.dt")
            R.inc(s and s.activation(tmpA, dt_t, AF.Ln), "ss", 1, mark="p1.lndt")
            R.wait(s, "sv", "p1.ncmb")
            R.inc(s and s.activation(dt_t, cum, AF.Exp), "ss", 1, mark="p1.expc")
            for c in range(NCH):
                R.inc(s and s.activation(tmpA[:, c * 128:(c + 1) * 128],
                                         ncmb[:, c * 128:(c + 1) * 128], AF.Exp,
                                         bias=cum[:, c * 128 + 127:(c + 1) * 128]),
                      "ss", 1, mark=f"p1.w.{c}")
            for c in range(NCH):
                R.wait(s, "st", f"p1.trn.{c}")
                R.inc(s and s.copy(ncmbT[:, c * 128:(c + 1) * 128],
                                   ps_x[:, (c % 2) * 128:(c % 2 + 1) * 128]),
                      "ss", 1, mark=f"p1.ncmbTe.{c}")
            for c in range(NCH):
                R.wait(s, "st", f"p1.trw.{c}")
                R.inc(s and s.copy(wTT[:, c * 128:(c + 1) * 128],
                                   ps_x[:, (c % 2) * 128:(c % 2 + 1) * 128]),
                      "ss", 1, mark=f"p1.wTe.{c}")
            for c in range(NCH):
                R.wait(s, "st", f"p1.trc.{c}")
                R.inc(s and s.copy(rowbuf[0:1, c * 128:(c + 1) * 128], ps_x[0:1, 256:384]),
                      "ss", 1, mark=f"p1.cdr.{c}")
            R.wait(s, "st", "p1.trd")
            R.inc(s and s.copy(mrow[0:1, 0:128], ps_x[0:1, 256:384]), "ss", 1, mark="p1.drow")

        def p1_vector_prep(v):
            R.wait(v, "ss", "p1.sp1")
            R.inc(v and v.tensor_tensor(dt_t, dt_t, tmpA, ALU.add), "sv", 1, mark="p1.dt")
            R.wait(v, "ss", "p1.alog")
            R.inc(v and v.tensor_scalar(negA[:, 0:1], smallsc[:, 0:1], -1.0, None, ALU.mult),
                  "sv", 1, mark="p1.negA")
            R.wait(v, "ss", "p1.lndt")
            if v:
                v.tensor_scalar(dtr, dt_t, negA[:, 0:1], None, ALU.mult)
                for c in range(NCH):
                    v.tensor_tensor_scan(cum[:, c * 128:(c + 1) * 128],
                                         dtr[:, c * 128:(c + 1) * 128],
                                         dtr[:, c * 128:(c + 1) * 128],
                                         0.0, ALU.add, ALU.bypass)
            R.inc(v and v.last, "sv", 1, mark="p1.cum")
            R.inc(v and v.tensor_tensor(ncmb, tmpA, cum, ALU.subtract), "sv", 1,
                  mark="p1.ncmb")
            R.wait(v, "ss", "p1.expc")
            R.inc(v and v.tensor_copy(expcum_b[:, :], dt_t), "sv", 1, mark="p1.expcb")
            if v:
                v.memset(rowbuf2[0:1, 0:128], 1.0)
            for c in range(1, NCH):
                R.wait(v, "ss", f"p1.cdr.{c-1}")
                if v:
                    v.tensor_tensor(rowbuf2[0:1, c * 128:(c + 1) * 128],
                                    rowbuf2[0:1, (c - 1) * 128: c * 128],
                                    rowbuf[0:1, (c - 1) * 128: c * 128], ALU.mult)
            R.inc(v and v.last, "sv", 1, mark="p1.dplrows")
            for c in range(NCH):
                R.wait(v, "st", f"p1.bcc.{c}")
                R.inc(v and v.tensor_copy(cdecb[:, c * 128:(c + 1) * 128],
                                          ps_x[:, 128 * (2 + c % 2): 128 * (3 + c % 2)]),
                      "sv", 1, mark=f"p1.cdecb.{c}")
            for c in range(NCH):
                R.wait(v, "st", f"p1.bcd.{c}")
                R.inc(v and v.tensor_scalar(dplb[:, c * 128:(c + 1) * 128],
                                            ps_x[:, 128 * (2 + c % 2): 128 * (3 + c % 2)],
                                            cmask[:, 0:1], None, ALU.mult),
                      "sv", 1, mark=f"p1.dplb.{c}")
            R.wait(v, "st", "p1.bcD")
            R.inc(v and v.tensor_copy(dtb[:, :], ps_x[:, 128:256]), "sv", 1, mark="p1.dtb")

        def p1_tensor_prep(t):
            for c in range(NCH):
                R.wait(t, "sv", "p1.ncmb")
                if c >= 1:
                    R.wait(t, "ss", f"p1.ncmbTe.{c-1}")
                R.inc(t and t.transpose(ps_x[:, (c % 2) * 128:(c % 2 + 1) * 128],
                                        ncmb[:, c * 128:(c + 1) * 128], idn_f[:, :]),
                      "st", 1, mark=f"p1.trn.{c}")
            for c in range(NCH):
                R.wait(t, "ss", f"p1.w.{c}")
                if c >= 1:
                    R.wait(t, "ss", f"p1.wTe.{c-1}")
                else:
                    R.wait(t, "ss", f"p1.ncmbTe.{NCH-1}")
                R.inc(t and t.transpose(ps_x[:, (c % 2) * 128:(c % 2 + 1) * 128],
                                        tmpA[:, c * 128:(c + 1) * 128], idn_f[:, :]),
                      "st", 1, mark=f"p1.trw.{c}")
            for c in range(NCH):
                R.wait(t, "ss", "p1.expc")
                if c >= 1:
                    R.wait(t, "ss", f"p1.cdr.{c-1}")
                R.inc(t and t.transpose(ps_x[0:1, 256:384],
                                        dt_t[:, c * 128 + 127:(c + 1) * 128], idn_f[:, :]),
                      "st", 1, mark=f"p1.trc.{c}")
            R.wait(t, "ss", f"p1.cdr.{NCH-1}")
            R.inc(t and t.transpose(ps_x[0:1, 256:384], dcol[:, 0:1], idn_f[:, :]),
                  "st", 1, mark="p1.trd")
            for c in range(NCH):
                R.wait(t, "ss", f"p1.cdr.{c}")
                if c == 0:
                    R.wait(t, "ss", "p1.drow")
                if c >= 1:
                    R.wait(t, "sv", f"p1.cdecb.{c-1}")
                R.inc(t and t.matmul(ps_x[:, 128 * (2 + c % 2): 128 * (3 + c % 2)],
                                     onesrow_f[0:1, :], rowbuf[0:1, c * 128:(c + 1) * 128],
                                     start=True, stop=True), "st", 1, mark=f"p1.bcc.{c}")
            for c in range(NCH):
                R.wait(t, "sv", "p1.dplrows")
                if c >= 1:
                    R.wait(t, "sv", f"p1.dplb.{c-1}")
                else:
                    R.wait(t, "sv", f"p1.cdecb.{NCH-1}")
                R.inc(t and t.matmul(ps_x[:, 128 * (2 + c % 2): 128 * (3 + c % 2)],
                                     onesrow_f[0:1, :], rowbuf2[0:1, c * 128:(c + 1) * 128],
                                     start=True, stop=True), "st", 1, mark=f"p1.bcd.{c}")
            R.wait(t, "ss", "p1.drow")
            R.wait(t, "sv", f"p1.dplb.{NCH-1}")
            R.inc(t and t.matmul(ps_x[:, 128:256], onesrow_f[0:1, :], mrow[0:1, 0:128],
                                 start=True, stop=True), "st", 1, mark="p1.bcD")

        def p2_tensor(t):
            R.wait(t, "sv", "p1.cum")
            R.wait(t, "sv", "p1.expcb")
            R.wait(t, "ss", f"p1.ev.{NCT-1}")
            R.wait(t, "ss", f"p1.ev.{NCT-2}")
            R.wait(t, "ss", "p1.patchs")
            R.wait(t, "ss", "p1.str0.63")
            for c in range(NCH):
                for w in range(2):
                    if c > 0 or w > 0:
                        pc, pw = (c, w - 1) if w > 0 else (c - 1, 1)
                        R.wait(t, "ss", f"p2.sre.{pc}.{pw}")
                    if c > 0 and w == 0:
                        R.wait(t, "sv", f"p2.scan.{c-1}.127")
                    if t:
                        for g_ in range(4 * w, 4 * w + 4):
                            t.matmul(
                                ps_cv[1][:, (g_ % 4) * 128:(g_ % 4 + 1) * 128],
                                B_sb[:, g_ * 1024 + c * 128: g_ * 1024 + (c + 1) * 128],
                                C_sb[:, g_ * 1024 + c * 128: g_ * 1024 + (c + 1) * 128],
                                start=True, stop=True)
                    R.inc(t and t.last, "st", 1, mark=f"p2.srw.{c}.{w}")
                    if c > 0 or w > 0:
                        pc, pw = (c, w - 1) if w > 0 else (c - 1, 1)
                        R.wait(t, "ss", f"p2.bte.{pc}.{pw}")
                    if t:
                        for g_ in range(4 * w, 4 * w + 4):
                            t.transpose(
                                ps_tr[:, g_ * 128:(g_ + 1) * 128],
                                B_sb[:, g_ * 1024 + c * 128: g_ * 1024 + (c + 1) * 128],
                                idn_b[:, :])
                    R.inc(t and t.last, "st", 1, mark=f"p2.btw.{c}.{w}")
                for j2 in range(128 + LA):
                    if j2 < 128:
                        h = j2
                        if h >= 2:
                            R.wait(t, "ss", f"p2.se.{c}.{h-2}")
                        elif c > 0:
                            R.wait(t, "ss", f"p2.se.{c-1}.127")
                        if t:
                            t.matmul(ps_mm[h % 2][:, 0:128],
                                     idn_f[:, h:h + 1].broadcast_to((128, 128)),
                                     cum[:, c * 128:(c + 1) * 128], start=True, stop=True)
                            t.matmul(ps_mm[h % 2][:, 128:256],
                                     idn_b[:, h:h + 1].broadcast_to((128, 128)),
                                     expcum_b[:, c * 128:(c + 1) * 128],
                                     start=True, stop=True)
                        R.inc(t and t.last, "st", 1, mark=f"p2.bc.{c}.{h}")
                    if j2 >= LA:
                        h = j2 - LA
                        if h == 0:
                            R.wait(t, "sqs", f"p2.xtok.{c}")
                            if c == 0:
                                R.wait(t, "sv", "p2.prev0")
                        R.wait(t, "sv", f"p2.dv.{c}.{h}")
                        p_ = h // 2
                        bank = ps_mm[2 + p_ % 2]
                        reg = bank[(h % 2) * 64:(h % 2) * 64 + 64, 0:128]
                        if h % 2 == 0:
                            if p_ >= 2:
                                R.wait(t, "ss", f"p2.ev.{c}.{p_-2}")
                            elif c > 0:
                                R.wait(t, "ss", f"p2.ev.{c-1}.{62 + p_}")
                        if c > 0:
                            R.wait(t, "sv", f"p2.scan.{c-1}.{h}")
                        if h >= 2:
                            R.wait(t, "sv", f"p2.scan.{c}.{h-2}")
                        elif c > 0:
                            R.wait(t, "sv", f"p2.scan.{c-1}.{126 + h}")
                        if h == 1:
                            R.wait(t, "ss", f"p2.sre.{c}.1")
                        if t:
                            t.matmul(reg, xtok_sb[c % 2][:, h * 64:(h + 1) * 64],
                                     stile[h % 4][:, :], start=True, stop=False)
                            t.matmul(reg, prev[:, h * 64:(h + 1) * 64],
                                     ctile[h % 4][:, :], start=False, stop=True)
                            t.matmul(ps_cv[h % 2][:, 256:320],
                                     bwt_s[h % 4][:, :],
                                     xtok_sb[c % 2][:, h * 64:(h + 1) * 64],
                                     start=True, stop=True)
                        R.inc(t and t.last, "st", 1, mark=f"p2.yg.{c}.{h}")

        def p2_scalar(s):
            for c in range(NCH):
                for w in range(2):
                    R.wait(s, "st", f"p2.srw.{c}.{w}")
                    if s:
                        for g_ in range(4 * w, 4 * w + 4):
                            s.copy(sraw[:, g_ * 128:(g_ + 1) * 128],
                                   ps_cv[1][:, (g_ % 4) * 128:(g_ % 4 + 1) * 128])
                    R.inc(s and s.last, "ss", 1, mark=f"p2.sre.{c}.{w}")
                    R.wait(s, "st", f"p2.btw.{c}.{w}")
                    if s:
                        for g_ in range(4 * w, 4 * w + 4):
                            s.copy(btok[:, g_ * 128:(g_ + 1) * 128],
                                   ps_tr[:, g_ * 128:(g_ + 1) * 128])
                    R.inc(s and s.last, "ss", 1, mark=f"p2.bte.{c}.{w}")
                for j2 in range(128 + LA):
                    if j2 < 128:
                        h = j2
                        R.wait(s, "st", f"p2.bc.{c}.{h}")
                        if h >= 4:
                            R.wait(s, "sv", f"p2.dv.{c}.{h-4}")
                        elif c > 0:
                            R.wait(s, "sv", f"p2.dv.{c-1}.127")
                        if s:
                            s.activation(etile[h % 4][:, :],
                                         ps_mm[h % 2][:, 0:128],
                                         AF.Exp,
                                         bias=ncmbT[:, c * 128 + h: c * 128 + h + 1])
                            s.copy(acb[h % 4][:, :], ps_mm[h % 2][:, 128:256])
                        R.inc(s and s.last, "ss", 1, mark=f"p2.se.{c}.{h}")
                    if j2 >= LA and (j2 - LA) % 2 == 1:
                        p_ = (j2 - LA) // 2
                        R.wait(s, "st", f"p2.yg.{c}.{2 * p_ + 1}")
                        if p_ == 0:
                            R.wait(s, "sqs", "p1.xsp0" if c == 0 else f"p2.ypsp.{c-1}")
                        bank = ps_mm[2 + p_ % 2]
                        R.inc(s and s.copy(ypc[:, p_ * 128:(p_ + 1) * 128],
                                           bank[:, 0:128]),
                              "ss", 1, mark=f"p2.ev.{c}.{p_}")

        def p2_vector(v):
            R.wait(v, "sqg", "consts_b")
            R.wait(v, "ss", f"p1.wTe.{NCH-1}")
            R.wait(v, "st", f"p1.mm.{NCT-1}")
            R.inc(v and v.memset(prev[:, :], 0.0), "sv", 1, mark="p2.prev0")
            for c in range(NCH):
                for j2 in range(128 + LA):
                    if j2 < 128:
                        h = j2
                        gr = h // 16
                        R.wait(v, "ss", f"p2.se.{c}.{h}")
                        R.wait(v, "ss", f"p2.sre.{c}.{h // 64}")
                        R.wait(v, "ss", f"p2.bte.{c}.{h // 64}")
                        if v:
                            v.tensor_tensor(stile[h % 4][:, :], etile[h % 4][:, :],
                                            tribig[:, :], ALU.min)
                            v.tensor_tensor(stile[h % 4][:, :], stile[h % 4][:, :],
                                            sraw[:, gr * 128:(gr + 1) * 128], ALU.mult)
                            v.scalar_tensor_tensor(stile[h % 4][:, :], idn_b[:, :],
                                                   dtb[:, h:h + 1], stile[h % 4][:, :],
                                                   ALU.mult, ALU.add)
                            v.tensor_tensor(ctile[h % 4][:, :],
                                            C_sb[:, gr * 1024 + c * 128:
                                                 gr * 1024 + (c + 1) * 128],
                                            acb[h % 4][:, :], ALU.mult)
                            v.tensor_scalar(bwt_s[h % 4][:, :],
                                            btok[:, gr * 128:(gr + 1) * 128],
                                            wTT[:, c * 128 + h: c * 128 + h + 1],
                                            None, ALU.mult)
                        R.inc(v and v.last, "sv", 1, mark=f"p2.dv.{c}.{h}")
                    if j2 >= LA:
                        h = j2 - LA
                        R.wait(v, "st", f"p2.yg.{c}.{h}")
                        R.inc(v and v.scalar_tensor_tensor(
                            prev[:, h * 64:(h + 1) * 64], prev[:, h * 64:(h + 1) * 64],
                            cdecb[:, c * 128 + h: c * 128 + h + 1],
                            ps_cv[h % 2][:, 256:320],
                            ALU.mult, ALU.add), "sv", 1, mark=f"p2.scan.{c}.{h}")
            R.inc(None, "sv", 0, mark="p2.scan_done")

        def p2_sync(sy):
            for c in range(NCH):
                R.wait(sy, "sqs", "p1.sync_done")
                R.wait(sy, "st", f"p1.mm.{NCT-1}")
                if c >= 2:
                    R.wait(sy, "st", f"p2.yg.{c-2}.127")
                R.inc(sy and sy.dma_start(out=xtok_sb[c % 2][:, :], in_=xtok_hbm[c]),
                      "sqs", 16, mark=f"p2.xtok.{c}")
                R.wait(sy, "ss", f"p2.ev.{c}.63")
                R.inc(sy and sy.dma_start(out=yp_hbm[c], in_=ypc[:, :]), "sqs", 16,
                      mark=f"p2.ypsp.{c}")
            R.fence("sqs", "p2.sync_done")

        def p2_gpsimd(g):
            R.wait(g, "sv", "p2.scan_done")
            R.inc(g and g.dma_start(out=carry_bi, in_=prev[:, :]), "sqg", 16, mark="p2.cout")
            R.wait(g, "sqg", "p2.cout")
            R.inc(g and g.collective_compute(
                "AllGather", ALU.bypass, replica_groups=[[0, 1], [2, 3], [4, 5], [6, 7]],
                ins=[carry_bi.opt()], outs=[carry_bo.opt()]), "sc", 1, mark="p2.cc")
            R.wait(g, "sc", "p2.cc")
            R.inc(g and g.dma_start(out=carry[:, :], in_=carry_bo[0:128, :]), "sqg", 16,
                  mark="p2.cin")

        def p3_tensor(t):
            for hf in range(2):
                for c in range(4 * hf, 4 * hf + 4):
                    for j2 in range(128 + LA):
                        if j2 < 128:
                            h = j2
                            if h >= 2:
                                R.wait(t, "ss", f"p3.se.{c}.{h-2}")
                            elif c > 0:
                                R.wait(t, "ss", f"p3.se.{c-1}.127")
                            else:
                                R.wait(t, "ss", "p2.se.7.127")
                            R.inc(t and t.matmul(
                                ps_mm[h % 2][:, 0:128],
                                idn_b[:, h:h + 1].broadcast_to((128, 128)),
                                expcum_b[:, c * 128:(c + 1) * 128],
                                start=True, stop=True), "st", 1, mark=f"p3.bc.{c}.{h}")
                        if j2 >= LA:
                            h = j2 - LA
                            R.wait(t, "sv", f"p3.dv.{c}.{h}")
                            p_ = h // 2
                            bank = ps_mm[2 + p_ % 2]
                            reg = bank[(h % 2) * 64:(h % 2) * 64 + 64, 0:128]
                            if h % 2 == 0:
                                if p_ >= 2:
                                    R.wait(t, "sv", f"p3.pr.{c}.{p_-2}")
                                elif c > 0:
                                    R.wait(t, "sv", f"p3.pr.{c-1}.{62 + p_}")
                                else:
                                    R.wait(t, "ss", "p2.ev.7.63")
                            R.inc(t and t.matmul(reg, bwts3[h % 4][:, 0:64],
                                                 ctile3[h % 4][:, :],
                                                 start=True, stop=True),
                                  "st", 1, mark=f"p3.y2.{c}.{h}")
                    for p_ in range(64):
                        R.wait(t, "ss", f"p3.sq.{c}.{p_}")
                        if p_ == 0:
                            if c == 0:
                                R.wait(t, "ss", "p2.sre.7.1")
                                R.wait(t, "sv", "p2.scan.7.127")
                            else:
                                R.wait(t, "ss", f"p3.rstd.{c-1}")
                        R.inc(t and t.matmul(
                            ps_cv[1][:, (c % 2) * 8 + p_ // 8:(c % 2) * 8 + p_ // 8 + 1],
                            g2t[p_ % 2][:, :], onescol_b[:, :],
                            start=(p_ % 8 == 0), stop=(p_ % 8 == 7)),
                            "st", 1, mark=f"p3.vm.{c}.{p_}")
                    R.inc(None, "st", 0, mark=f"p3.var.{c}")
                    for g_ in range(8):
                        R.wait(t, "ss", f"p3.rstd.{c}")
                        if g_ >= 1:
                            R.wait(t, "ss", f"p3.re.{c}.{g_-1}")
                        R.inc(t and t.transpose(ps_x[0:1, 384:512],
                                                rstdT[:, g_:g_ + 1], idn_f[:, :]),
                              "st", 1, mark=f"p3.rt.{c}.{g_}")
                    for g_ in range(8):
                        R.wait(t, "ss", f"p3.re.{c}.7")
                        if g_ >= 1:
                            R.wait(t, "sv", f"p3.gn.{c}.{8 * (g_ - 1) + 7}")
                        R.inc(t and t.matmul(ps_x[:, (g_ % 3) * 128:(g_ % 3 + 1) * 128],
                                             onesrow_f[0:1, :],
                                             rowbuf[0:1, g_ * 128:(g_ + 1) * 128],
                                             start=True, stop=True),
                              "st", 1, mark=f"p3.rb.{c}.{g_}")
                for dm in range(NDT):
                    R.wait(t, "sqs", f"p3.wo.{hf}.{dm}")
                    R.wait(t, "sv", f"p3.gn.{4*hf+3}.63")
                    if dm >= 2:
                        R.wait(t, "sv", f"p3.x2.{hf}.{dm-2}")
                    if t:
                        for kt in range(64):
                            t.matmul(ps_mm[2 + dm % 2][:, :],
                                     wob[dm % 2][:, kt * 128:(kt + 1) * 128],
                                     ghalf[:, kt * 512:(kt + 1) * 512],
                                     start=(kt == 0), stop=(kt == 63))
                    R.inc(t and t.last, "st", 1, mark=f"p3.omm.{hf}.{dm}")
                p4_tensor_half(t, hf)

        def p3_scalar(s):
            for hf in range(2):
                for c in range(4 * hf, 4 * hf + 4):
                    for j2 in range(128 + LA):
                        if j2 < 128:
                            h = j2
                            R.wait(s, "st", f"p3.bc.{c}.{h}")
                            if h >= 4:
                                R.wait(s, "sv", f"p3.dv.{c}.{h-4}")
                            elif c > 0:
                                R.wait(s, "sv", f"p3.dv.{c-1}.127")
                            R.inc(s and s.copy(acb3[h % 4][:, :],
                                               ps_mm[h % 2][:, 0:128]),
                                  "ss", 1, mark=f"p3.se.{c}.{h}")
                    for p_ in range(64):
                        R.wait(s, "sv", f"p3.pr.{c}.{p_}")
                        if p_ >= 2:
                            R.wait(s, "st", f"p3.vm.{c}.{p_-2}")
                        R.inc(s and s.activation(
                            g2t[p_ % 2][:, :],
                            ghalf[:, p_ * 512 + (c % 4) * 128:
                                  p_ * 512 + (c % 4 + 1) * 128],
                            AF.Square), "ss", 1, mark=f"p3.sq.{c}.{p_}")
                    R.wait(s, "st", f"p3.var.{c}")
                    if s:
                        s.activation(smallsc[:, 0:8],
                                     ps_cv[1][:, (c % 2) * 8:(c % 2) * 8 + 8],
                                     AF.Ln, scale=1.0 / 1024.0, bias=constc[:, 0:1])
                        s.activation(rstdT[:, :], smallsc[:, 0:8], AF.Exp, scale=-0.5)
                    R.inc(s and s.last, "ss", 1, mark=f"p3.rstd.{c}")
                    for g_ in range(8):
                        R.wait(s, "st", f"p3.rt.{c}.{g_}")
                        R.inc(s and s.copy(rowbuf[0:1, g_ * 128:(g_ + 1) * 128],
                                           ps_x[0:1, 384:512]),
                              "ss", 1, mark=f"p3.re.{c}.{g_}")
                p4_scalar_half(s, hf)

        def p3_vector(v):
            for hf in range(2):
                for c in range(4 * hf, 4 * hf + 4):
                    for j2 in range(128 + LA):
                        if j2 < 128:
                            h = j2
                            gr = h // 16
                            R.wait(v, "ss", f"p3.se.{c}.{h}")
                            if h == 0 and c == 0:
                                R.wait(v, "sqg", "p2.cin")
                            if h >= 4:
                                R.wait(v, "st", f"p3.y2.{c}.{h-4}")
                            elif c > 0:
                                R.wait(v, "st", f"p3.y2.{c-1}.127")
                            if v:
                                v.tensor_tensor(ctile3[h % 4][:, :],
                                                C_sb[:, gr * 1024 + c * 128:
                                                     gr * 1024 + (c + 1) * 128],
                                                acb3[h % 4][:, :], ALU.mult)
                                v.tensor_scalar(bwts3[h % 4][:, 0:64],
                                                carry[:, h * 64:(h + 1) * 64],
                                                dplb[:, c * 128 + h: c * 128 + h + 1],
                                                None, ALU.mult)
                            R.inc(v and v.last, "sv", 1, mark=f"p3.dv.{c}.{h}")
                        if j2 >= LA and (j2 - LA) % 2 == 1:
                            p_ = (j2 - LA) // 2
                            if c == 0 and p_ == 0:
                                R.wait(v, "sqs", "p2.sync_done")
                            if c == 4 and p_ == 0:
                                R.wait(v, "sqs", "p4.out.0.31")
                            R.wait(v, "st", f"p3.y2.{c}.{2 * p_ + 1}")
                            R.wait(v, "sqs", f"p3.yp.{c}")
                            R.wait(v, "sqs", f"p3.sg.{c}")
                            bank = ps_mm[2 + p_ % 2]
                            if v:
                                v.tensor_tensor(ttile[:, :],
                                                ypr[c % 2][:, p_ * 128:(p_ + 1) * 128],
                                                bank[:, 0:128], ALU.add)
                                v.tensor_tensor(
                                    ghalf[:, p_ * 512 + (c % 4) * 128:
                                          p_ * 512 + (c % 4 + 1) * 128],
                                    ttile[:, :], sgr[c % 2][:, p_ * 128:(p_ + 1) * 128],
                                    ALU.mult)
                            R.inc(v and v.last, "sv", 1, mark=f"p3.pr.{c}.{p_}")
                    for p_ in range(64):
                        g_ = p_ // 8
                        R.wait(v, "st", f"p3.rb.{c}.{g_}")
                        sl = ghalf[:, p_ * 512 + (c % 4) * 128: p_ * 512 + (c % 4 + 1) * 128]
                        R.inc(v and v.scalar_tensor_tensor(
                            sl, sl, nw[:, p_:p_ + 1],
                            ps_x[:, (g_ % 3) * 128:(g_ % 3 + 1) * 128],
                            ALU.mult, ALU.mult), "sv", 1, mark=f"p3.gn.{c}.{p_}")
                for dm in range(NDT):
                    R.wait(v, "st", f"p3.omm.{hf}.{dm}")
                    R.wait(v, "sqs", f"p3.xr.{hf}.{dm}")
                    if dm >= 2:
                        R.wait(v, "sqs", f"p3.x2s.{hf}.{dm-2}")
                    R.inc(v and v.tensor_tensor(x2t[dm % 2][:, :], xre[dm % 2][:, :],
                                                ps_mm[2 + dm % 2][:, :], ALU.add),
                          "sv", 1, mark=f"p3.x2.{hf}.{dm}")
                p4_vector_half(v, hf)

        def p3_sync(sy):
            for hf in range(2):
                for c in range(4 * hf, 4 * hf + 4):
                    if c == 0:
                        R.wait(sy, "st", "p2.yg.7.127")
                        R.wait(sy, "sv", "p2.scan.7.127")
                        R.wait(sy, "ss", "p2.ev.7.63")
                    if c == 4:
                        R.wait(sy, "st", f"p3.omm.0.{NDT-1}")
                    if c % 4 >= 2:
                        R.wait(sy, "sv", f"p3.pr.{c-2}.63")
                    R.wait(sy, "sqs", "p1.sync_done")
                    R.wait(sy, "sqs", f"p2.ypsp.{c}")
                    R.inc(sy and sy.dma_start(out=ypr[c % 2][:, :], in_=yp_hbm[c]),
                          "sqs", 16, mark=f"p3.yp.{c}")
                    R.inc(sy and sy.dma_start(out=sgr[c % 2][:, :], in_=sg_hbm[c]),
                          "sqs", 16, mark=f"p3.sg.{c}")
                for dm in range(NDT):
                    R.wait(sy, "sqg", "wo_precast")
                    R.wait(sy, "sv", f"p3.pr.{4*hf+3}.63")
                    if dm >= 2:
                        R.wait(sy, "st", f"p3.omm.{hf}.{dm-2}")
                    R.inc(sy and sy.dma_start(out=wob[dm % 2][:, :], in_=wo_bf[dm]),
                          "sqs", 16, mark=f"p3.wo.{hf}.{dm}")
                    if dm == 0:
                        R.wait(sy, "sqg", "p2.cout")
                    if dm >= 2:
                        R.wait(sy, "sv", f"p3.x2.{hf}.{dm-2}")
                    R.inc(sy and sy.dma_start(out=xre[dm % 2][:, :],
                                              in_=x_fm[dm][:, hf * 512:(hf + 1) * 512]),
                          "sqs", 16, mark=f"p3.xr.{hf}.{dm}")
                    R.wait(sy, "sv", f"p3.x2.{hf}.{dm}")
                    R.inc(sy and sy.dma_start(out=x2_hbm[dm][:, hf * 512:(hf + 1) * 512],
                                              in_=x2t[dm % 2][:, :]),
                          "sqs", 16, mark=f"p3.x2s.{hf}.{dm}")
                p4_sync_half(sy, hf)

        def p4_tensor_half(t, hf):
            for dm in range(NDT):
                R.wait(t, "sv", f"p4.xc.{hf}.{dm}")
                R.wait(t, "ss", f"p4.sq.{hf}.{dm}")
                if dm == 0:
                    R.wait(t, "ss", f"p3.se.{4*hf+3}.127")
                if t:
                    t.matmul(ps_mm[0][0:1, :], onescol_b[:, :],
                             xn2[:, dm * 512:(dm + 1) * 512],
                             start=(dm == 0), stop=(dm == NDT - 1))
                    t.matmul(ps_mm[1][0:1, :], onescol_b[:, :],
                             zgsb[:, (dm % 2) * 512:(dm % 2 + 1) * 512],
                             start=(dm == 0), stop=(dm == NDT - 1))
                R.inc(t and t.last, "st", 1, mark=f"p4.st.{hf}.{dm}")
            for r in range(2):
                R.wait(t, "sv", f"p4.var.{hf}")
                R.wait(t, "sv", f"p3.gn.{4*hf+3}.63")
                if r == 1:
                    R.wait(t, "ss", f"p4.rstd.{hf}")
                    R.wait(t, "sv", f"p4.bce.{hf}.0")
                src = mrow[0:1, 0:512] if r == 0 else rrow[0:1, 0:512]
                R.inc(t and t.matmul(ps_x[:, 0:512], onesrow_f[0:1, :], src,
                                     start=True, stop=True), "st", 1,
                      mark=f"p4.bc.{hf}.{r}")
            for tt in range(4):
                for kt in range(32):
                    R.wait(t, "sv", f"p4.xn2.{hf}.{kt}")
                    if kt == 0 and tt >= 1:
                        R.wait(t, "sv", f"p4.qe.{hf}.{tt-1}")
                    if tt == 0 and kt == 0 and hf == 0:
                        R.wait(t, "sv", "p2.scan.7.127")
                    if t:
                        t.matmul(ps_cv[0][:, tt * 16:(tt + 1) * 16],
                                 xn2[:, kt * 512 + tt * 128: kt * 512 + (tt + 1) * 128],
                                 wqt_all[:, kt * 16:(kt + 1) * 16],
                                 start=(kt == 0), stop=(kt == 31))
                R.inc(t and t.last, "st", 1, mark=f"p4.q.{hf}.{tt}")
                R.wait(t, "sv", f"p4.qe.{hf}.{tt}")
                if tt == 0 and hf == 1:
                    R.wait(t, "ss", "p4.zte.0.3.3")
                if tt == 0 and hf == 0:
                    R.wait(t, "ss", "p2.bte.7.1")
                if tt >= 1:
                    R.wait(t, "ss", f"p4.qte.{hf}.{tt-1}")
                R.inc(t and t.transpose(ps_tr[0:16, 0:128], qsb[:, :], idn_b[:, :]),
                      "st", 1, mark=f"p4.qt.{hf}.{tt}")
                R.wait(t, "ss", f"p4.qte.{hf}.{tt}")
                if tt >= 1:
                    R.wait(t, "sv", f"p4.m8.{hf}.{tt-1}")
                else:
                    R.wait(t, "ss", f"p3.rstd.{4*hf+3}")
                    if hf == 1:
                        R.wait(t, "sv", "p4.m8.0.3")
                R.inc(t and t.matmul(ps_cv[1][:, 448:512], qT[0:16, :], keyst[:, :],
                                     start=True, stop=True), "st", 1,
                      mark=f"p4.sc.{hf}.{tt}")
            for kt in range(32):
                R.wait(t, "sqg", f"p4.aw.{hf}.{kt}")
                if kt == 0:
                    R.wait(t, "sv", f"p4.var.{hf}")
                    R.wait(t, "sv", f"p3.x2.{hf}.{NDT-1}")
                if t:
                    for tt in range(4):
                        t.matmul(ps_mm[tt][:, :],
                                 xn2[:, kt * 512 + tt * 128: kt * 512 + (tt + 1) * 128],
                                 awtb[kt % 4][:, :], start=(kt == 0), stop=(kt == 31))
                R.inc(t and t.last, "st", 1, mark=f"p4.zk.{hf}.{kt}")
            for tt in range(4):
                for kb in range(4):
                    R.wait(t, "sv", f"p4.zg.{hf}.{tt}")
                    sl = 4 * tt + kb
                    if sl >= 1:
                        R.wait(t, "ss", f"p4.zte.{hf}.{(sl - 1) // 4}.{(sl - 1) % 4}")
                    R.inc(t and t.transpose(ps_tr[:, (sl % 8) * 128:(sl % 8 + 1) * 128],
                                            zgsb[:, tt * 512 + kb * 128:
                                                 tt * 512 + (kb + 1) * 128],
                                            idn_b[:, :]), "st", 1,
                          mark=f"p4.zt.{hf}.{tt}.{kb}")
            for dm in range(NDT):
                R.wait(t, "sqg", f"p4.bw.{hf}.{dm}")
                R.wait(t, "ss", f"p4.zte.{hf}.3.3")
                if dm >= 2:
                    R.wait(t, "sv", f"p4.fin.{hf}.{dm-2}")
                if t:
                    for kb in range(4):
                        t.matmul(ps_mm[2 + dm % 2][:, :],
                                 bwtd[dm % 4][:, kb * 128:(kb + 1) * 128],
                                 zgt_half[:, kb * 512:(kb + 1) * 512],
                                 start=(kb == 0), stop=(kb == 3))
                R.inc(t and t.last, "st", 1, mark=f"p4.o.{hf}.{dm}")

        def p4_scalar_half(s, hf):
            for dm in range(NDT):
                R.wait(s, "sv", f"p4.xc.{hf}.{dm}")
                if dm >= 2:
                    R.wait(s, "st", f"p4.st.{hf}.{dm-2}")
                R.inc(s and s.activation(zgsb[:, (dm % 2) * 512:(dm % 2 + 1) * 512],
                                         xn2[:, dm * 512:(dm + 1) * 512], AF.Square),
                      "ss", 1, mark=f"p4.sq.{hf}.{dm}")
            R.wait(s, "sv", f"p4.var.{hf}")
            if s:
                s.activation(rowbuf2[0:1, 0:512], rowbuf[0:1, 0:512], AF.Ln,
                             bias=constc[0:1, 0:1])
                s.activation(rrow[0:1, 0:512], rowbuf2[0:1, 0:512], AF.Exp, scale=-0.5)
            R.inc(s and s.last, "ss", 1, mark=f"p4.rstd.{hf}")
            for tt in range(4):
                R.wait(s, "st", f"p4.qt.{hf}.{tt}")
                R.inc(s and s.copy(qT[0:16, :], ps_tr[0:16, 0:128]), "ss", 1,
                      mark=f"p4.qte.{hf}.{tt}")
                R.wait(s, "sv", f"p4.m8.{hf}.{tt}")
                R.inc(s and s.mul(negt1[:, 0:1], m8a[:, 0:1], -1.0), "ss", 1,
                      mark=f"p4.nt1.{hf}.{tt}")
                R.wait(s, "sv", f"p4.m8b.{hf}.{tt}")
                R.inc(s and s.activation(esb[:, :], ssb[:, :], AF.Exp,
                                         bias=negt1[:, 0:1]), "ss", 1,
                      mark=f"p4.e.{hf}.{tt}")
                R.wait(s, "sv", f"p4.rs.{hf}.{tt}")
                if s:
                    s.activation(smallsc[:, 1:2], rs1[:, 0:1], AF.Ln)
                    s.activation(rs1[:, 0:1], smallsc[:, 1:2], AF.Exp, scale=-1.0)
                R.inc(s and s.last, "ss", 1, mark=f"p4.rr.{hf}.{tt}")
            for tt in range(4):
                for kb in range(4):
                    R.wait(s, "st", f"p4.zt.{hf}.{tt}.{kb}")
                    sl = 4 * tt + kb
                    R.inc(s and s.copy(zgt_half[:, kb * 512 + tt * 128:
                                                kb * 512 + (tt + 1) * 128],
                                       ps_tr[:, (sl % 8) * 128:(sl % 8 + 1) * 128]),
                          "ss", 1, mark=f"p4.zte.{hf}.{tt}.{kb}")

        def p4_vector_half(v, hf):
            for dm in range(NDT):
                R.wait(v, "sqs", f"p4.x.{hf}.{dm}")
                if dm == 0:
                    R.wait(v, "st", f"p3.omm.{hf}.{NDT-1}")
                R.inc(v and v.tensor_copy(xn2[:, dm * 512:(dm + 1) * 512],
                                          x2l[dm % 2][:, :]),
                      "sv", 1, mark=f"p4.xc.{hf}.{dm}")
            R.wait(v, "st", f"p4.st.{hf}.{NDT-1}")
            if v:
                v.tensor_scalar(mrow[0:1, 0:512], ps_mm[0][0:1, :], 1.0 / DM, None, ALU.mult)
                v.tensor_scalar(rowbuf[0:1, 0:512], ps_mm[1][0:1, :], 1.0 / DM, None,
                                ALU.mult)
                v.tensor_tensor(rowbuf2[0:1, 0:512], mrow[0:1, 0:512], mrow[0:1, 0:512],
                                ALU.mult)
                v.tensor_tensor(rowbuf[0:1, 0:512], rowbuf[0:1, 0:512],
                                rowbuf2[0:1, 0:512], ALU.subtract)
            R.inc(v and v.last, "sv", 1, mark=f"p4.var.{hf}")
            for r, dst in enumerate([mb2[:, :], rb2[:, :]]):
                R.wait(v, "st", f"p4.bc.{hf}.{r}")
                R.inc(v and v.tensor_copy(dst, ps_x[:, 0:512]), "sv", 1,
                      mark=f"p4.bce.{hf}.{r}")
            for dm in range(NDT):
                if v:
                    sl = xn2[:, dm * 512:(dm + 1) * 512]
                    v.tensor_tensor(sl, sl, mb2[:, :], ALU.subtract)
                    v.tensor_tensor(sl, sl, rb2[:, :], ALU.mult)
                R.inc(v and v.last, "sv", 1, mark=f"p4.xn2.{hf}.{dm}")
            for tt in range(4):
                R.wait(v, "st", f"p4.q.{hf}.{tt}")
                R.inc(v and v.tensor_copy(qsb[:, :], ps_cv[0][:, tt * 16:(tt + 1) * 16]),
                      "sv", 1, mark=f"p4.qe.{hf}.{tt}")
                R.wait(v, "st", f"p4.sc.{hf}.{tt}")
                if v:
                    v.tensor_copy(ssb[:, :], ps_cv[1][:, 448:512])
                    v.max(m8a[:, :], ssb[:, :])
                R.inc(v and v.last, "sv", 1, mark=f"p4.m8.{hf}.{tt}")
                R.wait(v, "ss", f"p4.nt1.{hf}.{tt}")
                if v:
                    v.tensor_scalar(msk[:, :], ssb[:, :], m8a[:, 7:8], None, ALU.is_ge)
                    v.scalar_tensor_tensor(esb[:, :], msk[:, :], -1e38, ssb[:, :],
                                           ALU.mult, ALU.add)
                    v.max(m8b[:, :], esb[:, :])
                R.inc(v and v.last, "sv", 1, mark=f"p4.m8b.{hf}.{tt}")
                R.wait(v, "ss", f"p4.e.{hf}.{tt}")
                if v:
                    v.tensor_scalar(msk[:, :], ssb[:, :], m8b[:, 7:8], None, ALU.is_ge)
                    v.tensor_tensor(esb[:, :], esb[:, :], msk[:, :], ALU.mult)
                    v.tensor_reduce(rs1[:, 0:1], esb[:, :], AX.X, ALU.add)
                R.inc(v and v.last, "sv", 1, mark=f"p4.rs.{hf}.{tt}")
                R.wait(v, "ss", f"p4.rr.{hf}.{tt}")
                R.inc(v and v.tensor_scalar(gsb[:, tt * 64:(tt + 1) * 64], esb[:, :],
                                            rs1[:, 0:1], None,
                                            ALU.mult), "sv", 1, mark=f"p4.g.{hf}.{tt}")
            for tt in range(4):
                R.wait(v, "st", f"p4.zk.{hf}.{31}")
                R.wait(v, "sv", f"p4.g.{hf}.{tt}")
                zv = ps_mm[tt][:, :].rearrange("p (e r) -> p e r", r=8)
                gv = gsb[:, tt * 64:(tt + 1) * 64].unsqueeze(2).broadcast_to((128, 64, 8))
                R.inc(v and v.tensor_tensor(
                    zgsb[:, tt * 512:(tt + 1) * 512].rearrange("p (e r) -> p e r", r=8),
                    zv, gv, ALU.mult), "sv", 1, mark=f"p4.zg.{hf}.{tt}")
            for dm in range(NDT):
                R.wait(v, "st", f"p4.o.{hf}.{dm}")
                R.wait(v, "sqs", f"p4.x2b.{hf}.{dm}")
                if dm >= 2:
                    R.wait(v, "sqs", f"p4.out.{hf}.{dm-2}")
                R.inc(v and v.scalar_tensor_tensor(
                    outbuf[dm % 2][:, :], ps_mm[2 + dm % 2][:, :], 2.0,
                    x2l[dm % 2][:, :], ALU.mult, ALU.add), "sv", 1,
                    mark=f"p4.fin.{hf}.{dm}")

        def p4_sync_half(sy, hf):
            for dm in range(NDT):
                R.wait(sy, "sqs", f"p3.x2s.{hf}.{dm}")
                if dm >= 2:
                    R.wait(sy, "sv", f"p4.xc.{hf}.{dm-2}")
                R.inc(sy and sy.dma_start(out=x2l[dm % 2][:, :],
                                          in_=x2_hbm[dm][:, hf * 512:(hf + 1) * 512]),
                      "sqs", 16, mark=f"p4.x.{hf}.{dm}")
            for dm in range(NDT):
                if dm < 2:
                    R.wait(sy, "sv", f"p4.xn2.{hf}.{NDT-1}")
                else:
                    R.wait(sy, "sv", f"p4.fin.{hf}.{dm-2}")
                R.inc(sy and sy.dma_start(out=x2l[dm % 2][:, :],
                                          in_=x2_hbm[dm][:, hf * 512:(hf + 1) * 512]),
                      "sqs", 16, mark=f"p4.x2b.{hf}.{dm}")
                R.wait(sy, "sv", f"p4.fin.{hf}.{dm}")
                R.inc(sy and sy.dma_start(out=out_fm[dm][:, hf * 512:(hf + 1) * 512],
                                          in_=outbuf[dm % 2][:, :]),
                      "sqs", 16, mark=f"p4.out.{hf}.{dm}")

        def p4_gpsimd(g):
            for hf in range(2):
                for kt in range(32):
                    if kt >= 4:
                        R.wait(g, "st", f"p4.zk.{hf}.{kt-4}")
                    else:
                        R.wait(g, "sv", f"p3.x2.{hf}.{NDT-1}")
                    R.inc(g and g.dma_start(out=awtb[kt % 4][:, :], in_=awt_l[kt]),
                          "sqg", 16, mark=f"p4.aw.{hf}.{kt}")
                for dm in range(NDT):
                    if dm >= 4:
                        R.wait(g, "st", f"p4.o.{hf}.{dm-4}")
                    R.inc(g and g.dma_start(out=bwtd[dm % 4][:, :], in_=bwt_l[dm]),
                          "sqg", 16, mark=f"p4.bw.{hf}.{dm}")

        def g_body(g):
            p1_gpsimd(g)
            if STAGE != "p1":
                p2_gpsimd(g)
                p4_gpsimd(g)

        def sy_body(sy):
            p0_sync(sy)
            p1_sync(sy)
            if STAGE != "p1":
                p2_sync(sy)
                p3_sync(sy)
            R.wait_all(sy, "sqs")
            R.wait_all(sy, "sqg")

        def t_body(t):
            p0_tensor(t)
            p1_tensor(t)
            if STAGE != "p1":
                p2_tensor(t)
                p3_tensor(t)

        def v_body(v):
            p0_vector(v)
            p1_vector(v)
            if STAGE != "p1":
                p2_vector(v)
                p3_vector(v)

        def s_body(s):
            p0_scalar(s)
            p1_scalar(s)
            if STAGE != "p1":
                p2_scalar(s)
                p3_scalar(s)

        R.reset_pass(True)
        for f in (g_body, sy_body, t_body, v_body, s_body):
            f(None)
        R.reset_pass(False)
        R.log = []
        global LAST_R
        LAST_R = R
        with nc.Block() as block:
            for nm, fb in (("g", g_body), ("sy", sy_body), ("t", t_body),
                           ("v", v_body), ("s", s_body)):
                R.stream = nm
                qn = {"g": "sqg", "sy": "sqs"}.get(nm)
                getattr(block, {"g": "gpsimd", "sy": "sync", "t": "tensor",
                                "v": "vector", "s": "scalar"}[nm])(
                    lambda e, fb=fb, qn=qn: fb(EngW(e, R, qn)))
    return nc


LAST_R = None


# version 05
# speedup vs baseline: 1.1437x; 1.0051x over previous
import contextlib
import numpy as np
import ml_dtypes
import concourse.bass as bass
import concourse.mybir as mybir
from concourse.bass_utils import run_bass_kernel_spmd

F32 = mybir.dt.float32
BF16 = mybir.dt.bfloat16
U8 = mybir.dt.uint8
AF = mybir.ActivationFunctionType
ALU = mybir.AluOpType
AX = mybir.AxisListType
BF = ml_dtypes.bfloat16

T = 1024
CS = 128
NCH = 8
DM = 4096
NDT = 32
H, P, N, G = 128, 64, 128, 8
INTER = 8192
NIT = 64
CONV = 10240
PROJ = 18560
NCT = 145
EPS = 1e-5
CT_DT = 144
CT_XBC = list(range(64, 144))
CT_GATE = list(range(0, 64))
CT_ORDER = [CT_DT] + CT_XBC + CT_GATE
LA = 2
NSLOT = 4

STAGE = "full"
DEBUG_TAPS = ()


class Rec:
    QDS = {"sqs": 16, "sqg": 8}

    def __init__(self):
        self.tbl = {}
        self.dry = True
        self.sems = {}
        self.qcnt = {}
        self.log = None
        self.stream = None

    def sem(self, name):
        if name in ("sqs", "sqg"):
            self.qcnt[name] = 0
            for i in range(self.QDS[name]):
                self.sems.setdefault(f"{name}{i}", [None, 0])
        else:
            self.sems.setdefault(name, [None, 0])

    def reset_pass(self, dry):
        self.dry = dry
        for v in self.sems.values():
            v[1] = 0
        for k in self.qcnt:
            self.qcnt[k] = 0

    def bind(self, name, handle):
        self.sems[name][0] = handle

    def inc(self, instr, name, n=1, mark=None):
        if name in ("sqs", "sqg"):
            assert n in (0, 16)
            if n == 0:
                return self.fence(name, mark)
            actual = f"{name}{self.qcnt[name] % self.QDS[name]}"
            self.qcnt[name] += 1
        else:
            actual = name
        s = self.sems[actual]
        s[1] += n
        if mark is not None:
            key = mark
            val = (actual, s[1])
            if self.dry:
                assert key not in self.tbl, f"dup mark {key}"
                self.tbl[key] = val
            else:
                assert self.tbl.get(key) == val, \
                    f"mark drift {key}: {self.tbl.get(key)} vs {val}"
        if instr is not None:
            instr.then_inc(s[0], n)
        if self.log is not None and not self.dry and n > 0:
            self.log.append((self.stream, "i", actual, n, mark))
        return s[1]

    def fence(self, qname, mark):
        snap = tuple((f"{qname}{i}", self.sems[f"{qname}{i}"][1])
                     for i in range(self.QDS[qname]))
        if self.dry:
            assert mark not in self.tbl, f"dup mark {mark}"
            self.tbl[mark] = ("F", snap)
        else:
            assert self.tbl.get(mark) == ("F", snap), f"fence drift {mark}"
        return None

    def wait(self, eng, name, mark, extra=0):
        if eng is None:
            return
        if mark not in self.tbl:
            if self.dry:
                return
            raise KeyError(f"wait on unknown mark {mark}")
        ent = self.tbl[mark]
        if ent[0] == "F":
            for actual, v in ent[1]:
                if v > 0:
                    eng.wait_ge(self.sems[actual][0], v)
                    if self.log is not None and not self.dry:
                        self.log.append((self.stream, "w", actual, v, mark))
            return
        actual, v = ent
        v += extra
        if v > 0:
            eng.wait_ge(self.sems[actual][0], v)
            if self.log is not None and not self.dry:
                self.log.append((self.stream, "w", actual, v, mark))

    def wait_all(self, eng, qname):
        if eng is None:
            return
        for i in range(self.QDS[qname]):
            actual = f"{qname}{i}"
            v = self.sems[actual][1]
            if v > 0:
                eng.wait_ge(self.sems[actual][0], v)
                if self.log is not None and not self.dry:
                    self.log.append((self.stream, "w", actual, v, None))


class EngW:
    def __init__(self, e, R=None, q=None):
        self._e = e
        self._R = R
        self._q = q
        self.last = None

    def __getattr__(self, k):
        f = getattr(self._e, k)
        if not callable(f):
            return f
        if k == "dma_start" and self._q is not None:
            def g(*a, **kw):
                R, q = self._R, self._q
                actual = f"{q}{R.qcnt[q] % R.QDS[q]}"
                prev = R.sems[actual][1]
                if prev > 0:
                    self._e.wait_ge(R.sems[actual][0], prev)
                    if R.log is not None and not R.dry:
                        R.log.append((R.stream, "w", actual, prev, "thr"))
                r = f(*a, **kw)
                self.last = r
                return r
            return g
        def g(*a, **kw):
            r = f(*a, **kw)
            self.last = r
            return r
        return g


def _host_inputs(core, x_seq_half, W1, conv_w, conv_b, dt_bias, A_log, D, norm_w,
                 W_o, A_w, B_w, Wq, keys):
    ins = {}
    ins["x_fm"] = np.ascontiguousarray(
        x_seq_half.T.reshape(NDT, 128, T))
    ins["w1_l"] = np.ascontiguousarray(
        W1.reshape(NCT, 128, NDT, 128).transpose(0, 3, 2, 1).reshape(NCT, 128, NDT * 128))
    cw = conv_w[:, 0, :]
    cwd = np.zeros((80, 128, 4, 128), np.float32)
    idx = np.arange(128)
    for k in range(4):
        cwd[:, idx, k, idx] = cw.reshape(80, 128, 4)[:, :, k]
    ins["convw_l"] = np.ascontiguousarray(cwd.reshape(80, 128, 4 * 128))
    ins["convw_cols"] = np.ascontiguousarray(
        cw.reshape(80, 128, 4).transpose(1, 0, 2).reshape(128, 320))
    ins["convb_l"] = np.ascontiguousarray(conv_b.reshape(80, 128).T)
    ins["dtbias_c"] = np.ascontiguousarray(dt_bias.reshape(128, 1))
    ins["alog_c"] = np.ascontiguousarray(A_log.reshape(128, 1))
    ins["d_c"] = np.ascontiguousarray(D.reshape(128, 1))
    ins["nw_l"] = np.ascontiguousarray(norm_w.reshape(NIT, 128).T)
    ins["cmask_c"] = np.full((128, 1), float(core & 1), np.float32)
    ins["wo_l"] = np.ascontiguousarray(
        W_o.reshape(32, 128, 64, 128).transpose(0, 3, 2, 1).reshape(32, 128, 64 * 128))
    ins["awt_l"] = np.ascontiguousarray(A_w.T.reshape(32, 128, 512))
    ins["bwt_l"] = np.ascontiguousarray(
        B_w.reshape(32, 128, 4, 128).transpose(0, 3, 2, 1).reshape(32, 128, 512))
    ins["wqt_l"] = np.ascontiguousarray(
        Wq.T.reshape(32, 128, 16).transpose(1, 0, 2).reshape(128, 512))
    ins["keyst_l"] = np.ascontiguousarray(keys.T)
    ins["idn_f_l"] = np.eye(128, dtype=np.float32)
    ins["tribig_l"] = np.where(np.triu(np.ones((128, 128), bool)), 1e30, 0.0).astype(np.float32)
    ins["onesrow_l"] = np.ones((1, 128), np.float32)
    ins["onescol_l"] = np.ones((128, 1), np.float32)
    return ins


def kernel(L0_attn_input_states, L0_attn_in_proj_w, L0_attn_conv_w, L0_attn_conv_b,
           L0_attn_dt_bias, L0_attn_A_log, L0_attn_D, L0_attn_norm_w, L0_attn_out_w,
           L0_ffn_A_w, L0_ffn_B_w, L0_ffn_Wq_w, L0_ffn_keys):
    x = np.ascontiguousarray(np.asarray(L0_attn_input_states, np.float32))
    args = (np.asarray(L0_attn_in_proj_w, np.float32),
            np.asarray(L0_attn_conv_w, np.float32),
            np.asarray(L0_attn_conv_b, np.float32),
            np.asarray(L0_attn_dt_bias, np.float32),
            np.asarray(L0_attn_A_log, np.float32),
            np.asarray(L0_attn_D, np.float32),
            np.asarray(L0_attn_norm_w, np.float32),
            np.asarray(L0_attn_out_w, np.float32),
            np.asarray(L0_ffn_A_w, np.float32),
            np.asarray(L0_ffn_B_w, np.float32),
            np.asarray(L0_ffn_Wq_w, np.float32),
            np.asarray(L0_ffn_keys, np.float32))
    nc = build_program()
    in_maps = []
    for core in range(8):
        s, hf = core >> 1, core & 1
        in_maps.append(_host_inputs(core, x[s, hf * T:(hf + 1) * T, :], *args))
    res = run_bass_kernel_spmd(nc, in_maps, core_ids=list(range(8)))
    out = np.empty((4, 2048, 4096), np.float32)
    for core in range(8):
        s, hf = core >> 1, core & 1
        o = np.asarray(res.results[core]["out_fm"])
        out[s, hf * T:(hf + 1) * T, :] = o.transpose(2, 0, 1).reshape(T, DM)
    return out


KB = 1024


def build_program():
    nc = bass.Bass("TRN2", target_bir_lowering=False)

    x_fm = nc.dram_tensor("x_fm", [NDT, 128, T], F32, kind="ExternalInput").ap()
    w1_l = nc.dram_tensor("w1_l", [NCT, 128, NDT * 128], F32, kind="ExternalInput").ap()
    convw_l = nc.dram_tensor("convw_l", [80, 128, 512], F32, kind="ExternalInput").ap()
    convw_cols = nc.dram_tensor("convw_cols", [128, 320], F32, kind="ExternalInput").ap()
    convb_l = nc.dram_tensor("convb_l", [128, 80], F32, kind="ExternalInput").ap()
    dtbias_c = nc.dram_tensor("dtbias_c", [128, 1], F32, kind="ExternalInput").ap()
    alog_c = nc.dram_tensor("alog_c", [128, 1], F32, kind="ExternalInput").ap()
    d_c = nc.dram_tensor("d_c", [128, 1], F32, kind="ExternalInput").ap()
    nw_l = nc.dram_tensor("nw_l", [128, NIT], F32, kind="ExternalInput").ap()
    cmask_c = nc.dram_tensor("cmask_c", [128, 1], F32, kind="ExternalInput").ap()
    wo_l = nc.dram_tensor("wo_l", [32, 128, 8192], F32, kind="ExternalInput").ap()
    awt_l = nc.dram_tensor("awt_l", [32, 128, 512], F32, kind="ExternalInput").ap()
    bwt_l = nc.dram_tensor("bwt_l", [32, 128, 512], F32, kind="ExternalInput").ap()
    wqt_l = nc.dram_tensor("wqt_l", [128, 512], F32, kind="ExternalInput").ap()
    keyst_l = nc.dram_tensor("keyst_l", [16, 64], F32, kind="ExternalInput").ap()
    idn_f_l = nc.dram_tensor("idn_f_l", [128, 128], F32, kind="ExternalInput").ap()
    tribig_l = nc.dram_tensor("tribig_l", [128, 128], F32, kind="ExternalInput").ap()
    onesrow_l = nc.dram_tensor("onesrow_l", [1, 128], F32, kind="ExternalInput").ap()
    onescol_l = nc.dram_tensor("onescol_l", [128, 1], F32, kind="ExternalInput").ap()

    out_fm = nc.dram_tensor("out_fm", [NDT, 128, T], F32, kind="ExternalOutput").ap()

    xtok_hbm = nc.dram_tensor("xtok_hbm", [NCH, 128, INTER], BF16).ap()
    sg_hbm = nc.dram_tensor("sg_hbm", [NCH, 128, INTER], BF16).ap()
    yp_hbm = nc.dram_tensor("yp_hbm", [NCH, 128, INTER], BF16).ap()
    x2_hbm = nc.dram_tensor("x2_hbm", [NDT, 128, T], F32).ap()
    wo_bf = nc.dram_tensor("wo_bf", [32, 128, 8192], BF16).ap()
    tails_bi = nc.dram_tensor("tails_bi", [128, 240], BF16).ap()
    tails_bo = nc.dram_tensor("tails_bo", [256, 240], BF16).ap()
    carry_bi = nc.dram_tensor("carry_bi", [128, INTER], BF16).ap()
    carry_bo = nc.dram_tensor("carry_bo", [256, INTER], BF16).ap()

    taps = {}
    def mktap(name, shape, dtype=F32):
        if name in DEBUG_TAPS:
            taps[name] = nc.dram_tensor("tap_" + name, shape, dtype,
                                        kind="ExternalOutput").ap()
        return taps.get(name)

    R = Rec()
    for s_ in ("sqg", "sqs", "st", "sv", "ss", "sc"):
        R.sem(s_)

    with contextlib.ExitStack() as ST0:
        def sbt(name, shape, dtype):
            return ST0.enter_context(nc.sbuf_tensor(name, shape, dtype))
        def psb(name, shape, dtype=F32):
            return ST0.enter_context(nc.psum_tensor(name, shape, dtype))

        C_sb = sbt("C_sb", [128, 8 * 1024], BF16)
        expcum_b = sbt("expcum_b", [128, T], BF16)
        dplb = sbt("dplb", [128, T], F32)
        dtb = sbt("dtb", [128, 128], F32)
        idn_f = sbt("idn_f", [128, 128], F32)
        idn_b = sbt("idn_b", [128, 128], BF16)
        tribig = sbt("tribig", [128, 128], BF16)
        onesrow_f = sbt("onesrow_f", [1, 128], F32)
        onescol_b = sbt("onescol_b", [128, 1], BF16)
        dtbias = sbt("dtbias", [128, 1], F32)
        negA = sbt("negA", [128, 1], F32)
        dcol = sbt("dcol", [128, 1], F32)
        cmask = sbt("cmask", [128, 1], F32)
        nw = sbt("nw", [128, NIT], F32)
        smallsc = sbt("smallsc", [128, 8], F32)
        rowbuf = sbt("rowbuf", [1, 1024], F32)
        rowbuf2 = sbt("rowbuf2", [1, 1024], F32)
        mrow = sbt("mrow", [1, T], F32)
        rrow = sbt("rrow", [1, T], F32)
        keyst = sbt("keyst", [16, 64], BF16)
        wqt_all = sbt("wqt_all", [128, 512], BF16)
        convb = sbt("convb", [128, 80], F32)
        constc = sbt("constc", [128, 2], F32)
        convwc = sbt("convwc", [128, 320], F32)

        arena = sbt("arena", [128, 160 * KB], U8)
        def vw(off_kb, parts, nel, dtype):
            esz = 2 if dtype == BF16 else 4
            off = int(off_kb * KB)
            assert off % 64 == 0 and off + nel * esz <= 160 * KB, (off_kb, nel)
            return arena[0:parts, off: off + nel * esz].bitcast(dtype)

        xn = vw(0, 128, NDT * 1024, BF16)
        xtok_sb = [vw(0, 128, INTER, BF16), vw(16, 128, INTER, BF16)]
        prev = vw(32, 128, INTER, BF16)
        etile = [vw(48 + 0.25 * i, 128, 128, BF16) for i in range(4)]
        stile = [vw(49 + 0.25 * i, 128, 128, BF16) for i in range(4)]
        ctile = [vw(50 + 0.25 * i, 128, 128, BF16) for i in range(4)]
        acb = [vw(51 + 0.25 * i, 128, 128, BF16) for i in range(4)]
        bwt_s = [vw(52 + 0.25 * i, 128, 128, BF16) for i in range(4)]
        sgr = [vw(0, 128, INTER, BF16), vw(16, 128, INTER, BF16)]
        wob = [vw(0, 128, 8192, BF16), vw(16, 128, 8192, BF16)]
        xre = [vw(32 + 2 * i, 128, 512, F32) for i in range(2)]
        x2t = [vw(36 + 2 * i, 128, 512, F32) for i in range(2)]
        rstdT = vw(40, 128, 8, F32)
        g2t = [vw(40.5 + 0.25 * i, 128, 128, BF16) for i in range(2)]
        ttile = vw(41, 128, 128, F32)
        acb3 = [vw(42 + 0.25 * i, 128, 128, BF16) for i in range(4)]
        ctile3 = [vw(43 + 0.25 * i, 128, 128, BF16) for i in range(4)]
        bwts3 = [vw(44 + 0.25 * i, 128, 128, BF16) for i in range(4)]
        carry = vw(48, 128, INTER, BF16)
        xn2 = vw(0, 128, 32 * 512, BF16)
        bwtd = [vw(32 + i, 128, 512, BF16) for i in range(4)]
        x2l = [vw(85 + 2 * i, 128, 512, F32) for i in range(2)]

        xld = [vw(64 + 4 * i, 128, T, F32) for i in range(2)]
        xsq = [vw(72 + 2 * i, 128, T, BF16) for i in range(2)]
        mb = vw(76, 128, T, F32)
        rb = vw(80, 128, T, F32)
        wbuf = [vw(64 + 4 * i, 128, 2048, BF16) for i in range(4)]
        cwb = [vw(80 + i, 128, 512, BF16) for i in range(2)]
        xbcr = [vw(82, 128, 3 + T, BF16), vw(84.0625, 128, 3 + T, BF16)]
        hsfm = [vw(86.25, 128, T, BF16), vw(88.25, 128, T, BF16)]
        xstrip = [vw(90.25, 128, T, BF16), vw(92.25, 128, T, BF16)]
        first3 = vw(94.25, 128, 240, BF16)
        tails = vw(94.75, 128, 240, BF16)
        pad3 = vw(95.25, 128, 240, BF16)
        ypr = [vw(80, 128, INTER, BF16), vw(64, 128, INTER, BF16)]
        zgsb = vw(64, 128, 4 * 512, BF16)
        zgt_half = vw(68, 128, 4 * 512, BF16)
        awtb = [vw(72 + i, 128, 512, BF16) for i in range(4)]
        qsb = vw(76, 128, 16, BF16)
        qT = vw(76.0625, 16, 128, BF16)
        ssb = vw(76.375, 128, 64, F32)
        esb = vw(76.625, 128, 64, F32)
        m8a = vw(76.875, 128, 8, F32)
        m8b = vw(76.9375, 128, 8, F32)
        msk = vw(77.25, 128, 64, F32)
        gsb = vw(77.5, 128, 256, BF16)
        negt1 = vw(78, 128, 1, F32)
        rs1 = vw(78.0625, 128, 1, F32)
        outbuf = [vw(100 + 2 * i, 128, 512, F32) for i in range(2)]
        mb2 = vw(96, 128, 512, F32)
        rb2 = vw(98, 128, 512, F32)

        B_sb = vw(96, 128, INTER, BF16)
        sraw = vw(112, 128, 1024, BF16)
        btok = vw(114, 128, 1024, BF16)
        x6 = vw(116, 128, 6, BF16)
        patch3 = vw(116.0625, 128, 240, F32)
        patchh = vw(117.0625, 128, 192, BF16)
        ghalf = vw(96, 128, 64 * 512, BF16)

        dtr = vw(128, 128, T, F32)
        dt_t = vw(132, 128, T, F32)
        tmpA = vw(136, 128, T, F32)
        ncmb = vw(140, 128, T, F32)
        patcht = vw(128, 3, INTER, BF16)
        ypc = vw(128, 128, INTER, BF16)
        awt_all = vw(128, 128, 32 * 512, BF16)
        cum = vw(144, 128, T, F32)
        ncmbT = vw(148, 128, T, F32)
        wTT = vw(152, 128, T, F32)
        cdecb = vw(156, 128, T, F32)

        ps_mm = [psb(f"ps_mm{i}", [128, 512], F32) for i in range(4)]
        ps_cv = [psb(f"ps_cv{i}", [128, 512], F32) for i in range(2)]
        ps_tr = psb("ps_tr", [128, 1024], BF16)
        ps_x = psb("ps_x", [128, 512], F32)

        for s_ in list(R.sems):
            R.bind(s_, ST0.enter_context(nc.semaphore(s_)))

        def tile_ct(i):
            return CT_ORDER[i]
        def is_xbc(i):
            return 1 <= i <= 80
        def is_gate(i):
            return i >= 81
        def evac_sem(i):
            return "ss" if is_gate(i) else "sv"
        def xn_sl(kt, th):
            return xn[:, kt * 1024 + th * 512: kt * 1024 + (th + 1) * 512]

        def p0_sync(sy):
            for dm in range(NDT):
                if dm >= 2:
                    R.wait(sy, "sv", f"p0.xc.{dm-2}")
                    R.wait(sy, "ss", f"p0.xq.{dm-2}")
                R.inc(sy and sy.dma_start(out=xld[dm % 2], in_=x_fm[dm]),
                      "sqs", 16, mark=f"p0.x.{dm}")
            for apd, aps_ in [(idn_f[:, :], idn_f_l), (onesrow_f[0:1, :], onesrow_l),
                              (dtbias[:, :], dtbias_c), (negA[:, :], alog_c),
                              (dcol[:, :], d_c), (cmask[:, :], cmask_c),
                              (convb[:, :], convb_l), (convwc[:, :], convw_cols),
                              (nw[:, :], nw_l)]:
                R.inc(sy and sy.dma_start(out=apd, in_=aps_), "sqs", 16)
            R.inc(None, "sqs", 0, mark="consts_f")

        def p0_vector(v):
            if v:
                v.memset(constc[:, 0:1], EPS)
                v.memset(constc[:, 1:2], 1.0)
            R.inc(v and v.last, "sv", 1, mark="p0.constc")
            for dm in range(NDT):
                R.wait(v, "sqs", f"p0.x.{dm}")
                R.inc(v and v.tensor_copy(xn[:, dm * 1024:(dm + 1) * 1024], xld[dm % 2]),
                      "sv", 1, mark=f"p0.xc.{dm}")
            R.wait(v, "st", f"p0.st.{NDT-1}")
            if v:
                v.tensor_scalar(mrow[0:1, 0:512], ps_mm[0][0:1, :], 1.0 / DM, None, ALU.mult)
                v.tensor_scalar(mrow[0:1, 512:1024], ps_mm[1][0:1, :], 1.0 / DM, None, ALU.mult)
            R.inc(v and v.last, "sv", 1, mark="p0.mrow")
            if v:
                v.tensor_scalar(rowbuf[0:1, 0:512], ps_mm[2][0:1, :], 1.0 / DM, None, ALU.mult)
                v.tensor_scalar(rowbuf[0:1, 512:1024], ps_mm[3][0:1, :], 1.0 / DM, None, ALU.mult)
                v.tensor_tensor(rowbuf2[0:1, :], mrow[0:1, :], mrow[0:1, :], ALU.mult)
                v.tensor_tensor(rowbuf[0:1, :], rowbuf[0:1, :], rowbuf2[0:1, :], ALU.subtract)
            R.inc(v and v.last, "sv", 1, mark="p0.var")
            for r, dst in enumerate([mb[:, 0:512], mb[:, 512:1024],
                                     rb[:, 0:512], rb[:, 512:1024]]):
                R.wait(v, "st", f"p0.bc.{r}")
                R.inc(v and v.tensor_copy(dst, ps_x[:, :]), "sv", 1, mark=f"p0.bce.{r}")
            for dm in range(NDT):
                if v:
                    sl = xn[:, dm * 1024:(dm + 1) * 1024]
                    v.tensor_tensor(sl, sl, mb[:, :], ALU.subtract)
                    v.tensor_tensor(sl, sl, rb[:, :], ALU.mult)
                R.inc(v and v.last, "sv", 1, mark=f"p0.app.{dm}")

        def p0_scalar(s):
            for dm in range(NDT):
                R.wait(s, "sv", f"p0.xc.{dm}")
                if dm >= 2:
                    R.wait(s, "st", f"p0.st.{dm-2}")
                R.inc(s and s.activation(xsq[dm % 2], xn[:, dm * 1024:(dm + 1) * 1024],
                                         AF.Square), "ss", 1, mark=f"p0.xq.{dm}")
            R.wait(s, "sv", "p0.var")
            R.wait(s, "sv", "p0.constc")
            if s:
                s.activation(rowbuf2[0:1, :], rowbuf[0:1, :], AF.Ln, bias=constc[0:1, 0:1])
                s.activation(rrow[0:1, :], rowbuf2[0:1, :], AF.Exp, scale=-0.5)
            R.inc(s and s.last, "ss", 1, mark="p0.rstd")

        def p0_tensor(t):
            R.wait(t, "sqg", "consts_b")
            for dm in range(NDT):
                R.wait(t, "sv", f"p0.xc.{dm}")
                R.wait(t, "ss", f"p0.xq.{dm}")
                if t:
                    for half in range(2):
                        t.matmul(ps_mm[half][0:1, :], onescol_b[:, :],
                                 xn_sl(dm, half), start=(dm == 0), stop=(dm == NDT - 1))
                        t.matmul(ps_mm[2 + half][0:1, :], onescol_b[:, :],
                                 xsq[dm % 2][:, half * 512:(half + 1) * 512],
                                 start=(dm == 0), stop=(dm == NDT - 1))
                R.inc(t and t.last, "st", 1, mark=f"p0.st.{dm}")
            R.wait(t, "sqs", "consts_f")
            srcs = [mrow[0:1, 0:512], mrow[0:1, 512:1024],
                    rrow[0:1, 0:512], rrow[0:1, 512:1024]]
            for r, src in enumerate(srcs):
                R.wait(t, "sv", "p0.mrow" if r < 2 else "p0.var")
                if r >= 2:
                    R.wait(t, "ss", "p0.rstd")
                if r >= 1:
                    R.wait(t, "sv", f"p0.bce.{r-1}")
                R.inc(t and t.matmul(ps_x[:, :], onesrow_f[0:1, :], src,
                                     start=True, stop=True), "st", 1, mark=f"p0.bc.{r}")

        def p1_gpsimd(g):
            R.inc(g and g.dma_start(out=wo_bf, in_=wo_l), "sqg", 16, mark="wo_precast")
            for apd, aps_ in [(idn_b[:, :], idn_f_l), (tribig[:, :], tribig_l),
                              (onescol_b[:, :], onescol_l), (keyst[:, :], keyst_l),
                              (wqt_all[:, :], wqt_l)]:
                R.inc(g and g.dma_start(out=apd, in_=aps_), "sqg", 16)
            R.inc(None, "sqg", 0, mark="consts_b")
            R.wait(g, "sv", f"p0.app.{NDT-1}")
            for i in range(NCT):
                ct = tile_ct(i)
                if i >= 2:
                    R.wait(g, "st", f"p1.mm.{i-2}")
                if is_xbc(i) and i >= 3 and is_xbc(i - 2):
                    R.wait(g, "st", f"p1.conv.{i-2}")
                for half in range(2):
                    R.inc(g and g.dma_start(out=wbuf[2 * (i % 2) + half],
                                            in_=w1_l[ct][:, half * 2048:(half + 1) * 2048]),
                          "sqg", 16, mark=f"p1.wl.{i}.{half}")
                if is_xbc(i):
                    R.inc(g and g.dma_start(out=cwb[i % 2], in_=convw_l[ct - 64]),
                          "sqg", 16, mark=f"p1.cw.{i}")
            R.wait(g, "sv", "p1.tails")
            R.inc(g and g.dma_start(out=tails_bi, in_=tails), "sqg", 16, mark="p1.tout")
            R.wait(g, "sqg", "p1.tout")
            R.inc(g and g.collective_compute(
                "AllGather", ALU.bypass, replica_groups=[[0, 1], [2, 3], [4, 5], [6, 7]],
                ins=[tails_bi.opt()], outs=[tails_bo.opt()]), "sc", 1, mark="p1.cc")
            R.wait(g, "sc", "p1.cc")
            R.inc(g and g.dma_start(out=pad3, in_=tails_bo[0:128, :]), "sqg", 16,
                  mark="p1.pad3")

        def p1_sync(sy):
            for i in range(1, NCT):
                ct = tile_ct(i)
                if is_xbc(i) and ct - 64 < 64:
                    j = ct - 64
                    R.wait(sy, "ss", f"p1.str.{i}")
                    R.inc(sy and sy.dma_start(
                        out=xtok_hbm[0:8, :, j * 128:(j + 1) * 128]
                        .rearrange("c p t -> p c t"),
                        in_=xstrip[i % 2].rearrange("p (c t) -> p c t", t=128)),
                        "sqs", 16, mark=f"p1.xsp.{i}")
                elif is_gate(i):
                    R.wait(sy, "ss", f"p1.ev.{i}")
                    R.inc(sy and sy.dma_start(
                        out=sg_hbm[:, :, ct * 128:(ct + 1) * 128]
                        .rearrange("c p t -> p c t"),
                        in_=hsfm[i % 2].rearrange("p (c t) -> p c t", t=128)),
                        "sqs", 16, mark=f"p1.sgsp.{i}")
            R.wait(sy, "ss", "p1.str0.63")
            R.inc(sy and sy.dma_start(out=xtok_hbm[0][0:3, :], in_=patcht[0:3, :]),
                  "sqs", 16, mark="p1.xsp0")
            R.fence("sqs", "p1.sync_done")

        def p1_tensor(t):
            R.wait(t, "sqg", "consts_b")
            for i in range(81):
                p1_tensor_tile(t, i)
            p1_tensor_prep(t)
            for i in range(81, NCT):
                p1_tensor_tile(t, i)
            for k in range(64):
                R.wait(t, "ss", "p1.patchs")
                if k >= 1:
                    R.wait(t, "ss", f"p1.str0.{k-1}")
                R.inc(t and t.transpose(ps_tr[0:3, (k % 8) * 128:(k % 8 + 1) * 128],
                                        patchh[:, k * 3:(k + 1) * 3], idn_b[:, :]),
                      "st", 1, mark=f"p1.tr0.{k}")

        def p1_tensor_tile(t, i):
            ct = tile_ct(i)
            if i >= 2:
                R.wait(t, evac_sem(i - 2), f"p1.ev.{i-2}")
            if i == 0:
                R.wait(t, "sv", "p0.var")
                R.wait(t, "sv", f"p0.app.{NDT-1}")
            if i == 1:
                R.wait(t, "sv", "p0.var")
            for half in range(2):
                R.wait(t, "sqg", f"p1.wl.{i}.{half}")
                if t:
                    for kt16 in range(16):
                        kt = half * 16 + kt16
                        for th in range(2):
                            t.matmul(ps_mm[2 * (i % 2) + th][:, :],
                                     wbuf[2 * (i % 2) + half][:, kt16 * 128:(kt16 + 1) * 128],
                                     xn_sl(kt, th), start=(kt == 0), stop=(kt == 31))
            R.inc(t and t.last, "st", 1, mark=f"p1.mm.{i}")
            if is_xbc(i):
                j = ct - 64
                R.wait(t, "sv", f"p1.ev.{i}")
                R.wait(t, "sqg", f"p1.cw.{i}")
                if i >= 2:
                    R.wait(t, "ss", f"p1.cve.{i-1}")
                if t:
                    for th in range(2):
                        for k in range(4):
                            t.matmul(ps_cv[th][:, :], cwb[i % 2][:, k * 128:(k + 1) * 128],
                                     xbcr[i % 2][:, k + th * 512: k + th * 512 + 512],
                                     start=(k == 0), stop=(k == 3))
                R.inc(t and t.last, "st", 1, mark=f"p1.conv.{i}")
                if j < 64:
                    R.wait(t, "ss", f"p1.cve.{i}")
                    if i >= 2 and 1 <= i - 1 <= 80 and tile_ct(i - 1) - 64 < 64:
                        R.wait(t, "ss", f"p1.str.{i-1}")
                    if t:
                        for c in range(8):
                            t.transpose(ps_tr[:, c * 128:(c + 1) * 128],
                                        hsfm[i % 2][:, c * 128:(c + 1) * 128], idn_b[:, :])
                    R.inc(t and t.last, "st", 1, mark=f"p1.tr.{i}")

        def p1_vector(v):
            R.wait(v, "sqs", "consts_f")
            if v:
                v.memset(xbcr[0][:, 0:3], 0.0)
                v.memset(xbcr[1][:, 0:3], 0.0)
            R.inc(v and v.last, "sv", 1, mark="p1.padz")
            for i in range(81):
                p1_vector_tile(v, i)
            p1_vector_prep(v)
            R.wait(v, "sqg", "p1.pad3")
            R.inc(v and v.tensor_scalar(pad3, pad3, cmask[:, 0:1], None, ALU.mult), "sv", 1)
            for j in range(80):
                if v:
                    v.tensor_copy(x6[:, 0:3], pad3[:, j * 3:(j + 1) * 3])
                    v.tensor_copy(x6[:, 3:6], first3[:, j * 3:(j + 1) * 3])
                    v.tensor_scalar(patch3[:, j * 3:(j + 1) * 3], x6[:, 0:3],
                                    convwc[:, 4 * j:4 * j + 1], None, ALU.mult)
                    for k in range(1, 4):
                        v.scalar_tensor_tensor(patch3[:, j * 3:(j + 1) * 3], x6[:, k:k + 3],
                                               convwc[:, 4 * j + k:4 * j + k + 1],
                                               patch3[:, j * 3:(j + 1) * 3], ALU.mult, ALU.add)
            R.inc(v and v.last, "sv", 1, mark="p1.patchv")

        def p1_vector_tile(v, i):
            ct = tile_ct(i)
            R.wait(v, "st", f"p1.mm.{i}")
            if i == 0:
                if v:
                    v.tensor_copy(dtr[:, 0:512], ps_mm[0][:, :])
                    v.tensor_copy(dtr[:, 512:1024], ps_mm[1][:, :])
                R.inc(v and v.last, "sv", 1, mark="p1.ev.0")
            elif is_xbc(i):
                j = ct - 64
                R.wait(v, "sv", "p1.padz")
                if i >= 3 and is_xbc(i - 2):
                    R.wait(v, "st", f"p1.conv.{i-2}")
                if v:
                    v.tensor_copy(xbcr[i % 2][:, 3:515], ps_mm[2 * (i % 2)][:, :])
                    v.tensor_copy(xbcr[i % 2][:, 515:1027], ps_mm[2 * (i % 2) + 1][:, :])
                    v.tensor_copy(first3[:, j * 3:(j + 1) * 3], xbcr[i % 2][:, 3:6])
                    v.tensor_copy(tails[:, j * 3:(j + 1) * 3], xbcr[i % 2][:, 1024:1027])
                R.inc(v and v.last, "sv", 1, mark=f"p1.ev.{i}")
                if i == 80:
                    R.inc(None, "sv", 0, mark="p1.tails")

        def p1_scalar(s):
            R.wait(s, "sqs", "consts_f")
            R.inc(s and s.activation(smallsc[:, 0:1], negA[:, 0:1], AF.Exp), "ss", 1,
                  mark="p1.alog")
            for i in range(1, 81):
                p1_scalar_tile(s, i)
            p1_scalar_prep(s)
            for i in range(81, NCT):
                p1_scalar_tile(s, i)
            R.wait(s, "sv", "p1.patchv")
            for j in range(80):
                dst = (patchh[:, j * 3:(j + 1) * 3] if j < 64 else
                       (B_sb[:, (j - 64) * 1024:(j - 64) * 1024 + 3] if j < 72 else
                        C_sb[:, (j - 72) * 1024:(j - 72) * 1024 + 3]))
                if s:
                    s.activation(dst, patch3[:, j * 3:(j + 1) * 3], AF.Silu,
                                 bias=convb[:, j:j + 1])
            R.inc(s and s.last, "ss", 1, mark="p1.patchs")
            for k in range(64):
                R.wait(s, "st", f"p1.tr0.{k}")
                R.inc(s and s.copy(patcht[0:3, k * 128:(k + 1) * 128],
                                   ps_tr[0:3, (k % 8) * 128:(k % 8 + 1) * 128]),
                      "ss", 1, mark=f"p1.str0.{k}")

        def p1_scalar_tile(s, i):
            ct = tile_ct(i)
            if is_gate(i):
                R.wait(s, "st", f"p1.mm.{i}")
                if i in (81, 82):
                    R.wait(s, "st", f"p1.tr.{i-18}")
                if i >= 83:
                    R.wait(s, "sqs", f"p1.sgsp.{i-2}")
                if s:
                    s.activation(hsfm[i % 2][:, 0:512], ps_mm[2 * (i % 2)][:, :], AF.Silu)
                    s.activation(hsfm[i % 2][:, 512:1024], ps_mm[2 * (i % 2) + 1][:, :],
                                 AF.Silu)
                R.inc(s and s.last, "ss", 1, mark=f"p1.ev.{i}")
                return
            j = ct - 64
            R.wait(s, "st", f"p1.conv.{i}")
            if j < 64:
                if i >= 3 and tile_ct(i - 2) - 64 < 64:
                    R.wait(s, "st", f"p1.tr.{i-2}")
                dst0, dst1 = hsfm[i % 2][:, 0:512], hsfm[i % 2][:, 512:1024]
            elif j < 72:
                dst0 = B_sb[:, (j - 64) * 1024:(j - 64) * 1024 + 512]
                dst1 = B_sb[:, (j - 64) * 1024 + 512:(j - 63) * 1024]
            else:
                dst0 = C_sb[:, (j - 72) * 1024:(j - 72) * 1024 + 512]
                dst1 = C_sb[:, (j - 72) * 1024 + 512:(j - 71) * 1024]
            if s:
                s.activation(dst0, ps_cv[0][:, :], AF.Silu, bias=convb[:, j:j + 1])
                s.activation(dst1, ps_cv[1][:, :], AF.Silu, bias=convb[:, j:j + 1])
            R.inc(s and s.last, "ss", 1, mark=f"p1.cve.{i}")
            if j < 64:
                R.wait(s, "st", f"p1.tr.{i}")
                if i >= 3 and tile_ct(i - 2) - 64 < 64:
                    R.wait(s, "sqs", f"p1.xsp.{i-2}")
                if s:
                    for c in range(8):
                        s.copy(xstrip[i % 2][:, c * 128:(c + 1) * 128],
                               ps_tr[:, c * 128:(c + 1) * 128])
                R.inc(s and s.last, "ss", 1, mark=f"p1.str.{i}")

        def p1_scalar_prep(s):
            R.wait(s, "sv", "p1.ev.0")
            if s:
                s.activation(dt_t, dtr, AF.Abs, bias=dtbias[:, 0:1])
                s.activation(tmpA, dt_t, AF.Exp, scale=-1.0)
                s.activation(dt_t, tmpA, AF.Ln, bias=constc[:, 1:2])
                s.activation(tmpA, dtr, AF.Relu, bias=dtbias[:, 0:1])
            R.inc(s and s.last, "ss", 1, mark="p1.sp1")
            R.wait(s, "sv", "p1.dt")
            R.inc(s and s.activation(tmpA, dt_t, AF.Ln), "ss", 1, mark="p1.lndt")
            R.wait(s, "sv", "p1.ncmb")
            R.inc(s and s.activation(dt_t, cum, AF.Exp), "ss", 1, mark="p1.expc")
            for c in range(NCH):
                R.inc(s and s.activation(tmpA[:, c * 128:(c + 1) * 128],
                                         ncmb[:, c * 128:(c + 1) * 128], AF.Exp,
                                         bias=cum[:, c * 128 + 127:(c + 1) * 128]),
                      "ss", 1, mark=f"p1.w.{c}")
            for c in range(NCH):
                R.wait(s, "st", f"p1.trn.{c}")
                R.inc(s and s.copy(ncmbT[:, c * 128:(c + 1) * 128],
                                   ps_x[:, (c % 2) * 128:(c % 2 + 1) * 128]),
                      "ss", 1, mark=f"p1.ncmbTe.{c}")
            for c in range(NCH):
                R.wait(s, "st", f"p1.trw.{c}")
                R.inc(s and s.copy(wTT[:, c * 128:(c + 1) * 128],
                                   ps_x[:, (c % 2) * 128:(c % 2 + 1) * 128]),
                      "ss", 1, mark=f"p1.wTe.{c}")
            for c in range(NCH):
                R.wait(s, "st", f"p1.trc.{c}")
                R.inc(s and s.copy(rowbuf[0:1, c * 128:(c + 1) * 128], ps_x[0:1, 256:384]),
                      "ss", 1, mark=f"p1.cdr.{c}")
            R.wait(s, "st", "p1.trd")
            R.inc(s and s.copy(mrow[0:1, 0:128], ps_x[0:1, 256:384]), "ss", 1, mark="p1.drow")

        def p1_vector_prep(v):
            R.wait(v, "ss", "p1.sp1")
            R.inc(v and v.tensor_tensor(dt_t, dt_t, tmpA, ALU.add), "sv", 1, mark="p1.dt")
            R.wait(v, "ss", "p1.alog")
            R.inc(v and v.tensor_scalar(negA[:, 0:1], smallsc[:, 0:1], -1.0, None, ALU.mult),
                  "sv", 1, mark="p1.negA")
            R.wait(v, "ss", "p1.lndt")
            if v:
                v.tensor_scalar(dtr, dt_t, negA[:, 0:1], None, ALU.mult)
                for c in range(NCH):
                    v.tensor_tensor_scan(cum[:, c * 128:(c + 1) * 128],
                                         dtr[:, c * 128:(c + 1) * 128],
                                         dtr[:, c * 128:(c + 1) * 128],
                                         0.0, ALU.add, ALU.bypass)
            R.inc(v and v.last, "sv", 1, mark="p1.cum")
            R.inc(v and v.tensor_tensor(ncmb, tmpA, cum, ALU.subtract), "sv", 1,
                  mark="p1.ncmb")
            R.wait(v, "ss", "p1.expc")
            R.inc(v and v.tensor_copy(expcum_b[:, :], dt_t), "sv", 1, mark="p1.expcb")
            if v:
                v.memset(rowbuf2[0:1, 0:128], 1.0)
            for c in range(1, NCH):
                R.wait(v, "ss", f"p1.cdr.{c-1}")
                if v:
                    v.tensor_tensor(rowbuf2[0:1, c * 128:(c + 1) * 128],
                                    rowbuf2[0:1, (c - 1) * 128: c * 128],
                                    rowbuf[0:1, (c - 1) * 128: c * 128], ALU.mult)
            R.inc(v and v.last, "sv", 1, mark="p1.dplrows")
            for c in range(NCH):
                R.wait(v, "st", f"p1.bcc.{c}")
                R.inc(v and v.tensor_copy(cdecb[:, c * 128:(c + 1) * 128],
                                          ps_x[:, 128 * (2 + c % 2): 128 * (3 + c % 2)]),
                      "sv", 1, mark=f"p1.cdecb.{c}")
            for c in range(NCH):
                R.wait(v, "st", f"p1.bcd.{c}")
                R.inc(v and v.tensor_scalar(dplb[:, c * 128:(c + 1) * 128],
                                            ps_x[:, 128 * (2 + c % 2): 128 * (3 + c % 2)],
                                            cmask[:, 0:1], None, ALU.mult),
                      "sv", 1, mark=f"p1.dplb.{c}")
            R.wait(v, "st", "p1.bcD")
            R.inc(v and v.tensor_copy(dtb[:, :], ps_x[:, 128:256]), "sv", 1, mark="p1.dtb")

        def p1_tensor_prep(t):
            for c in range(NCH):
                R.wait(t, "sv", "p1.ncmb")
                if c >= 1:
                    R.wait(t, "ss", f"p1.ncmbTe.{c-1}")
                R.inc(t and t.transpose(ps_x[:, (c % 2) * 128:(c % 2 + 1) * 128],
                                        ncmb[:, c * 128:(c + 1) * 128], idn_f[:, :]),
                      "st", 1, mark=f"p1.trn.{c}")
            for c in range(NCH):
                R.wait(t, "ss", f"p1.w.{c}")
                if c >= 1:
                    R.wait(t, "ss", f"p1.wTe.{c-1}")
                else:
                    R.wait(t, "ss", f"p1.ncmbTe.{NCH-1}")
                R.inc(t and t.transpose(ps_x[:, (c % 2) * 128:(c % 2 + 1) * 128],
                                        tmpA[:, c * 128:(c + 1) * 128], idn_f[:, :]),
                      "st", 1, mark=f"p1.trw.{c}")
            for c in range(NCH):
                R.wait(t, "ss", "p1.expc")
                if c >= 1:
                    R.wait(t, "ss", f"p1.cdr.{c-1}")
                R.inc(t and t.transpose(ps_x[0:1, 256:384],
                                        dt_t[:, c * 128 + 127:(c + 1) * 128], idn_f[:, :]),
                      "st", 1, mark=f"p1.trc.{c}")
            R.wait(t, "ss", f"p1.cdr.{NCH-1}")
            R.inc(t and t.transpose(ps_x[0:1, 256:384], dcol[:, 0:1], idn_f[:, :]),
                  "st", 1, mark="p1.trd")
            for c in range(NCH):
                R.wait(t, "ss", f"p1.cdr.{c}")
                if c == 0:
                    R.wait(t, "ss", "p1.drow")
                if c >= 1:
                    R.wait(t, "sv", f"p1.cdecb.{c-1}")
                R.inc(t and t.matmul(ps_x[:, 128 * (2 + c % 2): 128 * (3 + c % 2)],
                                     onesrow_f[0:1, :], rowbuf[0:1, c * 128:(c + 1) * 128],
                                     start=True, stop=True), "st", 1, mark=f"p1.bcc.{c}")
            for c in range(NCH):
                R.wait(t, "sv", "p1.dplrows")
                if c >= 1:
                    R.wait(t, "sv", f"p1.dplb.{c-1}")
                else:
                    R.wait(t, "sv", f"p1.cdecb.{NCH-1}")
                R.inc(t and t.matmul(ps_x[:, 128 * (2 + c % 2): 128 * (3 + c % 2)],
                                     onesrow_f[0:1, :], rowbuf2[0:1, c * 128:(c + 1) * 128],
                                     start=True, stop=True), "st", 1, mark=f"p1.bcd.{c}")
            R.wait(t, "ss", "p1.drow")
            R.wait(t, "sv", f"p1.dplb.{NCH-1}")
            R.inc(t and t.matmul(ps_x[:, 128:256], onesrow_f[0:1, :], mrow[0:1, 0:128],
                                 start=True, stop=True), "st", 1, mark="p1.bcD")

        def p2_tensor(t):
            R.wait(t, "sv", "p1.cum")
            R.wait(t, "sv", "p1.expcb")
            R.wait(t, "ss", f"p1.ev.{NCT-1}")
            R.wait(t, "ss", f"p1.ev.{NCT-2}")
            R.wait(t, "ss", "p1.patchs")
            R.wait(t, "ss", "p1.str0.63")
            for c in range(NCH):
                for w in range(2):
                    if c > 0 or w > 0:
                        pc, pw = (c, w - 1) if w > 0 else (c - 1, 1)
                        R.wait(t, "ss", f"p2.sre.{pc}.{pw}")
                    if c > 0 and w == 0:
                        R.wait(t, "sv", f"p2.scan.{c-1}.127")
                    if t:
                        for g_ in range(4 * w, 4 * w + 4):
                            t.matmul(
                                ps_cv[1][:, (g_ % 4) * 128:(g_ % 4 + 1) * 128],
                                B_sb[:, g_ * 1024 + c * 128: g_ * 1024 + (c + 1) * 128],
                                C_sb[:, g_ * 1024 + c * 128: g_ * 1024 + (c + 1) * 128],
                                start=True, stop=True)
                    R.inc(t and t.last, "st", 1, mark=f"p2.srw.{c}.{w}")
                    if c > 0 or w > 0:
                        pc, pw = (c, w - 1) if w > 0 else (c - 1, 1)
                        R.wait(t, "ss", f"p2.bte.{pc}.{pw}")
                    if t:
                        for g_ in range(4 * w, 4 * w + 4):
                            t.transpose(
                                ps_tr[:, g_ * 128:(g_ + 1) * 128],
                                B_sb[:, g_ * 1024 + c * 128: g_ * 1024 + (c + 1) * 128],
                                idn_b[:, :])
                    R.inc(t and t.last, "st", 1, mark=f"p2.btw.{c}.{w}")
                for j2 in range(128 + LA):
                    if j2 < 128:
                        h = j2
                        if h >= 2:
                            R.wait(t, "ss", f"p2.se.{c}.{h-2}")
                        elif c > 0:
                            R.wait(t, "ss", f"p2.se.{c-1}.127")
                        if t:
                            t.matmul(ps_mm[h % 2][:, 0:128],
                                     idn_f[:, h:h + 1].broadcast_to((128, 128)),
                                     cum[:, c * 128:(c + 1) * 128], start=True, stop=True)
                            t.matmul(ps_mm[h % 2][:, 128:256],
                                     idn_b[:, h:h + 1].broadcast_to((128, 128)),
                                     expcum_b[:, c * 128:(c + 1) * 128],
                                     start=True, stop=True)
                        R.inc(t and t.last, "st", 1, mark=f"p2.bc.{c}.{h}")
                    if j2 >= LA:
                        h = j2 - LA
                        if h == 0:
                            R.wait(t, "sqs", f"p2.xtok.{c}")
                            if c == 0:
                                R.wait(t, "sv", "p2.prev0")
                        R.wait(t, "sv", f"p2.dv.{c}.{h}")
                        p_ = h // 2
                        bank = ps_mm[2 + p_ % 2]
                        reg = bank[(h % 2) * 64:(h % 2) * 64 + 64, 0:128]
                        if h % 2 == 0:
                            if p_ >= 2:
                                R.wait(t, "ss", f"p2.ev.{c}.{p_-2}")
                            elif c > 0:
                                R.wait(t, "ss", f"p2.ev.{c-1}.{62 + p_}")
                        if c > 0:
                            R.wait(t, "sv", f"p2.scan.{c-1}.{h}")
                        if h >= 2:
                            R.wait(t, "sv", f"p2.scan.{c}.{h-2}")
                        elif c > 0:
                            R.wait(t, "sv", f"p2.scan.{c-1}.{126 + h}")
                        if h == 1:
                            R.wait(t, "ss", f"p2.sre.{c}.1")
                        if t:
                            t.matmul(reg, xtok_sb[c % 2][:, h * 64:(h + 1) * 64],
                                     stile[h % 4][:, :], start=True, stop=False)
                            t.matmul(reg, prev[:, h * 64:(h + 1) * 64],
                                     ctile[h % 4][:, :], start=False, stop=True)
                            t.matmul(ps_cv[h % 2][:, 256:320],
                                     bwt_s[h % 4][:, :],
                                     xtok_sb[c % 2][:, h * 64:(h + 1) * 64],
                                     start=True, stop=True)
                        R.inc(t and t.last, "st", 1, mark=f"p2.yg.{c}.{h}")

        def p2_scalar(s):
            for c in range(NCH):
                for w in range(2):
                    R.wait(s, "st", f"p2.srw.{c}.{w}")
                    if s:
                        for g_ in range(4 * w, 4 * w + 4):
                            s.copy(sraw[:, g_ * 128:(g_ + 1) * 128],
                                   ps_cv[1][:, (g_ % 4) * 128:(g_ % 4 + 1) * 128])
                    R.inc(s and s.last, "ss", 1, mark=f"p2.sre.{c}.{w}")
                    R.wait(s, "st", f"p2.btw.{c}.{w}")
                    if s:
                        for g_ in range(4 * w, 4 * w + 4):
                            s.copy(btok[:, g_ * 128:(g_ + 1) * 128],
                                   ps_tr[:, g_ * 128:(g_ + 1) * 128])
                    R.inc(s and s.last, "ss", 1, mark=f"p2.bte.{c}.{w}")
                for j2 in range(128 + LA):
                    if j2 < 128:
                        h = j2
                        R.wait(s, "st", f"p2.bc.{c}.{h}")
                        if h >= 4:
                            R.wait(s, "sv", f"p2.dv.{c}.{h-4}")
                        elif c > 0:
                            R.wait(s, "sv", f"p2.dv.{c-1}.127")
                        if s:
                            s.activation(etile[h % 4][:, :],
                                         ps_mm[h % 2][:, 0:128],
                                         AF.Exp,
                                         bias=ncmbT[:, c * 128 + h: c * 128 + h + 1])
                            s.copy(acb[h % 4][:, :], ps_mm[h % 2][:, 128:256])
                        R.inc(s and s.last, "ss", 1, mark=f"p2.se.{c}.{h}")
                    if j2 >= LA and (j2 - LA) % 2 == 1:
                        p_ = (j2 - LA) // 2
                        R.wait(s, "st", f"p2.yg.{c}.{2 * p_ + 1}")
                        if p_ == 0:
                            R.wait(s, "sqs", "p1.xsp0" if c == 0 else f"p2.ypsp.{c-1}")
                        bank = ps_mm[2 + p_ % 2]
                        R.inc(s and s.copy(ypc[:, p_ * 128:(p_ + 1) * 128],
                                           bank[:, 0:128]),
                              "ss", 1, mark=f"p2.ev.{c}.{p_}")

        def p2_vector(v):
            R.wait(v, "sqg", "consts_b")
            R.wait(v, "ss", f"p1.wTe.{NCH-1}")
            R.wait(v, "st", f"p1.mm.{NCT-1}")
            R.inc(v and v.memset(prev[:, :], 0.0), "sv", 1, mark="p2.prev0")
            for c in range(NCH):
                for j2 in range(128 + LA):
                    if j2 < 128:
                        h = j2
                        gr = h // 16
                        R.wait(v, "ss", f"p2.se.{c}.{h}")
                        R.wait(v, "ss", f"p2.sre.{c}.{h // 64}")
                        R.wait(v, "ss", f"p2.bte.{c}.{h // 64}")
                        if v:
                            v.tensor_tensor(stile[h % 4][:, :], etile[h % 4][:, :],
                                            tribig[:, :], ALU.min)
                            v.tensor_tensor(stile[h % 4][:, :], stile[h % 4][:, :],
                                            sraw[:, gr * 128:(gr + 1) * 128], ALU.mult)
                            v.scalar_tensor_tensor(stile[h % 4][:, :], idn_b[:, :],
                                                   dtb[:, h:h + 1], stile[h % 4][:, :],
                                                   ALU.mult, ALU.add)
                            v.tensor_tensor(ctile[h % 4][:, :],
                                            C_sb[:, gr * 1024 + c * 128:
                                                 gr * 1024 + (c + 1) * 128],
                                            acb[h % 4][:, :], ALU.mult)
                            v.tensor_scalar(bwt_s[h % 4][:, :],
                                            btok[:, gr * 128:(gr + 1) * 128],
                                            wTT[:, c * 128 + h: c * 128 + h + 1],
                                            None, ALU.mult)
                        R.inc(v and v.last, "sv", 1, mark=f"p2.dv.{c}.{h}")
                    if j2 >= LA:
                        h = j2 - LA
                        R.wait(v, "st", f"p2.yg.{c}.{h}")
                        R.inc(v and v.scalar_tensor_tensor(
                            prev[:, h * 64:(h + 1) * 64], prev[:, h * 64:(h + 1) * 64],
                            cdecb[:, c * 128 + h: c * 128 + h + 1],
                            ps_cv[h % 2][:, 256:320],
                            ALU.mult, ALU.add), "sv", 1, mark=f"p2.scan.{c}.{h}")
            R.inc(None, "sv", 0, mark="p2.scan_done")

        def p2_sync(sy):
            for c in range(NCH):
                R.wait(sy, "sqs", "p1.sync_done")
                R.wait(sy, "st", f"p1.mm.{NCT-1}")
                if c >= 2:
                    R.wait(sy, "st", f"p2.yg.{c-2}.127")
                R.inc(sy and sy.dma_start(out=xtok_sb[c % 2][:, :], in_=xtok_hbm[c]),
                      "sqs", 16, mark=f"p2.xtok.{c}")
                R.wait(sy, "ss", f"p2.ev.{c}.63")
                R.inc(sy and sy.dma_start(out=yp_hbm[c], in_=ypc[:, :]), "sqs", 16,
                      mark=f"p2.ypsp.{c}")
            R.fence("sqs", "p2.sync_done")

        def p2_gpsimd(g):
            R.wait(g, "sv", "p2.scan_done")
            R.inc(g and g.dma_start(out=carry_bi, in_=prev[:, :]), "sqg", 16, mark="p2.cout")
            R.wait(g, "sqg", "p2.cout")
            R.inc(g and g.collective_compute(
                "AllGather", ALU.bypass, replica_groups=[[0, 1], [2, 3], [4, 5], [6, 7]],
                ins=[carry_bi.opt()], outs=[carry_bo.opt()]), "sc", 1, mark="p2.cc")
            R.wait(g, "sc", "p2.cc")
            R.inc(g and g.dma_start(out=carry[:, :], in_=carry_bo[0:128, :]), "sqg", 16,
                  mark="p2.cin")

        def p3_tensor(t):
            for hf in range(2):
                for c in range(4 * hf, 4 * hf + 4):
                    bcbanks = [ps_mm[0], ps_mm[1], ps_cv[0], ps_cv[1]]
                    for j2 in range(128 + LA):
                        if j2 < 128:
                            h = j2
                            if h >= 4:
                                R.wait(t, "ss", f"p3.se.{c}.{h-4}")
                            elif c > 0:
                                R.wait(t, "ss", f"p3.se.{c-1}.127")
                            else:
                                R.wait(t, "ss", "p2.se.7.127")
                                R.wait(t, "sv", "p2.scan.7.127")
                                R.wait(t, "ss", "p2.sre.7.1")
                            if c > 0 and h == 0:
                                R.wait(t, "ss", f"p3.rstd.{c-1}")
                            R.inc(t and t.matmul(
                                bcbanks[h % 4][:, 0:128],
                                idn_b[:, h:h + 1].broadcast_to((128, 128)),
                                expcum_b[:, c * 128:(c + 1) * 128],
                                start=True, stop=True), "st", 1, mark=f"p3.bc.{c}.{h}")
                        if j2 >= LA:
                            h = j2 - LA
                            R.wait(t, "sv", f"p3.dv.{c}.{h}")
                            p_ = h // 2
                            bank = ps_mm[2 + p_ % 2]
                            reg = bank[(h % 2) * 64:(h % 2) * 64 + 64, 0:128]
                            if h % 2 == 0:
                                if p_ >= 2:
                                    R.wait(t, "sv", f"p3.pr.{c}.{p_-2}")
                                elif c > 0:
                                    R.wait(t, "sv", f"p3.pr.{c-1}.{62 + p_}")
                                else:
                                    R.wait(t, "ss", "p2.ev.7.63")
                            R.inc(t and t.matmul(reg, bwts3[h % 4][:, 0:64],
                                                 ctile3[h % 4][:, :],
                                                 start=True, stop=True),
                                  "st", 1, mark=f"p3.y2.{c}.{h}")
                    for p_ in range(64):
                        R.wait(t, "ss", f"p3.sq.{c}.{p_}")
                        if p_ == 0:
                            R.wait(t, "ss", f"p3.se.{c}.127")
                        R.inc(t and t.matmul(
                            ps_cv[1][:, (c % 2) * 8 + p_ // 8:(c % 2) * 8 + p_ // 8 + 1],
                            g2t[p_ % 2][:, :], onescol_b[:, :],
                            start=(p_ % 8 == 0), stop=(p_ % 8 == 7)),
                            "st", 1, mark=f"p3.vm.{c}.{p_}")
                    R.inc(None, "st", 0, mark=f"p3.var.{c}")
                    for g_ in range(8):
                        R.wait(t, "ss", f"p3.rstd.{c}")
                        if g_ >= 1:
                            R.wait(t, "ss", f"p3.re.{c}.{g_-1}")
                        R.inc(t and t.transpose(ps_x[0:1, 384:512],
                                                rstdT[:, g_:g_ + 1], idn_f[:, :]),
                              "st", 1, mark=f"p3.rt.{c}.{g_}")
                    for g_ in range(8):
                        R.wait(t, "ss", f"p3.re.{c}.7")
                        if g_ >= 1:
                            R.wait(t, "sv", f"p3.gn.{c}.{8 * (g_ - 1) + 7}")
                        R.inc(t and t.matmul(ps_x[:, (g_ % 3) * 128:(g_ % 3 + 1) * 128],
                                             onesrow_f[0:1, :],
                                             rowbuf[0:1, g_ * 128:(g_ + 1) * 128],
                                             start=True, stop=True),
                              "st", 1, mark=f"p3.rb.{c}.{g_}")
                for dm in range(NDT):
                    R.wait(t, "sqs", f"p3.wo.{hf}.{dm}")
                    R.wait(t, "sv", f"p3.gn.{4*hf+3}.63")
                    if dm >= 2:
                        R.wait(t, "sv", f"p3.x2.{hf}.{dm-2}")
                    if t:
                        for kt in range(64):
                            t.matmul(ps_mm[2 + dm % 2][:, :],
                                     wob[dm % 2][:, kt * 128:(kt + 1) * 128],
                                     ghalf[:, kt * 512:(kt + 1) * 512],
                                     start=(kt == 0), stop=(kt == 63))
                    R.inc(t and t.last, "st", 1, mark=f"p3.omm.{hf}.{dm}")
                p4_tensor_half(t, hf)

        def p3_scalar(s):
            for hf in range(2):
                for c in range(4 * hf, 4 * hf + 4):
                    for j2 in range(128 + LA):
                        if j2 < 128:
                            h = j2
                            R.wait(s, "st", f"p3.bc.{c}.{h}")
                            if h >= 4:
                                R.wait(s, "sv", f"p3.dv.{c}.{h-4}")
                            elif c > 0:
                                R.wait(s, "sv", f"p3.dv.{c-1}.127")
                            bcb = [ps_mm[0], ps_mm[1], ps_cv[0], ps_cv[1]][h % 4]
                            R.inc(s and s.copy(acb3[h % 4][:, :], bcb[:, 0:128]),
                                  "ss", 1, mark=f"p3.se.{c}.{h}")
                    for p_ in range(64):
                        R.wait(s, "sv", f"p3.pr.{c}.{p_}")
                        if p_ >= 2:
                            R.wait(s, "st", f"p3.vm.{c}.{p_-2}")
                        R.inc(s and s.activation(
                            g2t[p_ % 2][:, :],
                            ghalf[:, p_ * 512 + (c % 4) * 128:
                                  p_ * 512 + (c % 4 + 1) * 128],
                            AF.Square), "ss", 1, mark=f"p3.sq.{c}.{p_}")
                    R.wait(s, "st", f"p3.var.{c}")
                    if s:
                        s.activation(smallsc[:, 0:8],
                                     ps_cv[1][:, (c % 2) * 8:(c % 2) * 8 + 8],
                                     AF.Ln, scale=1.0 / 1024.0, bias=constc[:, 0:1])
                        s.activation(rstdT[:, :], smallsc[:, 0:8], AF.Exp, scale=-0.5)
                    R.inc(s and s.last, "ss", 1, mark=f"p3.rstd.{c}")
                    for g_ in range(8):
                        R.wait(s, "st", f"p3.rt.{c}.{g_}")
                        R.inc(s and s.copy(rowbuf[0:1, g_ * 128:(g_ + 1) * 128],
                                           ps_x[0:1, 384:512]),
                              "ss", 1, mark=f"p3.re.{c}.{g_}")
                p4_scalar_half(s, hf)

        def p3_vector(v):
            for hf in range(2):
                for c in range(4 * hf, 4 * hf + 4):
                    for j2 in range(128 + LA):
                        if j2 < 128:
                            h = j2
                            gr = h // 16
                            R.wait(v, "ss", f"p3.se.{c}.{h}")
                            if h == 0 and c == 0:
                                R.wait(v, "sqg", "p2.cin")
                            if h >= 4:
                                R.wait(v, "st", f"p3.y2.{c}.{h-4}")
                            elif c > 0:
                                R.wait(v, "st", f"p3.y2.{c-1}.127")
                            if v:
                                v.tensor_tensor(ctile3[h % 4][:, :],
                                                C_sb[:, gr * 1024 + c * 128:
                                                     gr * 1024 + (c + 1) * 128],
                                                acb3[h % 4][:, :], ALU.mult)
                                v.tensor_scalar(bwts3[h % 4][:, 0:64],
                                                carry[:, h * 64:(h + 1) * 64],
                                                dplb[:, c * 128 + h: c * 128 + h + 1],
                                                None, ALU.mult)
                            R.inc(v and v.last, "sv", 1, mark=f"p3.dv.{c}.{h}")
                        if j2 >= LA and (j2 - LA) % 2 == 1:
                            p_ = (j2 - LA) // 2
                            if c == 0 and p_ == 0:
                                R.wait(v, "sqs", "p2.sync_done")
                            if c == 4 and p_ == 0:
                                R.wait(v, "sqs", "p4.out.0.31")
                            R.wait(v, "st", f"p3.y2.{c}.{2 * p_ + 1}")
                            R.wait(v, "sqs", f"p3.yp.{c}")
                            R.wait(v, "sqs", f"p3.sg.{c}")
                            bank = ps_mm[2 + p_ % 2]
                            if v:
                                v.tensor_tensor(ttile[:, :],
                                                ypr[c % 2][:, p_ * 128:(p_ + 1) * 128],
                                                bank[:, 0:128], ALU.add)
                                v.tensor_tensor(
                                    ghalf[:, p_ * 512 + (c % 4) * 128:
                                          p_ * 512 + (c % 4 + 1) * 128],
                                    ttile[:, :], sgr[c % 2][:, p_ * 128:(p_ + 1) * 128],
                                    ALU.mult)
                            R.inc(v and v.last, "sv", 1, mark=f"p3.pr.{c}.{p_}")
                    for p_ in range(64):
                        g_ = p_ // 8
                        R.wait(v, "st", f"p3.rb.{c}.{g_}")
                        sl = ghalf[:, p_ * 512 + (c % 4) * 128: p_ * 512 + (c % 4 + 1) * 128]
                        R.inc(v and v.scalar_tensor_tensor(
                            sl, sl, nw[:, p_:p_ + 1],
                            ps_x[:, (g_ % 3) * 128:(g_ % 3 + 1) * 128],
                            ALU.mult, ALU.mult), "sv", 1, mark=f"p3.gn.{c}.{p_}")
                for dm in range(NDT):
                    R.wait(v, "st", f"p3.omm.{hf}.{dm}")
                    R.wait(v, "sqs", f"p3.xr.{hf}.{dm}")
                    if dm >= 2:
                        R.wait(v, "sqs", f"p3.x2s.{hf}.{dm-2}")
                    R.inc(v and v.tensor_tensor(x2t[dm % 2][:, :], xre[dm % 2][:, :],
                                                ps_mm[2 + dm % 2][:, :], ALU.add),
                          "sv", 1, mark=f"p3.x2.{hf}.{dm}")
                p4_vector_half(v, hf)

        def p3_sync(sy):
            for hf in range(2):
                for c in range(4 * hf, 4 * hf + 4):
                    if c == 0:
                        R.wait(sy, "st", "p2.yg.7.127")
                        R.wait(sy, "sv", "p2.scan.7.127")
                        R.wait(sy, "ss", "p2.ev.7.63")
                    if c == 4:
                        R.wait(sy, "st", f"p3.omm.0.{NDT-1}")
                    if c % 4 >= 2:
                        R.wait(sy, "sv", f"p3.pr.{c-2}.63")
                    R.wait(sy, "sqs", "p1.sync_done")
                    R.wait(sy, "sqs", f"p2.ypsp.{c}")
                    R.inc(sy and sy.dma_start(out=ypr[c % 2][:, :], in_=yp_hbm[c]),
                          "sqs", 16, mark=f"p3.yp.{c}")
                    R.inc(sy and sy.dma_start(out=sgr[c % 2][:, :], in_=sg_hbm[c]),
                          "sqs", 16, mark=f"p3.sg.{c}")
                for dm in range(NDT):
                    R.wait(sy, "sqg", "wo_precast")
                    R.wait(sy, "sv", f"p3.pr.{4*hf+3}.63")
                    if dm >= 2:
                        R.wait(sy, "st", f"p3.omm.{hf}.{dm-2}")
                    R.inc(sy and sy.dma_start(out=wob[dm % 2][:, :], in_=wo_bf[dm]),
                          "sqs", 16, mark=f"p3.wo.{hf}.{dm}")
                    if dm == 0:
                        R.wait(sy, "sqg", "p2.cout")
                    if dm >= 2:
                        R.wait(sy, "sv", f"p3.x2.{hf}.{dm-2}")
                    R.inc(sy and sy.dma_start(out=xre[dm % 2][:, :],
                                              in_=x_fm[dm][:, hf * 512:(hf + 1) * 512]),
                          "sqs", 16, mark=f"p3.xr.{hf}.{dm}")
                    R.wait(sy, "sv", f"p3.x2.{hf}.{dm}")
                    R.inc(sy and sy.dma_start(out=x2_hbm[dm][:, hf * 512:(hf + 1) * 512],
                                              in_=x2t[dm % 2][:, :]),
                          "sqs", 16, mark=f"p3.x2s.{hf}.{dm}")
                p4_sync_half(sy, hf)

        def p4_tensor_half(t, hf):
            for dm in range(NDT):
                R.wait(t, "sv", f"p4.xc.{hf}.{dm}")
                R.wait(t, "ss", f"p4.sq.{hf}.{dm}")
                if dm == 0:
                    R.wait(t, "ss", f"p3.se.{4*hf+3}.127")
                if t:
                    t.matmul(ps_mm[0][0:1, :], onescol_b[:, :],
                             xn2[:, dm * 512:(dm + 1) * 512],
                             start=(dm == 0), stop=(dm == NDT - 1))
                    t.matmul(ps_mm[1][0:1, :], onescol_b[:, :],
                             zgsb[:, (dm % 2) * 512:(dm % 2 + 1) * 512],
                             start=(dm == 0), stop=(dm == NDT - 1))
                R.inc(t and t.last, "st", 1, mark=f"p4.st.{hf}.{dm}")
            for r in range(2):
                R.wait(t, "sv", f"p4.var.{hf}")
                R.wait(t, "sv", f"p3.gn.{4*hf+3}.63")
                if r == 1:
                    R.wait(t, "ss", f"p4.rstd.{hf}")
                    R.wait(t, "sv", f"p4.bce.{hf}.0")
                src = mrow[0:1, 0:512] if r == 0 else rrow[0:1, 0:512]
                R.inc(t and t.matmul(ps_x[:, 0:512], onesrow_f[0:1, :], src,
                                     start=True, stop=True), "st", 1,
                      mark=f"p4.bc.{hf}.{r}")
            for tt in range(4):
                for kt in range(32):
                    R.wait(t, "sv", f"p4.xn2.{hf}.{kt}")
                    if kt == 0 and tt >= 1:
                        R.wait(t, "sv", f"p4.qe.{hf}.{tt-1}")
                    if tt == 0 and kt == 0 and hf == 0:
                        R.wait(t, "sv", "p2.scan.7.127")
                    if t:
                        t.matmul(ps_cv[0][:, tt * 16:(tt + 1) * 16],
                                 xn2[:, kt * 512 + tt * 128: kt * 512 + (tt + 1) * 128],
                                 wqt_all[:, kt * 16:(kt + 1) * 16],
                                 start=(kt == 0), stop=(kt == 31))
                R.inc(t and t.last, "st", 1, mark=f"p4.q.{hf}.{tt}")
                R.wait(t, "sv", f"p4.qe.{hf}.{tt}")
                if tt == 0 and hf == 1:
                    R.wait(t, "ss", "p4.zte.0.3.3")
                if tt == 0 and hf == 0:
                    R.wait(t, "ss", "p2.bte.7.1")
                if tt >= 1:
                    R.wait(t, "ss", f"p4.qte.{hf}.{tt-1}")
                R.inc(t and t.transpose(ps_tr[0:16, 0:128], qsb[:, :], idn_b[:, :]),
                      "st", 1, mark=f"p4.qt.{hf}.{tt}")
                R.wait(t, "ss", f"p4.qte.{hf}.{tt}")
                if tt >= 1:
                    R.wait(t, "sv", f"p4.m8.{hf}.{tt-1}")
                else:
                    R.wait(t, "ss", f"p3.rstd.{4*hf+3}")
                    if hf == 1:
                        R.wait(t, "sv", "p4.m8.0.3")
                R.inc(t and t.matmul(ps_cv[1][:, 448:512], qT[0:16, :], keyst[:, :],
                                     start=True, stop=True), "st", 1,
                      mark=f"p4.sc.{hf}.{tt}")
            for kt in range(32):
                R.wait(t, "sqg", f"p4.aw.{hf}")
                if kt == 0:
                    R.wait(t, "sv", f"p4.var.{hf}")
                    R.wait(t, "sv", f"p3.x2.{hf}.{NDT-1}")
                if t:
                    for tt in range(4):
                        t.matmul(ps_mm[tt][:, :],
                                 xn2[:, kt * 512 + tt * 128: kt * 512 + (tt + 1) * 128],
                                 awt_all[:, kt * 512:(kt + 1) * 512],
                                 start=(kt == 0), stop=(kt == 31))
                R.inc(t and t.last, "st", 1, mark=f"p4.zk.{hf}.{kt}")
            for tt in range(4):
                for kb in range(4):
                    R.wait(t, "sv", f"p4.zg.{hf}.{tt}")
                    sl = 4 * tt + kb
                    if sl >= 1:
                        R.wait(t, "ss", f"p4.zte.{hf}.{(sl - 1) // 4}.{(sl - 1) % 4}")
                    R.inc(t and t.transpose(ps_tr[:, (sl % 8) * 128:(sl % 8 + 1) * 128],
                                            zgsb[:, tt * 512 + kb * 128:
                                                 tt * 512 + (kb + 1) * 128],
                                            idn_b[:, :]), "st", 1,
                          mark=f"p4.zt.{hf}.{tt}.{kb}")
            for dm in range(NDT):
                R.wait(t, "sqg", f"p4.bw.{hf}.{dm}")
                R.wait(t, "ss", f"p4.zte.{hf}.3.3")
                if dm >= 2:
                    R.wait(t, "sv", f"p4.fin.{hf}.{dm-2}")
                if t:
                    for kb in range(4):
                        t.matmul(ps_mm[2 + dm % 2][:, :],
                                 bwtd[dm % 4][:, kb * 128:(kb + 1) * 128],
                                 zgt_half[:, kb * 512:(kb + 1) * 512],
                                 start=(kb == 0), stop=(kb == 3))
                R.inc(t and t.last, "st", 1, mark=f"p4.o.{hf}.{dm}")

        def p4_scalar_half(s, hf):
            for dm in range(NDT):
                R.wait(s, "sv", f"p4.xc.{hf}.{dm}")
                if dm >= 2:
                    R.wait(s, "st", f"p4.st.{hf}.{dm-2}")
                R.inc(s and s.activation(zgsb[:, (dm % 2) * 512:(dm % 2 + 1) * 512],
                                         xn2[:, dm * 512:(dm + 1) * 512], AF.Square),
                      "ss", 1, mark=f"p4.sq.{hf}.{dm}")
            R.wait(s, "sv", f"p4.var.{hf}")
            if s:
                s.activation(rowbuf2[0:1, 0:512], rowbuf[0:1, 0:512], AF.Ln,
                             bias=constc[0:1, 0:1])
                s.activation(rrow[0:1, 0:512], rowbuf2[0:1, 0:512], AF.Exp, scale=-0.5)
            R.inc(s and s.last, "ss", 1, mark=f"p4.rstd.{hf}")
            for tt in range(4):
                R.wait(s, "st", f"p4.qt.{hf}.{tt}")
                R.inc(s and s.copy(qT[0:16, :], ps_tr[0:16, 0:128]), "ss", 1,
                      mark=f"p4.qte.{hf}.{tt}")
                R.wait(s, "sv", f"p4.m8.{hf}.{tt}")
                R.inc(s and s.mul(negt1[:, 0:1], m8a[:, 0:1], -1.0), "ss", 1,
                      mark=f"p4.nt1.{hf}.{tt}")
                R.wait(s, "sv", f"p4.m8b.{hf}.{tt}")
                R.inc(s and s.activation(esb[:, :], ssb[:, :], AF.Exp,
                                         bias=negt1[:, 0:1]), "ss", 1,
                      mark=f"p4.e.{hf}.{tt}")
                R.wait(s, "sv", f"p4.rs.{hf}.{tt}")
                if s:
                    s.activation(smallsc[:, 1:2], rs1[:, 0:1], AF.Ln)
                    s.activation(rs1[:, 0:1], smallsc[:, 1:2], AF.Exp, scale=-1.0)
                R.inc(s and s.last, "ss", 1, mark=f"p4.rr.{hf}.{tt}")
            for tt in range(4):
                for kb in range(4):
                    R.wait(s, "st", f"p4.zt.{hf}.{tt}.{kb}")
                    sl = 4 * tt + kb
                    R.inc(s and s.copy(zgt_half[:, kb * 512 + tt * 128:
                                                kb * 512 + (tt + 1) * 128],
                                       ps_tr[:, (sl % 8) * 128:(sl % 8 + 1) * 128]),
                          "ss", 1, mark=f"p4.zte.{hf}.{tt}.{kb}")

        def p4_vector_half(v, hf):
            for dm in range(NDT):
                R.wait(v, "sqs", f"p4.x.{hf}.{dm}")
                if dm == 0:
                    R.wait(v, "st", f"p3.omm.{hf}.{NDT-1}")
                R.inc(v and v.tensor_copy(xn2[:, dm * 512:(dm + 1) * 512],
                                          x2l[dm % 2][:, :]),
                      "sv", 1, mark=f"p4.xc.{hf}.{dm}")
            R.wait(v, "st", f"p4.st.{hf}.{NDT-1}")
            if v:
                v.tensor_scalar(mrow[0:1, 0:512], ps_mm[0][0:1, :], 1.0 / DM, None, ALU.mult)
                v.tensor_scalar(rowbuf[0:1, 0:512], ps_mm[1][0:1, :], 1.0 / DM, None,
                                ALU.mult)
                v.tensor_tensor(rowbuf2[0:1, 0:512], mrow[0:1, 0:512], mrow[0:1, 0:512],
                                ALU.mult)
                v.tensor_tensor(rowbuf[0:1, 0:512], rowbuf[0:1, 0:512],
                                rowbuf2[0:1, 0:512], ALU.subtract)
            R.inc(v and v.last, "sv", 1, mark=f"p4.var.{hf}")
            for r, dst in enumerate([mb2[:, :], rb2[:, :]]):
                R.wait(v, "st", f"p4.bc.{hf}.{r}")
                R.inc(v and v.tensor_copy(dst, ps_x[:, 0:512]), "sv", 1,
                      mark=f"p4.bce.{hf}.{r}")
            for dm in range(NDT):
                if v:
                    sl = xn2[:, dm * 512:(dm + 1) * 512]
                    v.tensor_tensor(sl, sl, mb2[:, :], ALU.subtract)
                    v.tensor_tensor(sl, sl, rb2[:, :], ALU.mult)
                R.inc(v and v.last, "sv", 1, mark=f"p4.xn2.{hf}.{dm}")
            for tt in range(4):
                R.wait(v, "st", f"p4.q.{hf}.{tt}")
                R.inc(v and v.tensor_copy(qsb[:, :], ps_cv[0][:, tt * 16:(tt + 1) * 16]),
                      "sv", 1, mark=f"p4.qe.{hf}.{tt}")
                R.wait(v, "st", f"p4.sc.{hf}.{tt}")
                if v:
                    v.tensor_copy(ssb[:, :], ps_cv[1][:, 448:512])
                    v.max(m8a[:, :], ssb[:, :])
                R.inc(v and v.last, "sv", 1, mark=f"p4.m8.{hf}.{tt}")
                R.wait(v, "ss", f"p4.nt1.{hf}.{tt}")
                if v:
                    v.tensor_scalar(msk[:, :], ssb[:, :], m8a[:, 7:8], None, ALU.is_ge)
                    v.scalar_tensor_tensor(esb[:, :], msk[:, :], -1e38, ssb[:, :],
                                           ALU.mult, ALU.add)
                    v.max(m8b[:, :], esb[:, :])
                R.inc(v and v.last, "sv", 1, mark=f"p4.m8b.{hf}.{tt}")
                R.wait(v, "ss", f"p4.e.{hf}.{tt}")
                if v:
                    v.tensor_scalar(msk[:, :], ssb[:, :], m8b[:, 7:8], None, ALU.is_ge)
                    v.tensor_tensor(esb[:, :], esb[:, :], msk[:, :], ALU.mult)
                    v.tensor_reduce(rs1[:, 0:1], esb[:, :], AX.X, ALU.add)
                R.inc(v and v.last, "sv", 1, mark=f"p4.rs.{hf}.{tt}")
                R.wait(v, "ss", f"p4.rr.{hf}.{tt}")
                R.inc(v and v.tensor_scalar(gsb[:, tt * 64:(tt + 1) * 64], esb[:, :],
                                            rs1[:, 0:1], None,
                                            ALU.mult), "sv", 1, mark=f"p4.g.{hf}.{tt}")
            for tt in range(4):
                R.wait(v, "st", f"p4.zk.{hf}.{31}")
                R.wait(v, "sv", f"p4.g.{hf}.{tt}")
                zv = ps_mm[tt][:, :].rearrange("p (e r) -> p e r", r=8)
                gv = gsb[:, tt * 64:(tt + 1) * 64].unsqueeze(2).broadcast_to((128, 64, 8))
                R.inc(v and v.tensor_tensor(
                    zgsb[:, tt * 512:(tt + 1) * 512].rearrange("p (e r) -> p e r", r=8),
                    zv, gv, ALU.mult), "sv", 1, mark=f"p4.zg.{hf}.{tt}")
            for dm in range(NDT):
                R.wait(v, "st", f"p4.o.{hf}.{dm}")
                R.wait(v, "sqs", f"p4.x2b.{hf}.{dm}")
                if dm >= 2:
                    R.wait(v, "sqs", f"p4.out.{hf}.{dm-2}")
                R.inc(v and v.scalar_tensor_tensor(
                    outbuf[dm % 2][:, :], ps_mm[2 + dm % 2][:, :], 2.0,
                    x2l[dm % 2][:, :], ALU.mult, ALU.add), "sv", 1,
                    mark=f"p4.fin.{hf}.{dm}")

        def p4_sync_half(sy, hf):
            for dm in range(NDT):
                R.wait(sy, "sqs", f"p3.x2s.{hf}.{dm}")
                if dm >= 2:
                    R.wait(sy, "sv", f"p4.xc.{hf}.{dm-2}")
                R.inc(sy and sy.dma_start(out=x2l[dm % 2][:, :],
                                          in_=x2_hbm[dm][:, hf * 512:(hf + 1) * 512]),
                      "sqs", 16, mark=f"p4.x.{hf}.{dm}")
            for dm in range(NDT):
                if dm < 2:
                    R.wait(sy, "sv", f"p4.xn2.{hf}.{NDT-1}")
                else:
                    R.wait(sy, "sv", f"p4.fin.{hf}.{dm-2}")
                R.inc(sy and sy.dma_start(out=x2l[dm % 2][:, :],
                                          in_=x2_hbm[dm][:, hf * 512:(hf + 1) * 512]),
                      "sqs", 16, mark=f"p4.x2b.{hf}.{dm}")
                R.wait(sy, "sv", f"p4.fin.{hf}.{dm}")
                R.inc(sy and sy.dma_start(out=out_fm[dm][:, hf * 512:(hf + 1) * 512],
                                          in_=outbuf[dm % 2][:, :]),
                      "sqs", 16, mark=f"p4.out.{hf}.{dm}")

        def p4_gpsimd(g):
            for hf in range(2):
                R.wait(g, "st", f"p3.omm.{hf}.{NDT-1}")
                R.inc(g and g.dma_start(
                    out=awt_all[:, :].rearrange("p (k f) -> p k f", f=512),
                    in_=awt_l.rearrange("k p f -> p k f")),
                    "sqg", 16, mark=f"p4.aw.{hf}")
                for dm in range(NDT):
                    if dm >= 4:
                        R.wait(g, "st", f"p4.o.{hf}.{dm-4}")
                    R.inc(g and g.dma_start(out=bwtd[dm % 4][:, :], in_=bwt_l[dm]),
                          "sqg", 16, mark=f"p4.bw.{hf}.{dm}")

        def g_body(g):
            p1_gpsimd(g)
            if STAGE != "p1":
                p2_gpsimd(g)
                p4_gpsimd(g)

        def sy_body(sy):
            p0_sync(sy)
            p1_sync(sy)
            if STAGE != "p1":
                p2_sync(sy)
                p3_sync(sy)
            R.wait_all(sy, "sqs")
            R.wait_all(sy, "sqg")

        def t_body(t):
            p0_tensor(t)
            p1_tensor(t)
            if STAGE != "p1":
                p2_tensor(t)
                p3_tensor(t)

        def v_body(v):
            p0_vector(v)
            p1_vector(v)
            if STAGE != "p1":
                p2_vector(v)
                p3_vector(v)

        def s_body(s):
            p0_scalar(s)
            p1_scalar(s)
            if STAGE != "p1":
                p2_scalar(s)
                p3_scalar(s)

        R.reset_pass(True)
        for f in (g_body, sy_body, t_body, v_body, s_body):
            f(None)
        R.reset_pass(False)
        R.log = []
        global LAST_R
        LAST_R = R
        with nc.Block() as block:
            for nm, fb in (("g", g_body), ("sy", sy_body), ("t", t_body),
                           ("v", v_body), ("s", s_body)):
                R.stream = nm
                qn = {"g": "sqg", "sy": "sqs"}.get(nm)
                getattr(block, {"g": "gpsimd", "sy": "sync", "t": "tensor",
                                "v": "vector", "s": "scalar"}[nm])(
                    lambda e, fb=fb, qn=qn: fb(EngW(e, R, qn)))
    return nc


LAST_R = None


# version 09
# speedup vs baseline: 1.2888x; 1.1269x over previous
import contextlib
import numpy as np
import ml_dtypes
import concourse.bass as bass
import concourse.mybir as mybir
from concourse.bass_utils import run_bass_kernel_spmd

F32 = mybir.dt.float32
BF16 = mybir.dt.bfloat16
U8 = mybir.dt.uint8
AF = mybir.ActivationFunctionType
ALU = mybir.AluOpType
AX = mybir.AxisListType
BF = ml_dtypes.bfloat16

T = 1024
CS = 128
NCH = 8
DM = 4096
NDT = 32
H, P, N, G = 128, 64, 128, 8
INTER = 8192
NIT = 64
CONV = 10240
PROJ = 18560
NCT = 145
EPS = 1e-5
CT_DT = 144
CT_XBC = list(range(64, 144))
CT_GATE = list(range(0, 64))
CT_ORDER = [CT_DT] + CT_XBC + CT_GATE
LA = 2
NSLOT = 4

STAGE = "full"
DEBUG_TAPS = ()


class Rec:
    QDS = {"sqs": 16, "sqg": 8}

    def __init__(self):
        self.tbl = {}
        self.dry = True
        self.sems = {}
        self.qcnt = {}
        self.log = None
        self.stream = None

    def sem(self, name):
        if name in ("sqs", "sqg"):
            self.qcnt[name] = 0
            for i in range(self.QDS[name]):
                self.sems.setdefault(f"{name}{i}", [None, 0])
        else:
            self.sems.setdefault(name, [None, 0])

    def reset_pass(self, dry):
        self.dry = dry
        for v in self.sems.values():
            v[1] = 0
        for k in self.qcnt:
            self.qcnt[k] = 0

    def bind(self, name, handle):
        self.sems[name][0] = handle

    def inc(self, instr, name, n=1, mark=None):
        if name in ("sqs", "sqg"):
            assert n in (0, 16)
            if n == 0:
                return self.fence(name, mark)
            actual = f"{name}{self.qcnt[name] % self.QDS[name]}"
            self.qcnt[name] += 1
        else:
            actual = name
        s = self.sems[actual]
        s[1] += n
        if mark is not None:
            key = mark
            val = (actual, s[1])
            if self.dry:
                assert key not in self.tbl, f"dup mark {key}"
                self.tbl[key] = val
            else:
                assert self.tbl.get(key) == val, \
                    f"mark drift {key}: {self.tbl.get(key)} vs {val}"
        if instr is not None:
            instr.then_inc(s[0], n)
        if self.log is not None and not self.dry and n > 0:
            self.log.append((self.stream, "i", actual, n, mark))
        return s[1]

    def fence(self, qname, mark):
        snap = tuple((f"{qname}{i}", self.sems[f"{qname}{i}"][1])
                     for i in range(self.QDS[qname]))
        if self.dry:
            assert mark not in self.tbl, f"dup mark {mark}"
            self.tbl[mark] = ("F", snap)
        else:
            assert self.tbl.get(mark) == ("F", snap), f"fence drift {mark}"
        return None

    def wait(self, eng, name, mark, extra=0):
        if eng is None:
            return
        if mark not in self.tbl:
            if self.dry:
                return
            raise KeyError(f"wait on unknown mark {mark}")
        ent = self.tbl[mark]
        if ent[0] == "F":
            for actual, v in ent[1]:
                if v > 0:
                    eng.wait_ge(self.sems[actual][0], v)
                    if self.log is not None and not self.dry:
                        self.log.append((self.stream, "w", actual, v, mark))
            return
        actual, v = ent
        v += extra
        if v > 0:
            eng.wait_ge(self.sems[actual][0], v)
            if self.log is not None and not self.dry:
                self.log.append((self.stream, "w", actual, v, mark))

    def wait_all(self, eng, qname):
        if eng is None:
            return
        for i in range(self.QDS[qname]):
            actual = f"{qname}{i}"
            v = self.sems[actual][1]
            if v > 0:
                eng.wait_ge(self.sems[actual][0], v)
                if self.log is not None and not self.dry:
                    self.log.append((self.stream, "w", actual, v, None))


class EngW:
    def __init__(self, e, R=None, q=None):
        self._e = e
        self._R = R
        self._q = q
        self.last = None

    def __getattr__(self, k):
        f = getattr(self._e, k)
        if not callable(f):
            return f
        if k == "dma_start" and self._q is not None:
            def g(*a, **kw):
                R, q = self._R, self._q
                actual = f"{q}{R.qcnt[q] % R.QDS[q]}"
                prev = R.sems[actual][1]
                if prev > 0:
                    self._e.wait_ge(R.sems[actual][0], prev)
                    if R.log is not None and not R.dry:
                        R.log.append((R.stream, "w", actual, prev, "thr"))
                r = f(*a, **kw)
                self.last = r
                return r
            return g
        def g(*a, **kw):
            r = f(*a, **kw)
            self.last = r
            return r
        return g


def _host_inputs(core, x_seq_half, W1, conv_w, conv_b, dt_bias, A_log, D, norm_w,
                 W_o, A_w, B_w, Wq, keys):
    ins = {}
    ins["x_fm"] = np.ascontiguousarray(
        x_seq_half.T.reshape(NDT, 128, T))
    ins["w1_l"] = np.ascontiguousarray(
        W1.reshape(NCT, 128, NDT, 128).transpose(0, 3, 2, 1).reshape(NCT, 128, NDT * 128))
    cw = conv_w[:, 0, :]
    cwd = np.zeros((80, 128, 4, 128), np.float32)
    idx = np.arange(128)
    for k in range(4):
        cwd[:, idx, k, idx] = cw.reshape(80, 128, 4)[:, :, k]
    ins["convw_l"] = np.ascontiguousarray(cwd.reshape(80, 128, 4 * 128))
    ins["convw_cols"] = np.ascontiguousarray(
        cw.reshape(80, 128, 4).transpose(1, 0, 2).reshape(128, 320))
    ins["convb_l"] = np.ascontiguousarray(conv_b.reshape(80, 128).T)
    ins["dtbias_c"] = np.ascontiguousarray(dt_bias.reshape(128, 1))
    ins["alog_c"] = np.ascontiguousarray(A_log.reshape(128, 1))
    ins["d_c"] = np.ascontiguousarray(D.reshape(128, 1))
    ins["nw_l"] = np.ascontiguousarray(norm_w.reshape(NIT, 128).T)
    ins["cmask_c"] = np.full((128, 1), float(core & 1), np.float32)
    ins["wo_l"] = np.ascontiguousarray(
        W_o.reshape(32, 128, 64, 128).transpose(0, 3, 2, 1).reshape(32, 128, 64 * 128))
    ins["awt_l"] = np.ascontiguousarray(A_w.T.reshape(32, 128, 512))
    ins["bwt_l"] = np.ascontiguousarray(
        B_w.reshape(32, 128, 4, 128).transpose(0, 3, 2, 1).reshape(32, 128, 512))
    ins["wqt_l"] = np.ascontiguousarray(
        Wq.T.reshape(32, 128, 16).transpose(1, 0, 2).reshape(128, 512))
    ins["keyst_l"] = np.ascontiguousarray(keys.T)
    ins["idn_f_l"] = np.eye(128, dtype=np.float32)
    ins["tribig_l"] = np.where(np.triu(np.ones((128, 128), bool)), 1e30, 0.0).astype(np.float32)
    ins["onesrow_l"] = np.ones((1, 128), np.float32)
    ins["onescol_l"] = np.ones((128, 1), np.float32)
    return ins


def kernel(L0_attn_input_states, L0_attn_in_proj_w, L0_attn_conv_w, L0_attn_conv_b,
           L0_attn_dt_bias, L0_attn_A_log, L0_attn_D, L0_attn_norm_w, L0_attn_out_w,
           L0_ffn_A_w, L0_ffn_B_w, L0_ffn_Wq_w, L0_ffn_keys):
    x = np.ascontiguousarray(np.asarray(L0_attn_input_states, np.float32))
    args = (np.asarray(L0_attn_in_proj_w, np.float32),
            np.asarray(L0_attn_conv_w, np.float32),
            np.asarray(L0_attn_conv_b, np.float32),
            np.asarray(L0_attn_dt_bias, np.float32),
            np.asarray(L0_attn_A_log, np.float32),
            np.asarray(L0_attn_D, np.float32),
            np.asarray(L0_attn_norm_w, np.float32),
            np.asarray(L0_attn_out_w, np.float32),
            np.asarray(L0_ffn_A_w, np.float32),
            np.asarray(L0_ffn_B_w, np.float32),
            np.asarray(L0_ffn_Wq_w, np.float32),
            np.asarray(L0_ffn_keys, np.float32))
    nc = build_program()
    in_maps = []
    for core in range(8):
        s, hf = core >> 1, core & 1
        in_maps.append(_host_inputs(core, x[s, hf * T:(hf + 1) * T, :], *args))
    res = run_bass_kernel_spmd(nc, in_maps, core_ids=list(range(8)))
    out = np.empty((4, 2048, 4096), np.float32)
    for core in range(8):
        s, hf = core >> 1, core & 1
        o = np.asarray(res.results[core]["out_fm"])
        out[s, hf * T:(hf + 1) * T, :] = o.transpose(2, 0, 1).reshape(T, DM)
    return out


KB = 1024


def build_program():
    nc = bass.Bass("TRN2", target_bir_lowering=False)

    x_fm = nc.dram_tensor("x_fm", [NDT, 128, T], F32, kind="ExternalInput").ap()
    w1_l = nc.dram_tensor("w1_l", [NCT, 128, NDT * 128], F32, kind="ExternalInput").ap()
    convw_l = nc.dram_tensor("convw_l", [80, 128, 512], F32, kind="ExternalInput").ap()
    convw_cols = nc.dram_tensor("convw_cols", [128, 320], F32, kind="ExternalInput").ap()
    convb_l = nc.dram_tensor("convb_l", [128, 80], F32, kind="ExternalInput").ap()
    dtbias_c = nc.dram_tensor("dtbias_c", [128, 1], F32, kind="ExternalInput").ap()
    alog_c = nc.dram_tensor("alog_c", [128, 1], F32, kind="ExternalInput").ap()
    d_c = nc.dram_tensor("d_c", [128, 1], F32, kind="ExternalInput").ap()
    nw_l = nc.dram_tensor("nw_l", [128, NIT], F32, kind="ExternalInput").ap()
    cmask_c = nc.dram_tensor("cmask_c", [128, 1], F32, kind="ExternalInput").ap()
    wo_l = nc.dram_tensor("wo_l", [32, 128, 8192], F32, kind="ExternalInput").ap()
    awt_l = nc.dram_tensor("awt_l", [32, 128, 512], F32, kind="ExternalInput").ap()
    bwt_l = nc.dram_tensor("bwt_l", [32, 128, 512], F32, kind="ExternalInput").ap()
    wqt_l = nc.dram_tensor("wqt_l", [128, 512], F32, kind="ExternalInput").ap()
    keyst_l = nc.dram_tensor("keyst_l", [16, 64], F32, kind="ExternalInput").ap()
    idn_f_l = nc.dram_tensor("idn_f_l", [128, 128], F32, kind="ExternalInput").ap()
    tribig_l = nc.dram_tensor("tribig_l", [128, 128], F32, kind="ExternalInput").ap()
    onesrow_l = nc.dram_tensor("onesrow_l", [1, 128], F32, kind="ExternalInput").ap()
    onescol_l = nc.dram_tensor("onescol_l", [128, 1], F32, kind="ExternalInput").ap()

    out_fm = nc.dram_tensor("out_fm", [NDT, 128, T], F32, kind="ExternalOutput").ap()

    xtok_hbm = nc.dram_tensor("xtok_hbm", [NCH, 128, INTER], BF16).ap()
    sg_hbm = nc.dram_tensor("sg_hbm", [NCH, 128, INTER], BF16).ap()
    yp_hbm = nc.dram_tensor("yp_hbm", [NCH, 128, INTER], BF16).ap()
    x2_hbm = nc.dram_tensor("x2_hbm", [NDT, 128, T], F32).ap()
    wo_bf = nc.dram_tensor("wo_bf", [32, 128, 8192], BF16).ap()
    tails_bi = nc.dram_tensor("tails_bi", [128, 240], BF16).ap()
    tails_bo = nc.dram_tensor("tails_bo", [256, 240], BF16).ap()
    carry_bi = nc.dram_tensor("carry_bi", [128, INTER], BF16).ap()
    carry_bo = nc.dram_tensor("carry_bo", [256, INTER], BF16).ap()

    taps = {}
    def mktap(name, shape, dtype=F32):
        if name in DEBUG_TAPS:
            taps[name] = nc.dram_tensor("tap_" + name, shape, dtype,
                                        kind="ExternalOutput").ap()
        return taps.get(name)

    R = Rec()
    for s_ in ("sqg", "sqs", "st", "sv", "ss", "sc"):
        R.sem(s_)

    with contextlib.ExitStack() as ST0:
        def sbt(name, shape, dtype):
            return ST0.enter_context(nc.sbuf_tensor(name, shape, dtype))
        def psb(name, shape, dtype=F32):
            return ST0.enter_context(nc.psum_tensor(name, shape, dtype))

        C_sb = sbt("C_sb", [128, 8 * 1024], BF16)
        expcum_b = sbt("expcum_b", [128, T], BF16)
        dplb = sbt("dplb", [128, T], F32)
        dtb = sbt("dtb", [128, 128], F32)
        idn_f = sbt("idn_f", [128, 128], F32)
        idn_b = sbt("idn_b", [128, 128], BF16)
        tribig = sbt("tribig", [128, 128], BF16)
        onesrow_f = sbt("onesrow_f", [1, 128], F32)
        onescol_b = sbt("onescol_b", [128, 1], BF16)
        dtbias = sbt("dtbias", [128, 1], F32)
        negA = sbt("negA", [128, 1], F32)
        dcol = sbt("dcol", [128, 1], F32)
        cmask = sbt("cmask", [128, 1], F32)
        nw = sbt("nw", [128, NIT], F32)
        smallsc = sbt("smallsc", [128, 8], F32)
        rowbuf = sbt("rowbuf", [1, 1024], F32)
        rowbuf2 = sbt("rowbuf2", [1, 1024], F32)
        mrow = sbt("mrow", [1, T], F32)
        rrow = sbt("rrow", [1, T], F32)
        keyst = sbt("keyst", [16, 64], BF16)
        wqt_all = sbt("wqt_all", [128, 512], BF16)
        convb = sbt("convb", [128, 80], F32)
        constc = sbt("constc", [128, 2], F32)
        convwc = sbt("convwc", [128, 320], F32)

        arena = sbt("arena", [128, 160 * KB], U8)
        def vw(off_kb, parts, nel, dtype):
            esz = 2 if dtype == BF16 else 4
            off = int(off_kb * KB)
            assert off % 64 == 0 and off + nel * esz <= 160 * KB, (off_kb, nel)
            return arena[0:parts, off: off + nel * esz].bitcast(dtype)

        xn = vw(0, 128, NDT * 1024, BF16)
        xtok_sb = [vw(0, 128, INTER, BF16), vw(16, 128, INTER, BF16)]
        prev = vw(32, 128, INTER, BF16)
        etile = [vw(48 + 0.25 * i, 128, 128, BF16) for i in range(4)]
        stile = [vw(49 + 0.25 * i, 128, 128, BF16) for i in range(4)]
        ctile = [vw(50 + 0.25 * i, 128, 128, BF16) for i in range(4)]
        acb = [vw(51 + 0.25 * i, 128, 128, BF16) for i in range(4)]
        bwt_s = [vw(52 + 0.25 * i, 128, 128, BF16) for i in range(4)]
        sgr = [vw(0, 128, INTER, BF16), vw(16, 128, INTER, BF16)]
        wob = [vw(0, 128, 8192, BF16), vw(16, 128, 8192, BF16)]
        xre = [vw(32 + 2 * i, 128, 512, F32) for i in range(2)]
        x2t = [vw(36 + 2 * i, 128, 512, F32) for i in range(2)]
        rstdT = vw(40, 128, 8, F32)
        g2t = [vw(40.5 + 0.25 * i, 128, 128, BF16) for i in range(2)]
        ttile = vw(41, 128, 128, F32)
        acb3 = [vw(42 + 0.25 * i, 128, 128, BF16) for i in range(4)]
        ctile3 = [vw(43 + 0.25 * i, 128, 128, BF16) for i in range(4)]
        bwts3 = [vw(44 + 0.25 * i, 128, 128, BF16) for i in range(4)]
        carry = vw(48, 128, INTER, BF16)
        xn2 = vw(0, 128, 32 * 512, BF16)
        bwtd = [vw(32 + i, 128, 512, BF16) for i in range(4)]
        x2l = [vw(85 + 2 * i, 128, 512, F32) for i in range(2)]

        xld = [vw(64 + 4 * i, 128, T, F32) for i in range(2)]
        xsq = [vw(72 + 2 * i, 128, T, BF16) for i in range(2)]
        mb = vw(76, 128, T, F32)
        rb = vw(80, 128, T, F32)
        wbuf = [vw(64 + 4 * i, 128, 2048, BF16) for i in range(4)]
        cwb = [vw(80 + i, 128, 512, BF16) for i in range(2)]
        xbcr = [vw(82, 128, 3 + T, BF16), vw(84.0625, 128, 3 + T, BF16)]
        hsfm = [vw(86.25, 128, T, BF16), vw(88.25, 128, T, BF16)]
        xstrip = [vw(90.25, 128, T, BF16), vw(92.25, 128, T, BF16)]
        first3 = vw(94.25, 128, 240, BF16)
        tails = vw(94.75, 128, 240, BF16)
        pad3 = vw(95.25, 128, 240, BF16)
        ypr = [vw(80, 128, INTER, BF16), vw(64, 128, INTER, BF16)]
        zgsb = vw(64, 128, 4 * 512, BF16)
        zgt_half = vw(68, 128, 4 * 512, BF16)
        awtb = [vw(72 + i, 128, 512, BF16) for i in range(4)]
        qsb = vw(76, 128, 16, BF16)
        qT = vw(76.0625, 16, 128, BF16)
        ssb = vw(76.375, 128, 64, F32)
        esb = vw(76.625, 128, 64, F32)
        m8a = vw(76.875, 128, 8, F32)
        m8b = vw(76.9375, 128, 8, F32)
        msk = vw(77.25, 128, 64, F32)
        gsb = vw(77.5, 128, 256, BF16)
        negt1 = vw(78, 128, 1, F32)
        rs1 = vw(78.0625, 128, 1, F32)
        outbuf = [vw(100 + 2 * i, 128, 512, F32) for i in range(2)]
        mb2 = vw(96, 128, 512, F32)
        rb2 = vw(98, 128, 512, F32)

        B_sb = vw(96, 128, INTER, BF16)
        sraw = vw(112, 128, 1024, BF16)
        btok = vw(114, 128, 1024, BF16)
        x6 = vw(116, 128, 6, BF16)
        patch3 = vw(116.0625, 128, 240, F32)
        patchh = vw(117.0625, 128, 192, BF16)
        ghalf = vw(96, 128, 64 * 512, BF16)

        dtr = vw(128, 128, T, F32)
        dt_t = vw(132, 128, T, F32)
        tmpA = vw(136, 128, T, F32)
        ncmb = vw(140, 128, T, F32)
        patcht = vw(128, 3, INTER, BF16)
        ypc = vw(128, 128, INTER, BF16)
        awt_all = vw(128, 128, 32 * 512, BF16)
        cum = vw(144, 128, T, F32)
        ncmbT = vw(148, 128, T, F32)
        wTT = vw(152, 128, T, F32)
        cdecb = vw(156, 128, T, F32)

        ps_mm = [psb(f"ps_mm{i}", [128, 512], F32) for i in range(4)]
        ps_cv = [psb(f"ps_cv{i}", [128, 512], F32) for i in range(2)]
        ps_tr = psb("ps_tr", [128, 1024], BF16)
        ps_x = psb("ps_x", [128, 512], F32)

        for s_ in list(R.sems):
            R.bind(s_, ST0.enter_context(nc.semaphore(s_)))

        def tile_ct(i):
            return CT_ORDER[i]
        def is_xbc(i):
            return 1 <= i <= 80
        def is_gate(i):
            return i >= 81
        def evac_sem(i):
            return "ss" if is_gate(i) else "sv"
        def xn_sl(kt, th):
            return xn[:, kt * 1024 + th * 512: kt * 1024 + (th + 1) * 512]

        def p0_sync(sy):
            for dm in range(NDT):
                if dm >= 2:
                    R.wait(sy, "sv", f"p0.xc.{dm-2}")
                    R.wait(sy, "ss", f"p0.xq.{dm-2}")
                R.inc(sy and sy.dma_start(out=xld[dm % 2], in_=x_fm[dm]),
                      "sqs", 16, mark=f"p0.x.{dm}")
            for apd, aps_ in [(idn_f[:, :], idn_f_l), (onesrow_f[0:1, :], onesrow_l),
                              (dtbias[:, :], dtbias_c), (negA[:, :], alog_c),
                              (dcol[:, :], d_c), (cmask[:, :], cmask_c),
                              (convb[:, :], convb_l), (convwc[:, :], convw_cols),
                              (nw[:, :], nw_l)]:
                R.inc(sy and sy.dma_start(out=apd, in_=aps_), "sqs", 16)
            R.inc(None, "sqs", 0, mark="consts_f")

        def p0_vector(v):
            if v:
                v.memset(constc[:, 0:1], EPS)
                v.memset(constc[:, 1:2], 1.0)
            R.inc(v and v.last, "sv", 1, mark="p0.constc")
            for dm in range(NDT):
                R.wait(v, "sqs", f"p0.x.{dm}")
                R.inc(v and v.tensor_copy(xn[:, dm * 1024:(dm + 1) * 1024], xld[dm % 2]),
                      "sv", 1, mark=f"p0.xc.{dm}")
            R.wait(v, "st", f"p0.st.{NDT-1}")
            if v:
                v.tensor_scalar(mrow[0:1, 0:512], ps_mm[0][0:1, :], 1.0 / DM, None, ALU.mult)
                v.tensor_scalar(mrow[0:1, 512:1024], ps_mm[1][0:1, :], 1.0 / DM, None, ALU.mult)
            R.inc(v and v.last, "sv", 1, mark="p0.mrow")
            if v:
                v.tensor_scalar(rowbuf[0:1, 0:512], ps_mm[2][0:1, :], 1.0 / DM, None, ALU.mult)
                v.tensor_scalar(rowbuf[0:1, 512:1024], ps_mm[3][0:1, :], 1.0 / DM, None, ALU.mult)
                v.tensor_tensor(rowbuf2[0:1, :], mrow[0:1, :], mrow[0:1, :], ALU.mult)
                v.tensor_tensor(rowbuf[0:1, :], rowbuf[0:1, :], rowbuf2[0:1, :], ALU.subtract)
            R.inc(v and v.last, "sv", 1, mark="p0.var")
            for r, dst in enumerate([mb[:, 0:512], mb[:, 512:1024],
                                     rb[:, 0:512], rb[:, 512:1024]]):
                R.wait(v, "st", f"p0.bc.{r}")
                R.inc(v and v.tensor_copy(dst, ps_x[:, :]), "sv", 1, mark=f"p0.bce.{r}")
            for dm in range(NDT):
                if v:
                    sl = xn[:, dm * 1024:(dm + 1) * 1024]
                    v.tensor_tensor(sl, sl, mb[:, :], ALU.subtract)
                    v.tensor_tensor(sl, sl, rb[:, :], ALU.mult)
                R.inc(v and v.last, "sv", 1, mark=f"p0.app.{dm}")

        def p0_scalar(s):
            for dm in range(NDT):
                R.wait(s, "sv", f"p0.xc.{dm}")
                if dm >= 2:
                    R.wait(s, "st", f"p0.st.{dm-2}")
                R.inc(s and s.activation(xsq[dm % 2], xn[:, dm * 1024:(dm + 1) * 1024],
                                         AF.Square), "ss", 1, mark=f"p0.xq.{dm}")
            R.wait(s, "sv", "p0.var")
            R.wait(s, "sv", "p0.constc")
            if s:
                s.activation(rowbuf2[0:1, :], rowbuf[0:1, :], AF.Ln, bias=constc[0:1, 0:1])
                s.activation(rrow[0:1, :], rowbuf2[0:1, :], AF.Exp, scale=-0.5)
            R.inc(s and s.last, "ss", 1, mark="p0.rstd")

        def p0_tensor(t):
            R.wait(t, "sqg", "consts_b")
            for dm in range(NDT):
                R.wait(t, "sv", f"p0.xc.{dm}")
                R.wait(t, "ss", f"p0.xq.{dm}")
                if t:
                    for half in range(2):
                        t.matmul(ps_mm[half][0:1, :], onescol_b[:, :],
                                 xn_sl(dm, half), start=(dm == 0), stop=(dm == NDT - 1))
                        t.matmul(ps_mm[2 + half][0:1, :], onescol_b[:, :],
                                 xsq[dm % 2][:, half * 512:(half + 1) * 512],
                                 start=(dm == 0), stop=(dm == NDT - 1))
                R.inc(t and t.last, "st", 1, mark=f"p0.st.{dm}")
            R.wait(t, "sqs", "consts_f")
            srcs = [mrow[0:1, 0:512], mrow[0:1, 512:1024],
                    rrow[0:1, 0:512], rrow[0:1, 512:1024]]
            for r, src in enumerate(srcs):
                R.wait(t, "sv", "p0.mrow" if r < 2 else "p0.var")
                if r >= 2:
                    R.wait(t, "ss", "p0.rstd")
                if r >= 1:
                    R.wait(t, "sv", f"p0.bce.{r-1}")
                R.inc(t and t.matmul(ps_x[:, :], onesrow_f[0:1, :], src,
                                     start=True, stop=True), "st", 1, mark=f"p0.bc.{r}")

        def p1_gpsimd(g):
            R.inc(g and g.dma_start(out=wo_bf, in_=wo_l), "sqg", 16, mark="wo_precast")
            for apd, aps_ in [(idn_b[:, :], idn_f_l), (tribig[:, :], tribig_l),
                              (onescol_b[:, :], onescol_l), (keyst[:, :], keyst_l),
                              (wqt_all[:, :], wqt_l)]:
                R.inc(g and g.dma_start(out=apd, in_=aps_), "sqg", 16)
            R.inc(None, "sqg", 0, mark="consts_b")
            R.wait(g, "sv", f"p0.app.{NDT-1}")
            for i in range(NCT):
                ct = tile_ct(i)
                if i >= 2:
                    R.wait(g, "st", f"p1.mm.{i-2}")
                if is_xbc(i) and i >= 3 and is_xbc(i - 2):
                    R.wait(g, "st", f"p1.conv.{i-2}")
                for half in range(2):
                    R.inc(g and g.dma_start(out=wbuf[2 * (i % 2) + half],
                                            in_=w1_l[ct][:, half * 2048:(half + 1) * 2048]),
                          "sqg", 16, mark=f"p1.wl.{i}.{half}")
                if is_xbc(i):
                    if i == 1:
                        R.wait(g, "sv", f"p0.app.{NDT-1}")
                    R.inc(g and g.dma_start(out=cwb[i % 2], in_=convw_l[ct - 64]),
                          "sqg", 16, mark=f"p1.cw.{i}")
            R.wait(g, "sv", "p1.tails")
            R.inc(g and g.dma_start(out=tails_bi, in_=tails), "sqg", 16, mark="p1.tout")
            R.wait(g, "sqg", "p1.tout")
            R.inc(g and g.collective_compute(
                "AllGather", ALU.bypass, replica_groups=[[0, 1], [2, 3], [4, 5], [6, 7]],
                ins=[tails_bi.opt()], outs=[tails_bo.opt()]), "sc", 1, mark="p1.cc")
            R.wait(g, "sc", "p1.cc")
            R.inc(g and g.dma_start(out=pad3, in_=tails_bo[0:128, :]), "sqg", 16,
                  mark="p1.pad3")

        def p1_sync(sy):
            for i in range(1, NCT):
                ct = tile_ct(i)
                if is_xbc(i) and ct - 64 < 64:
                    j = ct - 64
                    R.wait(sy, "ss", f"p1.str.{i}")
                    R.inc(sy and sy.dma_start(
                        out=xtok_hbm[0:8, :, j * 128:(j + 1) * 128]
                        .rearrange("c p t -> p c t"),
                        in_=xstrip[i % 2].rearrange("p (c t) -> p c t", t=128)),
                        "sqs", 16, mark=f"p1.xsp.{i}")
                elif is_gate(i):
                    R.wait(sy, "ss", f"p1.ev.{i}")
                    R.inc(sy and sy.dma_start(
                        out=sg_hbm[:, :, ct * 128:(ct + 1) * 128]
                        .rearrange("c p t -> p c t"),
                        in_=hsfm[i % 2].rearrange("p (c t) -> p c t", t=128)),
                        "sqs", 16, mark=f"p1.sgsp.{i}")
            R.wait(sy, "ss", "p1.str0.63")
            R.inc(sy and sy.dma_start(out=xtok_hbm[0][0:3, :], in_=patcht[0:3, :]),
                  "sqs", 16, mark="p1.xsp0")
            R.fence("sqs", "p1.sync_done")

        def p1_tensor(t):
            R.wait(t, "sqg", "consts_b")
            for i in range(81):
                p1_tensor_tile(t, i)
            p1_tensor_prep(t)
            for i in range(81, NCT):
                p1_tensor_tile(t, i)
            for k in range(64):
                R.wait(t, "ss", "p1.patchs")
                if k >= 1:
                    R.wait(t, "ss", f"p1.str0.{k-1}")
                R.inc(t and t.transpose(ps_tr[0:3, (k % 8) * 128:(k % 8 + 1) * 128],
                                        patchh[:, k * 3:(k + 1) * 3], idn_b[:, :]),
                      "st", 1, mark=f"p1.tr0.{k}")

        def p1_tensor_tile(t, i):
            ct = tile_ct(i)
            if i >= 2:
                R.wait(t, evac_sem(i - 2), f"p1.ev.{i-2}")
            if i == 0:
                R.wait(t, "sv", "p0.var")
                R.wait(t, "sv", f"p0.app.{NDT-1}")
            if i == 1:
                R.wait(t, "sv", "p0.var")
            for half in range(2):
                R.wait(t, "sqg", f"p1.wl.{i}.{half}")
                if t:
                    for kt16 in range(16):
                        kt = half * 16 + kt16
                        for th in range(2):
                            t.matmul(ps_mm[2 * (i % 2) + th][:, :],
                                     wbuf[2 * (i % 2) + half][:, kt16 * 128:(kt16 + 1) * 128],
                                     xn_sl(kt, th), start=(kt == 0), stop=(kt == 31))
            R.inc(t and t.last, "st", 1, mark=f"p1.mm.{i}")
            if is_xbc(i):
                j = ct - 64
                R.wait(t, "sv", f"p1.ev.{i}")
                R.wait(t, "sqg", f"p1.cw.{i}")
                if i >= 2:
                    R.wait(t, "ss", f"p1.cve.{i-1}")
                if t:
                    for th in range(2):
                        for k in range(4):
                            t.matmul(ps_cv[th][:, :], cwb[i % 2][:, k * 128:(k + 1) * 128],
                                     xbcr[i % 2][:, k + th * 512: k + th * 512 + 512],
                                     start=(k == 0), stop=(k == 3))
                R.inc(t and t.last, "st", 1, mark=f"p1.conv.{i}")
                if j < 64:
                    R.wait(t, "ss", f"p1.cve.{i}")
                    if i >= 2 and 1 <= i - 1 <= 80 and tile_ct(i - 1) - 64 < 64:
                        R.wait(t, "ss", f"p1.str.{i-1}")
                    if t:
                        for c in range(8):
                            t.transpose(ps_tr[:, c * 128:(c + 1) * 128],
                                        hsfm[i % 2][:, c * 128:(c + 1) * 128], idn_b[:, :])
                    R.inc(t and t.last, "st", 1, mark=f"p1.tr.{i}")

        def p1_vector(v):
            R.wait(v, "sqs", "consts_f")
            if v:
                v.memset(xbcr[0][:, 0:3], 0.0)
                v.memset(xbcr[1][:, 0:3], 0.0)
            R.inc(v and v.last, "sv", 1, mark="p1.padz")
            for i in range(81):
                p1_vector_tile(v, i)
            p1_vector_prep(v)
            R.wait(v, "sqg", "p1.pad3")
            R.inc(v and v.tensor_scalar(pad3, pad3, cmask[:, 0:1], None, ALU.mult), "sv", 1)
            for j in range(80):
                if v:
                    v.tensor_copy(x6[:, 0:3], pad3[:, j * 3:(j + 1) * 3])
                    v.tensor_copy(x6[:, 3:6], first3[:, j * 3:(j + 1) * 3])
                    v.tensor_scalar(patch3[:, j * 3:(j + 1) * 3], x6[:, 0:3],
                                    convwc[:, 4 * j:4 * j + 1], None, ALU.mult)
                    for k in range(1, 4):
                        v.scalar_tensor_tensor(patch3[:, j * 3:(j + 1) * 3], x6[:, k:k + 3],
                                               convwc[:, 4 * j + k:4 * j + k + 1],
                                               patch3[:, j * 3:(j + 1) * 3], ALU.mult, ALU.add)
            R.inc(v and v.last, "sv", 1, mark="p1.patchv")

        def p1_vector_tile(v, i):
            ct = tile_ct(i)
            R.wait(v, "st", f"p1.mm.{i}")
            if i == 0:
                if v:
                    v.tensor_copy(dtr[:, 0:512], ps_mm[0][:, :])
                    v.tensor_copy(dtr[:, 512:1024], ps_mm[1][:, :])
                R.inc(v and v.last, "sv", 1, mark="p1.ev.0")
            elif is_xbc(i):
                j = ct - 64
                R.wait(v, "sv", "p1.padz")
                if i >= 3 and is_xbc(i - 2):
                    R.wait(v, "st", f"p1.conv.{i-2}")
                if v:
                    v.tensor_copy(xbcr[i % 2][:, 3:515], ps_mm[2 * (i % 2)][:, :])
                    v.tensor_copy(xbcr[i % 2][:, 515:1027], ps_mm[2 * (i % 2) + 1][:, :])
                    v.tensor_copy(first3[:, j * 3:(j + 1) * 3], xbcr[i % 2][:, 3:6])
                    v.tensor_copy(tails[:, j * 3:(j + 1) * 3], xbcr[i % 2][:, 1024:1027])
                R.inc(v and v.last, "sv", 1, mark=f"p1.ev.{i}")
                if i == 80:
                    R.inc(None, "sv", 0, mark="p1.tails")

        def p1_scalar(s):
            R.wait(s, "sqs", "consts_f")
            R.inc(s and s.activation(smallsc[:, 0:1], negA[:, 0:1], AF.Exp), "ss", 1,
                  mark="p1.alog")
            for i in range(1, 81):
                p1_scalar_tile(s, i)
            p1_scalar_prep(s)
            for i in range(81, NCT):
                p1_scalar_tile(s, i)
            R.wait(s, "sv", "p1.patchv")
            for j in range(80):
                dst = (patchh[:, j * 3:(j + 1) * 3] if j < 64 else
                       (B_sb[:, (j - 64) * 1024:(j - 64) * 1024 + 3] if j < 72 else
                        C_sb[:, (j - 72) * 1024:(j - 72) * 1024 + 3]))
                if s:
                    s.activation(dst, patch3[:, j * 3:(j + 1) * 3], AF.Silu,
                                 bias=convb[:, j:j + 1])
            R.inc(s and s.last, "ss", 1, mark="p1.patchs")
            for k in range(64):
                R.wait(s, "st", f"p1.tr0.{k}")
                R.inc(s and s.copy(patcht[0:3, k * 128:(k + 1) * 128],
                                   ps_tr[0:3, (k % 8) * 128:(k % 8 + 1) * 128]),
                      "ss", 1, mark=f"p1.str0.{k}")

        def p1_scalar_tile(s, i):
            ct = tile_ct(i)
            if is_gate(i):
                R.wait(s, "st", f"p1.mm.{i}")
                if i in (81, 82):
                    R.wait(s, "st", f"p1.tr.{i-18}")
                if i >= 83:
                    R.wait(s, "sqs", f"p1.sgsp.{i-2}")
                if s:
                    s.activation(hsfm[i % 2][:, 0:512], ps_mm[2 * (i % 2)][:, :], AF.Silu)
                    s.activation(hsfm[i % 2][:, 512:1024], ps_mm[2 * (i % 2) + 1][:, :],
                                 AF.Silu)
                R.inc(s and s.last, "ss", 1, mark=f"p1.ev.{i}")
                return
            j = ct - 64
            R.wait(s, "st", f"p1.conv.{i}")
            if j < 64:
                if i >= 3 and tile_ct(i - 2) - 64 < 64:
                    R.wait(s, "st", f"p1.tr.{i-2}")
                dst0, dst1 = hsfm[i % 2][:, 0:512], hsfm[i % 2][:, 512:1024]
            elif j < 72:
                dst0 = B_sb[:, (j - 64) * 1024:(j - 64) * 1024 + 512]
                dst1 = B_sb[:, (j - 64) * 1024 + 512:(j - 63) * 1024]
            else:
                dst0 = C_sb[:, (j - 72) * 1024:(j - 72) * 1024 + 512]
                dst1 = C_sb[:, (j - 72) * 1024 + 512:(j - 71) * 1024]
            if s:
                s.activation(dst0, ps_cv[0][:, :], AF.Silu, bias=convb[:, j:j + 1])
                s.activation(dst1, ps_cv[1][:, :], AF.Silu, bias=convb[:, j:j + 1])
            R.inc(s and s.last, "ss", 1, mark=f"p1.cve.{i}")
            if j < 64:
                R.wait(s, "st", f"p1.tr.{i}")
                if i >= 3 and tile_ct(i - 2) - 64 < 64:
                    R.wait(s, "sqs", f"p1.xsp.{i-2}")
                if s:
                    for c in range(8):
                        s.copy(xstrip[i % 2][:, c * 128:(c + 1) * 128],
                               ps_tr[:, c * 128:(c + 1) * 128])
                R.inc(s and s.last, "ss", 1, mark=f"p1.str.{i}")

        def p1_scalar_prep(s):
            R.wait(s, "sv", "p1.ev.0")
            if s:
                s.activation(dt_t, dtr, AF.Abs, bias=dtbias[:, 0:1])
                s.activation(tmpA, dt_t, AF.Exp, scale=-1.0)
                s.activation(dt_t, tmpA, AF.Ln, bias=constc[:, 1:2])
                s.activation(tmpA, dtr, AF.Relu, bias=dtbias[:, 0:1])
            R.inc(s and s.last, "ss", 1, mark="p1.sp1")
            R.wait(s, "sv", "p1.dt")
            R.inc(s and s.activation(tmpA, dt_t, AF.Ln), "ss", 1, mark="p1.lndt")
            R.wait(s, "sv", "p1.ncmb")
            R.inc(s and s.activation(dt_t, cum, AF.Exp), "ss", 1, mark="p1.expc")
            for c in range(NCH):
                R.inc(s and s.activation(tmpA[:, c * 128:(c + 1) * 128],
                                         ncmb[:, c * 128:(c + 1) * 128], AF.Exp,
                                         bias=cum[:, c * 128 + 127:(c + 1) * 128]),
                      "ss", 1, mark=f"p1.w.{c}")
            for c in range(NCH):
                R.wait(s, "st", f"p1.trn.{c}")
                R.inc(s and s.copy(ncmbT[:, c * 128:(c + 1) * 128],
                                   ps_x[:, (c % 2) * 128:(c % 2 + 1) * 128]),
                      "ss", 1, mark=f"p1.ncmbTe.{c}")
            for c in range(NCH):
                R.wait(s, "st", f"p1.trw.{c}")
                R.inc(s and s.copy(wTT[:, c * 128:(c + 1) * 128],
                                   ps_x[:, (c % 2) * 128:(c % 2 + 1) * 128]),
                      "ss", 1, mark=f"p1.wTe.{c}")
            for c in range(NCH):
                R.wait(s, "st", f"p1.trc.{c}")
                R.inc(s and s.copy(rowbuf[0:1, c * 128:(c + 1) * 128], ps_x[0:1, 256:384]),
                      "ss", 1, mark=f"p1.cdr.{c}")
            R.wait(s, "st", "p1.trd")
            R.inc(s and s.copy(mrow[0:1, 0:128], ps_x[0:1, 256:384]), "ss", 1, mark="p1.drow")

        def p1_vector_prep(v):
            R.wait(v, "ss", "p1.sp1")
            R.inc(v and v.tensor_tensor(dt_t, dt_t, tmpA, ALU.add), "sv", 1, mark="p1.dt")
            R.wait(v, "ss", "p1.alog")
            R.inc(v and v.tensor_scalar(negA[:, 0:1], smallsc[:, 0:1], -1.0, None, ALU.mult),
                  "sv", 1, mark="p1.negA")
            R.wait(v, "ss", "p1.lndt")
            if v:
                v.tensor_scalar(dtr, dt_t, negA[:, 0:1], None, ALU.mult)
                for c in range(NCH):
                    v.tensor_tensor_scan(cum[:, c * 128:(c + 1) * 128],
                                         dtr[:, c * 128:(c + 1) * 128],
                                         dtr[:, c * 128:(c + 1) * 128],
                                         0.0, ALU.add, ALU.bypass)
            R.inc(v and v.last, "sv", 1, mark="p1.cum")
            R.inc(v and v.tensor_tensor(ncmb, tmpA, cum, ALU.subtract), "sv", 1,
                  mark="p1.ncmb")
            R.wait(v, "ss", "p1.expc")
            R.inc(v and v.tensor_copy(expcum_b[:, :], dt_t), "sv", 1, mark="p1.expcb")
            if v:
                v.memset(rowbuf2[0:1, 0:128], 1.0)
            for c in range(1, NCH):
                R.wait(v, "ss", f"p1.cdr.{c-1}")
                if v:
                    v.tensor_tensor(rowbuf2[0:1, c * 128:(c + 1) * 128],
                                    rowbuf2[0:1, (c - 1) * 128: c * 128],
                                    rowbuf[0:1, (c - 1) * 128: c * 128], ALU.mult)
            R.inc(v and v.last, "sv", 1, mark="p1.dplrows")
            for c in range(NCH):
                R.wait(v, "st", f"p1.bcc.{c}")
                R.inc(v and v.tensor_copy(cdecb[:, c * 128:(c + 1) * 128],
                                          ps_x[:, 128 * (2 + c % 2): 128 * (3 + c % 2)]),
                      "sv", 1, mark=f"p1.cdecb.{c}")
            for c in range(NCH):
                R.wait(v, "st", f"p1.bcd.{c}")
                R.inc(v and v.tensor_scalar(dplb[:, c * 128:(c + 1) * 128],
                                            ps_x[:, 128 * (2 + c % 2): 128 * (3 + c % 2)],
                                            cmask[:, 0:1], None, ALU.mult),
                      "sv", 1, mark=f"p1.dplb.{c}")
            R.wait(v, "st", "p1.bcD")
            R.inc(v and v.tensor_copy(dtb[:, :], ps_x[:, 128:256]), "sv", 1, mark="p1.dtb")

        def p1_tensor_prep(t):
            for c in range(NCH):
                R.wait(t, "sv", "p1.ncmb")
                if c >= 1:
                    R.wait(t, "ss", f"p1.ncmbTe.{c-1}")
                R.inc(t and t.transpose(ps_x[:, (c % 2) * 128:(c % 2 + 1) * 128],
                                        ncmb[:, c * 128:(c + 1) * 128], idn_f[:, :]),
                      "st", 1, mark=f"p1.trn.{c}")
            for c in range(NCH):
                R.wait(t, "ss", f"p1.w.{c}")
                if c >= 1:
                    R.wait(t, "ss", f"p1.wTe.{c-1}")
                else:
                    R.wait(t, "ss", f"p1.ncmbTe.{NCH-1}")
                R.inc(t and t.transpose(ps_x[:, (c % 2) * 128:(c % 2 + 1) * 128],
                                        tmpA[:, c * 128:(c + 1) * 128], idn_f[:, :]),
                      "st", 1, mark=f"p1.trw.{c}")
            for c in range(NCH):
                R.wait(t, "ss", "p1.expc")
                if c >= 1:
                    R.wait(t, "ss", f"p1.cdr.{c-1}")
                R.inc(t and t.transpose(ps_x[0:1, 256:384],
                                        dt_t[:, c * 128 + 127:(c + 1) * 128], idn_f[:, :]),
                      "st", 1, mark=f"p1.trc.{c}")
            R.wait(t, "ss", f"p1.cdr.{NCH-1}")
            R.inc(t and t.transpose(ps_x[0:1, 256:384], dcol[:, 0:1], idn_f[:, :]),
                  "st", 1, mark="p1.trd")
            for c in range(NCH):
                R.wait(t, "ss", f"p1.cdr.{c}")
                if c == 0:
                    R.wait(t, "ss", "p1.drow")
                if c >= 1:
                    R.wait(t, "sv", f"p1.cdecb.{c-1}")
                R.inc(t and t.matmul(ps_x[:, 128 * (2 + c % 2): 128 * (3 + c % 2)],
                                     onesrow_f[0:1, :], rowbuf[0:1, c * 128:(c + 1) * 128],
                                     start=True, stop=True), "st", 1, mark=f"p1.bcc.{c}")
            for c in range(NCH):
                R.wait(t, "sv", "p1.dplrows")
                if c >= 1:
                    R.wait(t, "sv", f"p1.dplb.{c-1}")
                else:
                    R.wait(t, "sv", f"p1.cdecb.{NCH-1}")
                R.inc(t and t.matmul(ps_x[:, 128 * (2 + c % 2): 128 * (3 + c % 2)],
                                     onesrow_f[0:1, :], rowbuf2[0:1, c * 128:(c + 1) * 128],
                                     start=True, stop=True), "st", 1, mark=f"p1.bcd.{c}")
            R.wait(t, "ss", "p1.drow")
            R.wait(t, "sv", f"p1.dplb.{NCH-1}")
            R.inc(t and t.matmul(ps_x[:, 128:256], onesrow_f[0:1, :], mrow[0:1, 0:128],
                                 start=True, stop=True), "st", 1, mark="p1.bcD")

        def p2_tensor(t):
            R.wait(t, "sv", "p1.cum")
            R.wait(t, "sv", "p1.expcb")
            R.wait(t, "ss", f"p1.ev.{NCT-1}")
            R.wait(t, "ss", f"p1.ev.{NCT-2}")
            R.wait(t, "ss", "p1.patchs")
            R.wait(t, "ss", "p1.str0.63")
            for c in range(NCH):
                for w in range(2):
                    if c > 0 or w > 0:
                        pc, pw = (c, w - 1) if w > 0 else (c - 1, 1)
                        R.wait(t, "ss", f"p2.sre.{pc}.{pw}")
                    if c > 0 and w == 0:
                        R.wait(t, "sv", f"p2.scan.{c-1}.127")
                    if t:
                        for g_ in range(4 * w, 4 * w + 4):
                            t.matmul(
                                ps_cv[1][:, (g_ % 4) * 128:(g_ % 4 + 1) * 128],
                                B_sb[:, g_ * 1024 + c * 128: g_ * 1024 + (c + 1) * 128],
                                C_sb[:, g_ * 1024 + c * 128: g_ * 1024 + (c + 1) * 128],
                                start=True, stop=True)
                    R.inc(t and t.last, "st", 1, mark=f"p2.srw.{c}.{w}")
                    if c > 0 or w > 0:
                        pc, pw = (c, w - 1) if w > 0 else (c - 1, 1)
                        R.wait(t, "ss", f"p2.bte.{pc}.{pw}")
                    if t:
                        for g_ in range(4 * w, 4 * w + 4):
                            t.transpose(
                                ps_tr[:, g_ * 128:(g_ + 1) * 128],
                                B_sb[:, g_ * 1024 + c * 128: g_ * 1024 + (c + 1) * 128],
                                idn_b[:, :])
                    R.inc(t and t.last, "st", 1, mark=f"p2.btw.{c}.{w}")
                for j2 in range(128 + LA):
                    if j2 < 128:
                        h = j2
                        if h >= 2:
                            R.wait(t, "ss", f"p2.se.{c}.{h-2}")
                        elif c > 0:
                            R.wait(t, "ss", f"p2.se.{c-1}.127")
                        if t:
                            t.matmul(ps_mm[h % 2][:, 0:128],
                                     idn_f[:, h:h + 1].broadcast_to((128, 128)),
                                     cum[:, c * 128:(c + 1) * 128], start=True, stop=True)
                            t.matmul(ps_mm[h % 2][:, 128:256],
                                     idn_b[:, h:h + 1].broadcast_to((128, 128)),
                                     expcum_b[:, c * 128:(c + 1) * 128],
                                     start=True, stop=True)
                        R.inc(t and t.last, "st", 1, mark=f"p2.bc.{c}.{h}")
                    if j2 >= LA:
                        h = j2 - LA
                        if h == 0:
                            R.wait(t, "sqs", f"p2.xtok.{c}")
                            if c == 0:
                                R.wait(t, "sv", "p2.prev0")
                        R.wait(t, "sv", f"p2.dv.{c}.{h}")
                        p_ = h // 2
                        bank = ps_mm[2 + p_ % 2]
                        reg = bank[(h % 2) * 64:(h % 2) * 64 + 64, 0:128]
                        if h % 2 == 0:
                            if p_ >= 2:
                                R.wait(t, "ss", f"p2.ev.{c}.{p_-2}")
                            elif c > 0:
                                R.wait(t, "ss", f"p2.ev.{c-1}.{62 + p_}")
                        if c > 0:
                            R.wait(t, "sv", f"p2.scan.{c-1}.{h}")
                        if h >= 2:
                            R.wait(t, "sv", f"p2.scan.{c}.{h-2}")
                        elif c > 0:
                            R.wait(t, "sv", f"p2.scan.{c-1}.{126 + h}")
                        if h == 1:
                            R.wait(t, "ss", f"p2.sre.{c}.1")
                        if t:
                            t.matmul(reg, xtok_sb[c % 2][:, h * 64:(h + 1) * 64],
                                     stile[h % 4][:, :], start=True, stop=False)
                            t.matmul(reg, prev[:, h * 64:(h + 1) * 64],
                                     ctile[h % 4][:, :], start=False, stop=True)
                            t.matmul(ps_cv[h % 2][:, 256:320],
                                     bwt_s[h % 4][:, :],
                                     xtok_sb[c % 2][:, h * 64:(h + 1) * 64],
                                     start=True, stop=True)
                        R.inc(t and t.last, "st", 1, mark=f"p2.yg.{c}.{h}")

        def p2_scalar(s):
            for c in range(NCH):
                for w in range(2):
                    R.wait(s, "st", f"p2.srw.{c}.{w}")
                    if s:
                        for g_ in range(4 * w, 4 * w + 4):
                            s.copy(sraw[:, g_ * 128:(g_ + 1) * 128],
                                   ps_cv[1][:, (g_ % 4) * 128:(g_ % 4 + 1) * 128])
                    R.inc(s and s.last, "ss", 1, mark=f"p2.sre.{c}.{w}")
                    R.wait(s, "st", f"p2.btw.{c}.{w}")
                    if s:
                        for g_ in range(4 * w, 4 * w + 4):
                            s.copy(btok[:, g_ * 128:(g_ + 1) * 128],
                                   ps_tr[:, g_ * 128:(g_ + 1) * 128])
                    R.inc(s and s.last, "ss", 1, mark=f"p2.bte.{c}.{w}")
                for j2 in range(128 + LA):
                    if j2 < 128:
                        h = j2
                        R.wait(s, "st", f"p2.bc.{c}.{h}")
                        if h >= 4:
                            R.wait(s, "sv", f"p2.dv.{c}.{h-4}")
                        elif c > 0:
                            R.wait(s, "sv", f"p2.dv.{c-1}.127")
                        if s:
                            s.activation(etile[h % 4][:, :],
                                         ps_mm[h % 2][:, 0:128],
                                         AF.Exp,
                                         bias=ncmbT[:, c * 128 + h: c * 128 + h + 1])
                            s.copy(acb[h % 4][:, :], ps_mm[h % 2][:, 128:256])
                        R.inc(s and s.last, "ss", 1, mark=f"p2.se.{c}.{h}")
                    if j2 >= LA and (j2 - LA) % 2 == 1:
                        p_ = (j2 - LA) // 2
                        R.wait(s, "st", f"p2.yg.{c}.{2 * p_ + 1}")
                        if p_ == 0:
                            R.wait(s, "sqs", "p1.xsp0" if c == 0 else f"p2.ypsp.{c-1}")
                        bank = ps_mm[2 + p_ % 2]
                        R.inc(s and s.copy(ypc[:, p_ * 128:(p_ + 1) * 128],
                                           bank[:, 0:128]),
                              "ss", 1, mark=f"p2.ev.{c}.{p_}")

        def p2_vector(v):
            R.wait(v, "sqg", "consts_b")
            R.wait(v, "ss", f"p1.wTe.{NCH-1}")
            R.wait(v, "st", f"p1.mm.{NCT-1}")
            R.inc(v and v.memset(prev[:, :], 0.0), "sv", 1, mark="p2.prev0")
            for c in range(NCH):
                for j2 in range(128 + LA):
                    if j2 < 128:
                        h = j2
                        gr = h // 16
                        R.wait(v, "ss", f"p2.se.{c}.{h}")
                        R.wait(v, "ss", f"p2.sre.{c}.{h // 64}")
                        R.wait(v, "ss", f"p2.bte.{c}.{h // 64}")
                        if v:
                            v.tensor_tensor(stile[h % 4][:, :], etile[h % 4][:, :],
                                            tribig[:, :], ALU.min)
                            v.tensor_tensor(stile[h % 4][:, :], stile[h % 4][:, :],
                                            sraw[:, gr * 128:(gr + 1) * 128], ALU.mult)
                            v.scalar_tensor_tensor(stile[h % 4][:, :], idn_b[:, :],
                                                   dtb[:, h:h + 1], stile[h % 4][:, :],
                                                   ALU.mult, ALU.add)
                            v.tensor_tensor(ctile[h % 4][:, :],
                                            C_sb[:, gr * 1024 + c * 128:
                                                 gr * 1024 + (c + 1) * 128],
                                            acb[h % 4][:, :], ALU.mult)
                            v.tensor_scalar(bwt_s[h % 4][:, :],
                                            btok[:, gr * 128:(gr + 1) * 128],
                                            wTT[:, c * 128 + h: c * 128 + h + 1],
                                            None, ALU.mult)
                        R.inc(v and v.last, "sv", 1, mark=f"p2.dv.{c}.{h}")
                    if j2 >= LA:
                        h = j2 - LA
                        R.wait(v, "st", f"p2.yg.{c}.{h}")
                        R.inc(v and v.scalar_tensor_tensor(
                            prev[:, h * 64:(h + 1) * 64], prev[:, h * 64:(h + 1) * 64],
                            cdecb[:, c * 128 + h: c * 128 + h + 1],
                            ps_cv[h % 2][:, 256:320],
                            ALU.mult, ALU.add), "sv", 1, mark=f"p2.scan.{c}.{h}")
            R.inc(None, "sv", 0, mark="p2.scan_done")

        def p2_sync(sy):
            for c in range(NCH):
                R.wait(sy, "sqs", "p1.sync_done")
                R.wait(sy, "st", f"p1.mm.{NCT-1}")
                if c >= 2:
                    R.wait(sy, "st", f"p2.yg.{c-2}.127")
                R.inc(sy and sy.dma_start(out=xtok_sb[c % 2][:, :], in_=xtok_hbm[c]),
                      "sqs", 16, mark=f"p2.xtok.{c}")
                R.wait(sy, "ss", f"p2.ev.{c}.63")
                R.inc(sy and sy.dma_start(out=yp_hbm[c], in_=ypc[:, :]), "sqs", 16,
                      mark=f"p2.ypsp.{c}")
            R.fence("sqs", "p2.sync_done")

        def p2_gpsimd(g):
            R.wait(g, "sv", "p2.scan_done")
            R.inc(g and g.dma_start(out=carry_bi, in_=prev[:, :]), "sqg", 16, mark="p2.cout")
            R.wait(g, "sqg", "p2.cout")
            R.inc(g and g.collective_compute(
                "AllGather", ALU.bypass, replica_groups=[[0, 1], [2, 3], [4, 5], [6, 7]],
                ins=[carry_bi.opt()], outs=[carry_bo.opt()]), "sc", 1, mark="p2.cc")
            R.wait(g, "sc", "p2.cc")
            R.inc(g and g.dma_start(out=carry[:, :], in_=carry_bo[0:128, :]), "sqg", 16,
                  mark="p2.cin")

        def p3_tensor(t):
            for hf in range(2):
                for c in range(4 * hf, 4 * hf + 4):
                    bcbanks = [ps_mm[0], ps_mm[1], ps_cv[0], ps_cv[1]]
                    for j2 in range(128 + LA):
                        if j2 < 128:
                            h = j2
                            if h >= 4:
                                R.wait(t, "ss", f"p3.se.{c}.{h-4}")
                            elif c > 0:
                                R.wait(t, "ss", f"p3.se.{c-1}.127")
                            else:
                                R.wait(t, "ss", "p2.se.7.127")
                                R.wait(t, "sv", "p2.scan.7.127")
                                R.wait(t, "ss", "p2.sre.7.1")
                            if c > 0 and h == 0:
                                R.wait(t, "ss", f"p3.rstd.{c-1}")
                            R.inc(t and t.matmul(
                                bcbanks[h % 4][:, 0:128],
                                idn_b[:, h:h + 1].broadcast_to((128, 128)),
                                expcum_b[:, c * 128:(c + 1) * 128],
                                start=True, stop=True), "st", 1, mark=f"p3.bc.{c}.{h}")
                        if j2 >= LA:
                            h = j2 - LA
                            R.wait(t, "sv", f"p3.dv.{c}.{h}")
                            p_ = h // 2
                            bank = ps_mm[2 + p_ % 2]
                            reg = bank[(h % 2) * 64:(h % 2) * 64 + 64, 0:128]
                            if h % 2 == 0:
                                if p_ >= 2:
                                    R.wait(t, "sv", f"p3.pr.{c}.{p_-2}")
                                elif c > 0:
                                    R.wait(t, "sv", f"p3.pr.{c-1}.{62 + p_}")
                                else:
                                    R.wait(t, "ss", "p2.ev.7.63")
                            R.inc(t and t.matmul(reg, bwts3[h % 4][:, 0:64],
                                                 ctile3[h % 4][:, :],
                                                 start=True, stop=True),
                                  "st", 1, mark=f"p3.y2.{c}.{h}")
                    for p_ in range(64):
                        R.wait(t, "ss", f"p3.sq.{c}.{p_}")
                        if p_ == 0:
                            R.wait(t, "ss", f"p3.se.{c}.127")
                        R.inc(t and t.matmul(
                            ps_cv[1][:, (c % 2) * 8 + p_ // 8:(c % 2) * 8 + p_ // 8 + 1],
                            g2t[p_ % 2][:, :], onescol_b[:, :],
                            start=(p_ % 8 == 0), stop=(p_ % 8 == 7)),
                            "st", 1, mark=f"p3.vm.{c}.{p_}")
                    R.inc(None, "st", 0, mark=f"p3.var.{c}")
                    for g_ in range(8):
                        R.wait(t, "ss", f"p3.rstd.{c}")
                        if g_ >= 1:
                            R.wait(t, "ss", f"p3.re.{c}.{g_-1}")
                        R.inc(t and t.transpose(ps_x[0:1, 384:512],
                                                rstdT[:, g_:g_ + 1], idn_f[:, :]),
                              "st", 1, mark=f"p3.rt.{c}.{g_}")
                    for g_ in range(8):
                        R.wait(t, "ss", f"p3.re.{c}.7")
                        if g_ >= 1:
                            R.wait(t, "sv", f"p3.gn.{c}.{8 * (g_ - 1) + 7}")
                        R.inc(t and t.matmul(ps_x[:, (g_ % 3) * 128:(g_ % 3 + 1) * 128],
                                             onesrow_f[0:1, :],
                                             rowbuf[0:1, g_ * 128:(g_ + 1) * 128],
                                             start=True, stop=True),
                              "st", 1, mark=f"p3.rb.{c}.{g_}")
                for dm in range(NDT):
                    R.wait(t, "sqs", f"p3.wo.{hf}.{dm}")
                    R.wait(t, "sv", f"p3.gn.{4*hf+3}.63")
                    if dm >= 2:
                        R.wait(t, "sv", f"p3.x2.{hf}.{dm-2}")
                    if t:
                        for kt in range(64):
                            t.matmul(ps_mm[2 + dm % 2][:, :],
                                     wob[dm % 2][:, kt * 128:(kt + 1) * 128],
                                     ghalf[:, kt * 512:(kt + 1) * 512],
                                     start=(kt == 0), stop=(kt == 63))
                    R.inc(t and t.last, "st", 1, mark=f"p3.omm.{hf}.{dm}")
                p4_tensor_half(t, hf)

        def p3_scalar(s):
            for hf in range(2):
                for c in range(4 * hf, 4 * hf + 4):
                    for j2 in range(128 + LA):
                        if j2 < 128:
                            h = j2
                            R.wait(s, "st", f"p3.bc.{c}.{h}")
                            if h >= 4:
                                R.wait(s, "sv", f"p3.dv.{c}.{h-4}")
                            elif c > 0:
                                R.wait(s, "sv", f"p3.dv.{c-1}.127")
                            bcb = [ps_mm[0], ps_mm[1], ps_cv[0], ps_cv[1]][h % 4]
                            R.inc(s and s.copy(acb3[h % 4][:, :], bcb[:, 0:128]),
                                  "ss", 1, mark=f"p3.se.{c}.{h}")
                    for p_ in range(64):
                        R.wait(s, "sv", f"p3.pr.{c}.{p_}")
                        if p_ >= 2:
                            R.wait(s, "st", f"p3.vm.{c}.{p_-2}")
                        R.inc(s and s.activation(
                            g2t[p_ % 2][:, :],
                            ghalf[:, p_ * 512 + (c % 4) * 128:
                                  p_ * 512 + (c % 4 + 1) * 128],
                            AF.Square), "ss", 1, mark=f"p3.sq.{c}.{p_}")
                    R.wait(s, "st", f"p3.var.{c}")
                    if s:
                        s.activation(smallsc[:, 0:8],
                                     ps_cv[1][:, (c % 2) * 8:(c % 2) * 8 + 8],
                                     AF.Ln, scale=1.0 / 1024.0, bias=constc[:, 0:1])
                        s.activation(rstdT[:, :], smallsc[:, 0:8], AF.Exp, scale=-0.5)
                    R.inc(s and s.last, "ss", 1, mark=f"p3.rstd.{c}")
                    for g_ in range(8):
                        R.wait(s, "st", f"p3.rt.{c}.{g_}")
                        R.inc(s and s.copy(rowbuf[0:1, g_ * 128:(g_ + 1) * 128],
                                           ps_x[0:1, 384:512]),
                              "ss", 1, mark=f"p3.re.{c}.{g_}")
                p4_scalar_half(s, hf)

        def p3_vector(v):
            for hf in range(2):
                for c in range(4 * hf, 4 * hf + 4):
                    for j2 in range(128 + LA):
                        if j2 < 128:
                            h = j2
                            gr = h // 16
                            R.wait(v, "ss", f"p3.se.{c}.{h}")
                            if h == 0 and c == 0:
                                R.wait(v, "sqg", "p2.cin")
                            if h >= 4:
                                R.wait(v, "st", f"p3.y2.{c}.{h-4}")
                            elif c > 0:
                                R.wait(v, "st", f"p3.y2.{c-1}.127")
                            if v:
                                v.tensor_tensor(ctile3[h % 4][:, :],
                                                C_sb[:, gr * 1024 + c * 128:
                                                     gr * 1024 + (c + 1) * 128],
                                                acb3[h % 4][:, :], ALU.mult)
                                v.tensor_scalar(bwts3[h % 4][:, 0:64],
                                                carry[:, h * 64:(h + 1) * 64],
                                                dplb[:, c * 128 + h: c * 128 + h + 1],
                                                None, ALU.mult)
                            R.inc(v and v.last, "sv", 1, mark=f"p3.dv.{c}.{h}")
                        if j2 >= LA and (j2 - LA) % 2 == 1:
                            p_ = (j2 - LA) // 2
                            if c == 0 and p_ == 0:
                                R.wait(v, "sqs", "p2.sync_done")
                            if c == 4 and p_ == 0:
                                R.wait(v, "sqg", "p4.out.0.31")
                            R.wait(v, "st", f"p3.y2.{c}.{2 * p_ + 1}")
                            R.wait(v, "sqs", f"p3.yp.{c}")
                            R.wait(v, "sqs", f"p3.sg.{c}")
                            bank = ps_mm[2 + p_ % 2]
                            if v:
                                v.tensor_tensor(ttile[:, :],
                                                ypr[c % 2][:, p_ * 128:(p_ + 1) * 128],
                                                bank[:, 0:128], ALU.add)
                                v.tensor_tensor(
                                    ghalf[:, p_ * 512 + (c % 4) * 128:
                                          p_ * 512 + (c % 4 + 1) * 128],
                                    ttile[:, :], sgr[c % 2][:, p_ * 128:(p_ + 1) * 128],
                                    ALU.mult)
                            R.inc(v and v.last, "sv", 1, mark=f"p3.pr.{c}.{p_}")
                    for p_ in range(64):
                        g_ = p_ // 8
                        R.wait(v, "st", f"p3.rb.{c}.{g_}")
                        sl = ghalf[:, p_ * 512 + (c % 4) * 128: p_ * 512 + (c % 4 + 1) * 128]
                        R.inc(v and v.scalar_tensor_tensor(
                            sl, sl, nw[:, p_:p_ + 1],
                            ps_x[:, (g_ % 3) * 128:(g_ % 3 + 1) * 128],
                            ALU.mult, ALU.mult), "sv", 1, mark=f"p3.gn.{c}.{p_}")
                for dm in range(NDT):
                    R.wait(v, "st", f"p3.omm.{hf}.{dm}")
                    R.wait(v, "sqs", f"p3.xr.{hf}.{dm}")
                    if dm >= 2:
                        R.wait(v, "sqs", f"p3.x2s.{hf}.{dm-2}")
                    R.inc(v and v.tensor_tensor(x2t[dm % 2][:, :], xre[dm % 2][:, :],
                                                ps_mm[2 + dm % 2][:, :], ALU.add),
                          "sv", 1, mark=f"p3.x2.{hf}.{dm}")
                p4_vector_half(v, hf)

        def p3_sync(sy):
            for hf in range(2):
                for c in range(4 * hf, 4 * hf + 4):
                    if c == 0:
                        R.wait(sy, "st", "p2.yg.7.127")
                        R.wait(sy, "sv", "p2.scan.7.127")
                        R.wait(sy, "ss", "p2.ev.7.63")
                    if c == 4:
                        R.wait(sy, "st", f"p3.omm.0.{NDT-1}")
                        R.wait(sy, "sv", "p4.fin.0.31")
                    if c % 4 >= 2:
                        R.wait(sy, "sv", f"p3.pr.{c-2}.63")
                    R.wait(sy, "sqs", "p1.sync_done")
                    R.wait(sy, "sqs", f"p2.ypsp.{c}")
                    R.inc(sy and sy.dma_start(out=ypr[c % 2][:, :], in_=yp_hbm[c]),
                          "sqs", 16, mark=f"p3.yp.{c}")
                    R.inc(sy and sy.dma_start(out=sgr[c % 2][:, :], in_=sg_hbm[c]),
                          "sqs", 16, mark=f"p3.sg.{c}")
                for dm in range(NDT):
                    R.wait(sy, "sqg", "wo_precast")
                    R.wait(sy, "sv", f"p3.pr.{4*hf+3}.63")
                    if dm >= 2:
                        R.wait(sy, "st", f"p3.omm.{hf}.{dm-2}")
                    R.inc(sy and sy.dma_start(out=wob[dm % 2][:, :], in_=wo_bf[dm]),
                          "sqs", 16, mark=f"p3.wo.{hf}.{dm}")
                    if dm == 0:
                        R.wait(sy, "sqg", "p2.cout")
                    if dm >= 2:
                        R.wait(sy, "sv", f"p3.x2.{hf}.{dm-2}")
                    R.inc(sy and sy.dma_start(out=xre[dm % 2][:, :],
                                              in_=x_fm[dm][:, hf * 512:(hf + 1) * 512]),
                          "sqs", 16, mark=f"p3.xr.{hf}.{dm}")
                p4_sync_half(sy, hf)

        def p4_tensor_half(t, hf):
            for dm in range(NDT):
                R.wait(t, "sv", f"p4.xc.{hf}.{dm}")
                R.wait(t, "ss", f"p4.sq.{hf}.{dm}")
                if dm == 0:
                    R.wait(t, "ss", f"p3.se.{4*hf+3}.127")
                if t:
                    t.matmul(ps_mm[0][0:1, :], onescol_b[:, :],
                             xn2[:, dm * 512:(dm + 1) * 512],
                             start=(dm == 0), stop=(dm == NDT - 1))
                    t.matmul(ps_mm[1][0:1, :], onescol_b[:, :],
                             zgsb[:, (dm % 2) * 512:(dm % 2 + 1) * 512],
                             start=(dm == 0), stop=(dm == NDT - 1))
                R.inc(t and t.last, "st", 1, mark=f"p4.st.{hf}.{dm}")
            for r in range(2):
                R.wait(t, "sv", f"p4.var.{hf}")
                R.wait(t, "sv", f"p3.gn.{4*hf+3}.63")
                if r == 1:
                    R.wait(t, "ss", f"p4.rstd.{hf}")
                    R.wait(t, "sv", f"p4.bce.{hf}.0")
                src = mrow[0:1, 0:512] if r == 0 else rrow[0:1, 0:512]
                R.inc(t and t.matmul(ps_x[:, 0:512], onesrow_f[0:1, :], src,
                                     start=True, stop=True), "st", 1,
                      mark=f"p4.bc.{hf}.{r}")
            for tt in range(4):
                for kt in range(32):
                    R.wait(t, "sv", f"p4.xn2.{hf}.{kt}")
                    if kt == 0 and tt >= 1:
                        R.wait(t, "sv", f"p4.qe.{hf}.{tt-1}")
                    if tt == 0 and kt == 0 and hf == 0:
                        R.wait(t, "sv", "p2.scan.7.127")
                    if t:
                        t.matmul(ps_cv[0][:, tt * 16:(tt + 1) * 16],
                                 xn2[:, kt * 512 + tt * 128: kt * 512 + (tt + 1) * 128],
                                 wqt_all[:, kt * 16:(kt + 1) * 16],
                                 start=(kt == 0), stop=(kt == 31))
                R.inc(t and t.last, "st", 1, mark=f"p4.q.{hf}.{tt}")
                R.wait(t, "sv", f"p4.qe.{hf}.{tt}")
                if tt == 0 and hf == 1:
                    R.wait(t, "ss", "p4.zte.0.3.3")
                if tt == 0 and hf == 0:
                    R.wait(t, "ss", "p2.bte.7.1")
                if tt >= 1:
                    R.wait(t, "ss", f"p4.qte.{hf}.{tt-1}")
                R.inc(t and t.transpose(ps_tr[0:16, 0:128], qsb[:, :], idn_b[:, :]),
                      "st", 1, mark=f"p4.qt.{hf}.{tt}")
                R.wait(t, "ss", f"p4.qte.{hf}.{tt}")
                if tt >= 1:
                    R.wait(t, "sv", f"p4.m8.{hf}.{tt-1}")
                else:
                    R.wait(t, "ss", f"p3.rstd.{4*hf+3}")
                    if hf == 1:
                        R.wait(t, "sv", "p4.m8.0.3")
                R.inc(t and t.matmul(ps_cv[1][:, 448:512], qT[0:16, :], keyst[:, :],
                                     start=True, stop=True), "st", 1,
                      mark=f"p4.sc.{hf}.{tt}")
            for kt in range(32):
                R.wait(t, "sqg", f"p4.aw.{hf}")
                if kt == 0:
                    R.wait(t, "sv", f"p4.var.{hf}")
                    R.wait(t, "sv", f"p3.x2.{hf}.{NDT-1}")
                if t:
                    for tt in range(4):
                        t.matmul(ps_mm[tt][:, :],
                                 xn2[:, kt * 512 + tt * 128: kt * 512 + (tt + 1) * 128],
                                 awt_all[:, kt * 512:(kt + 1) * 512],
                                 start=(kt == 0), stop=(kt == 31))
                R.inc(t and t.last, "st", 1, mark=f"p4.zk.{hf}.{kt}")
            for tt in range(4):
                for kb in range(4):
                    R.wait(t, "sv", f"p4.zg.{hf}.{tt}")
                    sl = 4 * tt + kb
                    if sl >= 1:
                        R.wait(t, "ss", f"p4.zte.{hf}.{(sl - 1) // 4}.{(sl - 1) % 4}")
                    R.inc(t and t.transpose(ps_tr[:, (sl % 8) * 128:(sl % 8 + 1) * 128],
                                            zgsb[:, tt * 512 + kb * 128:
                                                 tt * 512 + (kb + 1) * 128],
                                            idn_b[:, :]), "st", 1,
                          mark=f"p4.zt.{hf}.{tt}.{kb}")
            for dm in range(NDT):
                R.wait(t, "sqg", f"p4.bw.{hf}.{dm}")
                R.wait(t, "ss", f"p4.zte.{hf}.3.3")
                if dm >= 2:
                    R.wait(t, "sv", f"p4.fin.{hf}.{dm-2}")
                if t:
                    for kb in range(4):
                        t.matmul(ps_mm[2 + dm % 2][:, :],
                                 bwtd[dm % 4][:, kb * 128:(kb + 1) * 128],
                                 zgt_half[:, kb * 512:(kb + 1) * 512],
                                 start=(kb == 0), stop=(kb == 3))
                R.inc(t and t.last, "st", 1, mark=f"p4.o.{hf}.{dm}")

        def p4_scalar_half(s, hf):
            for dm in range(NDT):
                R.wait(s, "sv", f"p4.xc.{hf}.{dm}")
                if dm >= 2:
                    R.wait(s, "st", f"p4.st.{hf}.{dm-2}")
                R.inc(s and s.activation(zgsb[:, (dm % 2) * 512:(dm % 2 + 1) * 512],
                                         xn2[:, dm * 512:(dm + 1) * 512], AF.Square),
                      "ss", 1, mark=f"p4.sq.{hf}.{dm}")
            R.wait(s, "sv", f"p4.var.{hf}")
            if s:
                s.activation(rowbuf2[0:1, 0:512], rowbuf[0:1, 0:512], AF.Ln,
                             bias=constc[0:1, 0:1])
                s.activation(rrow[0:1, 0:512], rowbuf2[0:1, 0:512], AF.Exp, scale=-0.5)
            R.inc(s and s.last, "ss", 1, mark=f"p4.rstd.{hf}")
            for tt in range(4):
                R.wait(s, "st", f"p4.qt.{hf}.{tt}")
                R.inc(s and s.copy(qT[0:16, :], ps_tr[0:16, 0:128]), "ss", 1,
                      mark=f"p4.qte.{hf}.{tt}")
                R.wait(s, "sv", f"p4.m8.{hf}.{tt}")
                R.inc(s and s.mul(negt1[:, 0:1], m8a[:, 0:1], -1.0), "ss", 1,
                      mark=f"p4.nt1.{hf}.{tt}")
                R.wait(s, "sv", f"p4.m8b.{hf}.{tt}")
                R.inc(s and s.activation(esb[:, :], ssb[:, :], AF.Exp,
                                         bias=negt1[:, 0:1]), "ss", 1,
                      mark=f"p4.e.{hf}.{tt}")
                R.wait(s, "sv", f"p4.rs.{hf}.{tt}")
                if s:
                    s.activation(smallsc[:, 1:2], rs1[:, 0:1], AF.Ln)
                    s.activation(rs1[:, 0:1], smallsc[:, 1:2], AF.Exp, scale=-1.0)
                R.inc(s and s.last, "ss", 1, mark=f"p4.rr.{hf}.{tt}")
            for tt in range(4):
                for kb in range(4):
                    R.wait(s, "st", f"p4.zt.{hf}.{tt}.{kb}")
                    sl = 4 * tt + kb
                    R.inc(s and s.copy(zgt_half[:, kb * 512 + tt * 128:
                                                kb * 512 + (tt + 1) * 128],
                                       ps_tr[:, (sl % 8) * 128:(sl % 8 + 1) * 128]),
                          "ss", 1, mark=f"p4.zte.{hf}.{tt}.{kb}")

        def p4_vector_half(v, hf):
            for dm in range(NDT):
                R.wait(v, "sqs", f"p4.x.{hf}.{dm}")
                if dm == 0:
                    R.wait(v, "st", f"p3.omm.{hf}.{NDT-1}")
                R.inc(v and v.tensor_copy(xn2[:, dm * 512:(dm + 1) * 512],
                                          x2l[dm % 2][:, :]),
                      "sv", 1, mark=f"p4.xc.{hf}.{dm}")
            R.wait(v, "st", f"p4.st.{hf}.{NDT-1}")
            if v:
                v.tensor_scalar(mrow[0:1, 0:512], ps_mm[0][0:1, :], 1.0 / DM, None, ALU.mult)
                v.tensor_scalar(rowbuf[0:1, 0:512], ps_mm[1][0:1, :], 1.0 / DM, None,
                                ALU.mult)
                v.tensor_tensor(rowbuf2[0:1, 0:512], mrow[0:1, 0:512], mrow[0:1, 0:512],
                                ALU.mult)
                v.tensor_tensor(rowbuf[0:1, 0:512], rowbuf[0:1, 0:512],
                                rowbuf2[0:1, 0:512], ALU.subtract)
            R.inc(v and v.last, "sv", 1, mark=f"p4.var.{hf}")
            for r, dst in enumerate([mb2[:, :], rb2[:, :]]):
                R.wait(v, "st", f"p4.bc.{hf}.{r}")
                R.inc(v and v.tensor_copy(dst, ps_x[:, 0:512]), "sv", 1,
                      mark=f"p4.bce.{hf}.{r}")
            for dm in range(NDT):
                if v:
                    sl = xn2[:, dm * 512:(dm + 1) * 512]
                    v.tensor_tensor(sl, sl, mb2[:, :], ALU.subtract)
                    v.tensor_tensor(sl, sl, rb2[:, :], ALU.mult)
                R.inc(v and v.last, "sv", 1, mark=f"p4.xn2.{hf}.{dm}")
            for tt in range(4):
                R.wait(v, "st", f"p4.q.{hf}.{tt}")
                R.inc(v and v.tensor_copy(qsb[:, :], ps_cv[0][:, tt * 16:(tt + 1) * 16]),
                      "sv", 1, mark=f"p4.qe.{hf}.{tt}")
                R.wait(v, "st", f"p4.sc.{hf}.{tt}")
                if v:
                    v.tensor_copy(ssb[:, :], ps_cv[1][:, 448:512])
                    v.max(m8a[:, :], ssb[:, :])
                R.inc(v and v.last, "sv", 1, mark=f"p4.m8.{hf}.{tt}")
                R.wait(v, "ss", f"p4.nt1.{hf}.{tt}")
                if v:
                    v.tensor_scalar(msk[:, :], ssb[:, :], m8a[:, 7:8], None, ALU.is_ge)
                    v.scalar_tensor_tensor(esb[:, :], msk[:, :], -1e38, ssb[:, :],
                                           ALU.mult, ALU.add)
                    v.max(m8b[:, :], esb[:, :])
                R.inc(v and v.last, "sv", 1, mark=f"p4.m8b.{hf}.{tt}")
                R.wait(v, "ss", f"p4.e.{hf}.{tt}")
                if v:
                    v.tensor_scalar(msk[:, :], ssb[:, :], m8b[:, 7:8], None, ALU.is_ge)
                    v.tensor_tensor(esb[:, :], esb[:, :], msk[:, :], ALU.mult)
                    v.tensor_reduce(rs1[:, 0:1], esb[:, :], AX.X, ALU.add)
                R.inc(v and v.last, "sv", 1, mark=f"p4.rs.{hf}.{tt}")
                R.wait(v, "ss", f"p4.rr.{hf}.{tt}")
                R.inc(v and v.tensor_scalar(gsb[:, tt * 64:(tt + 1) * 64], esb[:, :],
                                            rs1[:, 0:1], None,
                                            ALU.mult), "sv", 1, mark=f"p4.g.{hf}.{tt}")
            for tt in range(4):
                R.wait(v, "st", f"p4.zk.{hf}.{31}")
                R.wait(v, "sv", f"p4.g.{hf}.{tt}")
                zv = ps_mm[tt][:, :].rearrange("p (e r) -> p e r", r=8)
                gv = gsb[:, tt * 64:(tt + 1) * 64].unsqueeze(2).broadcast_to((128, 64, 8))
                R.inc(v and v.tensor_tensor(
                    zgsb[:, tt * 512:(tt + 1) * 512].rearrange("p (e r) -> p e r", r=8),
                    zv, gv, ALU.mult), "sv", 1, mark=f"p4.zg.{hf}.{tt}")
            for dm in range(NDT):
                R.wait(v, "st", f"p4.o.{hf}.{dm}")
                R.wait(v, "sqs", f"p4.x2b.{hf}.{dm}")
                if dm >= 2:
                    R.wait(v, "sqg", f"p4.out.{hf}.{dm-2}")
                R.inc(v and v.scalar_tensor_tensor(
                    outbuf[dm % 2][:, :], ps_mm[2 + dm % 2][:, :], 2.0,
                    x2l[dm % 2][:, :], ALU.mult, ALU.add), "sv", 1,
                    mark=f"p4.fin.{hf}.{dm}")

        def p4_sync_half(sy, hf):
            for dm in range(NDT):
                R.wait(sy, "sqg", f"p3.x2s.{hf}.{dm}")
                if dm >= 2:
                    R.wait(sy, "sv", f"p4.xc.{hf}.{dm-2}")
                R.inc(sy and sy.dma_start(out=x2l[dm % 2][:, :],
                                          in_=x2_hbm[dm][:, hf * 512:(hf + 1) * 512]),
                      "sqs", 16, mark=f"p4.x.{hf}.{dm}")
            for dm in range(NDT):
                if dm < 2:
                    R.wait(sy, "sv", f"p4.xn2.{hf}.{NDT-1}")
                else:
                    R.wait(sy, "sv", f"p4.fin.{hf}.{dm-2}")
                R.inc(sy and sy.dma_start(out=x2l[dm % 2][:, :],
                                          in_=x2_hbm[dm][:, hf * 512:(hf + 1) * 512]),
                      "sqs", 16, mark=f"p4.x2b.{hf}.{dm}")

        def p4_gpsimd(g):
            for hf in range(2):
                for dm in range(NDT):
                    R.wait(g, "sv", f"p3.x2.{hf}.{dm}")
                    R.inc(g and g.dma_start(out=x2_hbm[dm][:, hf * 512:(hf + 1) * 512],
                                            in_=x2t[dm % 2][:, :]),
                          "sqg", 16, mark=f"p3.x2s.{hf}.{dm}")
                R.wait(g, "st", f"p3.omm.{hf}.{NDT-1}")
                R.inc(g and g.dma_start(
                    out=awt_all[:, :].rearrange("p (k f) -> p k f", f=512),
                    in_=awt_l.rearrange("k p f -> p k f")),
                    "sqg", 16, mark=f"p4.aw.{hf}")
                for dm in range(NDT + 1):
                    if dm < NDT:
                        if dm >= 4:
                            R.wait(g, "st", f"p4.o.{hf}.{dm-4}")
                        else:
                            R.wait(g, "sv", f"p3.x2.{hf}.{NDT-1}")
                        R.inc(g and g.dma_start(out=bwtd[dm % 4][:, :], in_=bwt_l[dm]),
                              "sqg", 16, mark=f"p4.bw.{hf}.{dm}")
                    if dm >= 1:
                        R.wait(g, "sv", f"p4.fin.{hf}.{dm-1}")
                        R.inc(g and g.dma_start(
                            out=out_fm[dm - 1][:, hf * 512:(hf + 1) * 512],
                            in_=outbuf[(dm - 1) % 2][:, :]),
                            "sqg", 16, mark=f"p4.out.{hf}.{dm-1}")

        def g_body(g):
            p1_gpsimd(g)
            if STAGE != "p1":
                p2_gpsimd(g)
                p4_gpsimd(g)

        def sy_body(sy):
            p0_sync(sy)
            p1_sync(sy)
            if STAGE != "p1":
                p2_sync(sy)
                p3_sync(sy)
            R.wait_all(sy, "sqs")
            R.wait_all(sy, "sqg")

        def t_body(t):
            p0_tensor(t)
            p1_tensor(t)
            if STAGE != "p1":
                p2_tensor(t)
                p3_tensor(t)

        def v_body(v):
            p0_vector(v)
            p1_vector(v)
            if STAGE != "p1":
                p2_vector(v)
                p3_vector(v)

        def s_body(s):
            p0_scalar(s)
            p1_scalar(s)
            if STAGE != "p1":
                p2_scalar(s)
                p3_scalar(s)

        R.reset_pass(True)
        for f in (g_body, sy_body, t_body, v_body, s_body):
            f(None)
        R.reset_pass(False)
        R.log = []
        global LAST_R
        LAST_R = R
        with nc.Block() as block:
            for nm, fb in (("g", g_body), ("sy", sy_body), ("t", t_body),
                           ("v", v_body), ("s", s_body)):
                R.stream = nm
                qn = {"g": "sqg", "sy": "sqs"}.get(nm)
                getattr(block, {"g": "gpsimd", "sy": "sync", "t": "tensor",
                                "v": "vector", "s": "scalar"}[nm])(
                    lambda e, fb=fb, qn=qn: fb(EngW(e, R, qn)))
    return nc


LAST_R = None
